# version 05
# baseline (speedup 1.0000x reference)
import numpy as np
import ml_dtypes
import concourse.bass as bass
import concourse.mybir as mybir
from concourse.bass_utils import run_bass_kernel_spmd

F32 = mybir.dt.float32
BF16 = mybir.dt.bfloat16
I32 = mybir.dt.int32
AF = mybir.ActivationFunctionType
OP = mybir.AluOpType
AX = mybir.AxisListType

D = 1024
ND = D // 128
TC = 512
N_CORES = 8
BF = ml_dtypes.bfloat16

L0_PHASE = 3
L1_SER_INNER = False
L1_SER_EVAC = False
ARENA_KB = 66
ARENA_BLOCK_KB = 34


class _CntMap(dict):
    def __init__(self):
        super().__init__()
        self.alias = {}

    def _r(self, k):
        return self.alias.get(k, k)

    def __getitem__(self, k):
        return dict.__getitem__(self, self._r(k))

    def __setitem__(self, k, v):
        dict.__setitem__(self, self._r(k), v)

    def __contains__(self, k):
        return dict.__contains__(self, self._r(k))

    def get(self, k, default=None):
        return dict.get(self, self._r(k), default)


class Ctx:
    def __init__(self, nc: bass.Bass, T: int):
        self.nc = nc
        self.T = T
        self.NT = T // 128
        self.NC = T // TC
        self._sems = {}
        self.cnt = _CntMap()
        self._dma_sems = set()
        self.dram = {}
        self.sbuf = {}
        self.input_names = []
        self._uid = 0
        self._arena_off = 0
        self._arena_marks = []

    def sem(self, name):
        name = self.cnt._r(name)
        if name not in self._sems:
            self._sems[name] = self.nc.alloc_semaphore(name)
            self.cnt[name] = 0
        return self._sems[name]

    def set_sem_scope(self, alias_map):
        self.cnt.alias = alias_map

    def inc(self, ins, name, n=1):
        ins.then_inc(self.sem(name), n)
        self.cnt[name] += n
        return self.cnt[name]

    def wait(self, eng, name, val=None):
        eng.wait_ge(self.sem(name), self.cnt[name] if val is None else val)

    def dma(self, dst, src, q=None, sem="gld", **kw):
        q = q or self.nc.gpsimd
        sem = self.cnt._r(sem)
        self._dma_sems.add(sem)
        ins = q.dma_start(dst, src, **kw)
        self.inc(ins, sem, 16)
        return self.cnt[sem]

    def dma_t(self, dst, src, sem="tld"):
        sem = self.cnt._r(sem)
        self._dma_sems.add(sem)
        ins = self.nc.sync.dma_start(dst, src, transpose=True)
        self.inc(ins, sem, 16)
        return self.cnt[sem]

    def store(self, dst, src, sem="sst"):
        sem = self.cnt._r(sem)
        self._dma_sems.add(sem)
        ins = self.nc.sync.dma_start(dst, src)
        self.inc(ins, sem, 16)
        return self.cnt[sem]

    def barrier(self):
        for s in sorted(self._dma_sems):
            if self.cnt[s]:
                self.nc.sync.wait_ge(self.sem(s), self.cnt[s])
        self.nc.all_engine_barrier()

    def din(self, name, shape, dtype):
        ap = self.nc.dram_tensor(name, list(shape), dtype, kind="ExternalInput").ap()
        self.dram[name] = ap
        self.input_names.append(name)
        return ap

    def dout(self, name, shape, dtype):
        ap = self.nc.dram_tensor(name, list(shape), dtype, kind="ExternalOutput").ap()
        self.dram[name] = ap
        return ap

    def dtmp(self, name, shape, dtype):
        ap = self.nc.dram_tensor(name, list(shape), dtype, kind="Internal").ap()
        self.dram[name] = ap
        return ap

    def sb(self, name, shape, dtype):
        if name in self.sbuf:
            ap = self.sbuf[name]
            assert list(ap.tensor.shape) == list(shape) and ap.tensor.dtype == dtype, name
            return ap
        ap = self.nc.alloc_sbuf_tensor(name, list(shape), dtype).ap()
        self.sbuf[name] = ap
        return ap

    def arena_init(self):
        self._arena = self.sb("arena", [128, ARENA_KB * 1024], mybir.dt.uint8)
        self.borrow = False

    def arena_mark(self):
        self._arena_marks.append(self._arena_off)

    def arena_release(self):
        self._arena_off = self._arena_marks.pop()

    def arena_at(self, off_bytes, shape, dtype):
        esz = mybir.dt.size(dtype)
        n = 1
        for s in shape[1:]:
            n *= s
        v = self._arena[0:shape[0], off_bytes:off_bytes + n * esz].bitcast(dtype)
        if len(shape) == 3:
            v = v.rearrange("p (a b) -> p a b", a=shape[1], b=shape[2])
        return v

    def arena(self, shape, dtype):
        esz = mybir.dt.size(dtype)
        n = 1
        for s in shape[1:]:
            n *= s
        nbytes = n * esz
        off = (self._arena_off + 31) & ~31
        cap = (ARENA_KB if self.borrow else ARENA_BLOCK_KB) * 1024
        assert off + nbytes <= cap, \
            f"arena overflow: {off}+{nbytes} > {cap} (borrow={self.borrow})"
        self._arena_off = off + nbytes
        v = self._arena[0:shape[0], off:off + nbytes].bitcast(dtype)
        if len(shape) == 2:
            return v
        if len(shape) == 3:
            return v.rearrange("p (a b) -> p a b", a=shape[1], b=shape[2])
        raise ValueError(shape)

    def uid(self):
        self._uid += 1
        return self._uid

    def vsync(self, ins, eng=None):
        eng = eng or self.nc.vector
        self.inc(ins, "vfence", 1)
        self.wait(eng, "vfence")


def l1_klist(qs, T):
    NTT = T // 128
    lo = max(0, qs * 4 - 1)
    hi = min(NTT - 1, qs * 4 + 3)
    ks = set(range(lo, hi + 1))
    ks.add(0)
    return sorted(ks)


def l1_mask(qs, kt, T):
    q = qs * 512 + np.arange(512)[None, :]
    k = kt * 128 + np.arange(128)[:, None]
    local = (k <= q) & (q - k <= 128)
    glob = (q < 1) | (k < 1)
    return (local | glob)


def make_consts(T=None):
    c = {}
    c["c_ident_bf"] = np.eye(128).astype(BF)
    c["c_ident_f32"] = np.eye(128, dtype=np.float32)
    c["c_ones_col_bf"] = np.ones((128, 1), dtype=BF)
    c["c_ones_col_f32"] = np.ones((128, 1), dtype=np.float32)
    c["c_ones_row_f32"] = np.ones((1, 128), dtype=np.float32)
    c["c_ones_row_bf"] = np.ones((1, 128), dtype=BF)
    kk = np.arange(128)[:, None]
    qq = np.arange(128)[None, :]
    c["c_mask_diag_bf"] = (kk <= qq).astype(BF)
    c["c_mask_prevwin_bf"] = (qq <= kk).astype(BF)
    m = np.zeros((128, 128), np.float32); m[0, :] = 1.0
    c["c_mask_col0_bf"] = m.astype(BF)
    m2 = np.zeros((128, 128), np.float32); m2[:, 0] = 1.0
    c["c_mask_q0row_bf"] = m2.astype(BF)
    c["c_mask_diagwin_bf"] = ((kk <= qq) & (qq - kk <= 128)).astype(BF)
    sel = np.zeros((8, 8 * 128), np.float32)
    for e in range(8):
        sel[e, e * 128:(e + 1) * 128] = 1.0
    c["c_esel_bf"] = sel.astype(BF)
    s64 = np.arange(64)[:, None]; c64 = np.arange(64)[None, :]
    c["c_u64_bf"] = (s64 <= c64).astype(BF)
    c["c_u64x_bf"] = np.tile((s64 <= c64), (2, 1)).astype(BF)
    sel16 = np.zeros((16, 16 * 128), np.float32)
    for h in range(16):
        sel16[h, h * 128:(h + 1) * 128] = 1.0
    c["c_esel16_bf"] = sel16.astype(BF)
    bsel = np.zeros((D, 16), np.float32)
    for h in range(16):
        bsel[h * 64:(h + 1) * 64, h] = 1.0
    c["c_bsel_bf"] = bsel.astype(BF)
    hmask = np.zeros((16, D), np.float32)
    for h in range(16):
        hmask[h, h * 64:(h + 1) * 64] = 1.0
    c["c_hmask_bf"] = hmask.astype(BF)
    if T is not None:
        for r in range(4):
            k_ = np.arange(128)[:, None] + r * 128
            q_ = np.arange(512)[None, :]
            c[f"c_l3m_{r}"] = (k_ <= q_).astype(BF)
        pat = {}
        idmap = {}
        for qs in range(T // 512):
            for kt in l1_klist(qs, T):
                m = l1_mask(qs, kt, T).astype(BF)
                key = m.tobytes()
                if key not in pat:
                    pat[key] = len(pat)
                    c[f"c_l1m_{pat[key]}"] = m
                idmap[(qs, kt)] = pat[key]
        c["_l1m_map"] = idmap
    return c


CONST_SBUF = ["c_ident_bf", "c_ident_f32", "c_ones_col_bf", "c_ones_col_f32",
              "c_ones_row_f32", "c_ones_row_bf"]


def declare_consts(ctx):
    consts = make_consts(ctx.T)
    ctx.consts_meta = consts
    for name, arr in consts.items():
        if name.startswith("_"):
            continue
        dt = BF16 if arr.dtype == BF else F32
        ctx.din(name, arr.shape, dt)
    for name in CONST_SBUF:
        arr = consts[name]
        dt = BF16 if arr.dtype == BF else F32
        sb = ctx.sb(name[2:], list(arr.shape), dt)
        ctx.dma(sb, ctx.dram[name])


def setup_resident(ctx):
    nc = ctx.nc
    T = ctx.T
    ctx.x = ctx.sb("x_res", [128, ND, T], F32)
    ctx.ps = [nc.alloc_psum_tensor(f"ps{i}", [128, TC], F32).ap() for i in range(8)]
    ctx.tmp_sb = ctx.sb("tmp_sb", [128, 2, TC], F32)
    ctx.arena_init()
    base = ARENA_BLOCK_KB * 1024
    ctx.rows = ctx.arena_at(base, [1, 4 * TC], F32)
    ctx.xn_sb = ctx.arena_at(base + 8192, [128, ND, TC], BF16)
    ctx.cast_sb = ctx.arena_at(base + 16384, [128, ND, TC], BF16)
    ctx.sq_sb = ctx.arena_at(base + 24576, [128, ND, TC], BF16)
    ctx.xn_dram = ctx.dtmp("xn_dram", [ctx.NC, 128, ND * TC], BF16)


def load_x(ctx, name="xin"):
    xin = ctx.din(name, [D, ctx.T], F32)
    for dt in range(ND):
        ctx.dma(ctx.x[:, dt, :], xin[dt * 128:(dt + 1) * 128, :])
    ctx.barrier()


def store_x(ctx, name="xout"):
    xout = ctx.dout(name, [D, ctx.T], F32)
    ctx.barrier()
    for dt in range(ND):
        ctx.store(xout[dt * 128:(dt + 1) * 128, :], ctx.x[:, dt, :])
    ctx.barrier()


def load_xn_chunk(ctx, buf, c, sem="xnl", src_dram=None):
    sd = src_dram if src_dram is not None else getattr(ctx, "xn_src", None)
    if sd is None:
        sd = ctx.xn_dram
    src = sd[c, :, :].rearrange("p (n t) -> p n t", n=ND)
    ctx.dma(buf, src, sem=sem)
    return ctx.cnt[sem]


def emit_norm(ctx, rms=False, eps=1e-5, gain_col=None, chunk_hook=None,
              save_rows=None, residual_add_xn=False, src_dram=None,
              out_dram=None):
    nc = ctx.nc
    x = ctx.x
    xn = ctx.xn_sb
    if out_dram is None:
        out_dram = ctx.xn_dram
    ones_bf = ctx.sbuf["ones_col_bf"]
    ones_row = ctx.sbuf["ones_row_f32"]
    sCast, sSq, sStat, sRecip, sSqrt, sRows, sB, sApp = (
        "n_c", "n_s", "n_t", "n_r", "n_q", "n_w", "n_b", "n_a")
    bApp = ctx.cnt.get(sApp, 0)
    psS, psM, psR, psN = ctx.ps[0], ctx.ps[1], ctx.ps[2], ctx.ps[3]
    r_mean = ctx.rows[0:1, 0:TC]
    r_var = ctx.rows[0:1, TC:2 * TC]
    r_rstd = ctx.rows[0:1, 2 * TC:3 * TC]
    r_nmr = ctx.rows[0:1, 3 * TC:4 * TC]
    for c in range(ctx.NC):
        sl = slice(c * TC, (c + 1) * TC)
        xb = ctx.cast_sb
        sq = ctx.sq_sb
        if c > 0:
            ctx.wait(nc.tensor, sApp, bApp + c)
        if src_dram is not None:
            if c > 0:
                ctx.wait(nc.gpsimd, sStat)
                ctx.wait(nc.gpsimd, sSq)
                ctx.wait(nc.gpsimd, sApp)
            ctx.dma(xb, src_dram[c, :, :].rearrange("p (n t) -> p n t", n=ND),
                    sem="nsl")
            ctx.wait(nc.scalar, "nsl")
            ctx.wait(nc.tensor, "nsl")
            for dt in range(ND):
                ia = nc.scalar.activation(sq[:, dt, :], xb[:, dt, :], AF.Square)
            ctx.inc(ia, sSq, 1)
            ctx.wait(nc.tensor, sSq)
        else:
            for dt in range(ND):
                iv = nc.vector.tensor_copy(xb[:, dt, :], x[:, dt, sl])
                ia = nc.scalar.activation(sq[:, dt, :], x[:, dt, sl], AF.Square)
            ctx.inc(iv, sCast, 1)
            ctx.inc(ia, sSq, 1)
            ctx.wait(nc.tensor, sCast)
            ctx.wait(nc.tensor, sSq)
        if not rms:
            for dt in range(ND):
                nc.tensor.matmul(psM[0:1, :], ones_bf, xb[:, dt, :],
                                 start=(dt == 0), stop=(dt == ND - 1))
        for dt in range(ND):
            i = nc.tensor.matmul(psS[0:1, :], ones_bf, sq[:, dt, :],
                                 start=(dt == 0), stop=(dt == ND - 1))
        ctx.inc(i, sStat, 1)
        ctx.wait(nc.vector, sStat)
        if rms:
            nc.vector.tensor_scalar(r_var, psS[0:1, :], 1.0 / D, None, OP.mult)
        else:
            nc.vector.tensor_scalar(r_mean, psM[0:1, :], 1.0 / D, None, OP.mult)
            nc.vector.tensor_scalar(r_var, psS[0:1, :], 1.0 / D, None, OP.mult)
            nc.vector.tensor_tensor(ctx.tmp_sb[0:1, 0, 0:TC], r_mean, r_mean, OP.mult)
            nc.vector.tensor_sub(r_var, r_var, ctx.tmp_sb[0:1, 0, 0:TC])
        nc.vector.tensor_scalar(r_var, r_var, eps, None, OP.add)
        i = nc.vector.reciprocal(ctx.tmp_sb[0:1, 0, 0:TC], r_var)
        ctx.inc(i, sRecip, 1)
        ctx.wait(nc.scalar, sRecip)
        i = nc.scalar.activation(r_rstd, ctx.tmp_sb[0:1, 0, 0:TC], AF.Sqrt)
        ctx.inc(i, sSqrt, 1)
        if not rms:
            ctx.wait(nc.vector, sSqrt)
            i = nc.vector.scalar_tensor_tensor(r_nmr, r_mean, -1.0, r_rstd,
                                               OP.mult, OP.mult)
            ctx.inc(i, sRows, 1)
            ctx.wait(nc.tensor, sRows)
        else:
            ctx.wait(nc.tensor, sSqrt)
        i = nc.tensor.matmul(psR, ones_row, r_rstd, start=True, stop=True)
        if not rms:
            i = nc.tensor.matmul(psN, ones_row, r_nmr, start=True, stop=True)
        ctx.inc(i, sB, 1)
        if chunk_hook is not None:
            chunk_hook(c, r_mean, r_rstd)
        ctx.wait(nc.vector, sB)
        if residual_add_xn:
            if chunk_hook is not None:
                ctx.wait(nc.vector, "hkM")
        for dt in range(ND):
            xin_dt = xb[:, dt, :] if src_dram is not None else x[:, dt, sl]
            if rms:
                if gain_col is not None:
                    i = nc.vector.scalar_tensor_tensor(
                        xn[:, dt, :], xin_dt, gain_col[:, dt:dt + 1], psR,
                        OP.mult, OP.mult)
                else:
                    i = nc.vector.tensor_tensor(xn[:, dt, :], xin_dt, psR,
                                                OP.mult)
            else:
                tmp = ctx.tmp_sb[:, 1, :]
                nc.vector.tensor_tensor(tmp, xin_dt, psR, OP.mult)
                i = nc.vector.tensor_tensor(xn[:, dt, :], tmp, psN, OP.add)
                if residual_add_xn:
                    nc.vector.tensor_tensor(x[:, dt, sl], x[:, dt, sl], tmp,
                                            OP.add)
                    i = nc.vector.tensor_tensor(x[:, dt, sl], x[:, dt, sl],
                                                psN, OP.add)
        ctx.inc(i, sApp, 1)
        ctx.wait(nc.sync, sApp)
        dst = out_dram[c, :, :].rearrange("p (n t) -> p n t", n=ND)
        ctx.store(dst, xn[:, :, :])
        if save_rows is not None:
            ctx.store(save_rows[0][0:1, sl], r_mean)
            ctx.store(save_rows[1][0:1, sl], r_rstd)
        ctx.wait(nc.vector, "sst", ctx.cnt["sst"])
    ctx.barrier()


def emit_swiglu(ctx, H, load_g, load_u, load_d, act=AF.Silu,
                sg=1.0, su=1.0, sd=1.0, FS=768,
                gate_bias_col=None, up_bias_col=None,
                h_row_mul=None, in_row_mul=None, out_add_bias_col=None):
    nc = ctx.nc
    ctx.arena_mark()
    prev_borrow = ctx.borrow
    ctx.borrow = True
    NFT_MAX = FS // 128
    wgT = ctx.arena([128, ND, FS], BF16)
    wuT = ctx.arena([128, ND, FS], BF16) if load_u is not None else None
    wdT = ctx.arena([128, NFT_MAX, D], BF16)
    hsb = ctx.arena([128, NFT_MAX, TC], BF16)
    gsb = ctx.arena([128, TC], F32)
    xnc = ctx.arena([128, ND, TC], BF16)
    mrow = (ctx.arena([128, TC], BF16)
            if (h_row_mul is not None or in_row_mul is not None) else None)

    f0 = 0
    while f0 < H:
        fw = min(FS, H - f0)
        nft = fw // 128
        for dt in range(ND):
            load_g(dt, f0, fw, wgT[:, dt, 0:fw])
            if load_u is not None:
                load_u(dt, f0, fw, wuT[:, dt, 0:fw])
        for ht in range(nft):
            load_d(f0 + ht * 128, wdT[:, ht, :])
        ctx.barrier()
        for c in range(ctx.NC):
            sl = slice(c * TC, (c + 1) * TC)
            sXn, sGate, sUp, sSilu, sH, sDown, sAcc, sMrow = (
                "sw_x", "sw_g", "sw_u", "sw_s", "sw_h", "sw_d", "sw_a", "sw_m")
            bG, bS, bH, bA = (ctx.cnt.get(sGate, 0), ctx.cnt.get(sSilu, 0),
                              ctx.cnt.get(sH, 0), ctx.cnt.get(sAcc, 0))
            bM = ctx.cnt.get(sMrow, 0)
            load_xn_chunk(ctx, xnc, c, sem=sXn)
            ctx._dma_sems.add(sXn)
            ctx.wait(nc.tensor, sXn)
            rmul = h_row_mul if h_row_mul is not None else in_row_mul
            if rmul is not None:
                esel_sb, rowsrc, eidx = rmul
                i = nc.tensor.matmul(ctx.ps[6], esel_sb[:, eidx * 128:(eidx + 1) * 128],
                                     rowsrc[:, sl], start=True, stop=True)
                ctx.inc(i, sMrow, 1)
                ctx.wait(nc.vector, sMrow)
                i = nc.vector.tensor_copy(mrow, ctx.ps[6])
                ctx.inc(i, sMrow, 1)
            if in_row_mul is not None:
                for dt in range(ND):
                    i = nc.vector.tensor_tensor(xnc[:, dt, :], xnc[:, dt, :],
                                                mrow, OP.mult)
                ctx.inc(i, sMrow, 1)
                ctx.wait(nc.tensor, sMrow, ctx.cnt[sMrow])
            for ft in range(nft):
                psG = ctx.ps[ft % 2]
                psU = ctx.ps[2 + ft % 2]
                if ft >= 2:
                    if load_u is not None:
                        ctx.wait(nc.tensor, sSilu, bS + ft - 1)
                    ctx.wait(nc.tensor, sH, bH + ft - 1)
                for dt in range(ND):
                    i = nc.tensor.matmul(psG, wgT[:, dt, ft * 128:(ft + 1) * 128],
                                         xnc[:, dt, :],
                                         start=(dt == 0), stop=(dt == ND - 1))
                ctx.inc(i, sGate, 1)
                if load_u is not None:
                    for dt in range(ND):
                        i = nc.tensor.matmul(psU, wuT[:, dt, ft * 128:(ft + 1) * 128],
                                             xnc[:, dt, :],
                                             start=(dt == 0), stop=(dt == ND - 1))
                    ctx.inc(i, sUp, 1)
                fi = (f0 // 128) + ft
                gb = (0.0 if gate_bias_col is None
                      else gate_bias_col[:, fi:fi + 1])
                ctx.wait(nc.scalar, sGate)
                if load_u is None:
                    i = nc.scalar.activation(hsb[:, ft, :], psG,
                                             act if act else AF.Copy,
                                             bias=gb, scale=sg)
                    ctx.inc(i, sH, 1)
                    continue
                if ft >= 1:
                    ctx.wait(nc.scalar, sH, bH + ft)
                i = nc.scalar.activation(gsb, psG, act if act else AF.Copy,
                                         bias=gb, scale=sg)
                ctx.inc(i, sSilu, 1)
                ctx.wait(nc.vector, sUp)
                ctx.wait(nc.vector, sSilu)
                if h_row_mul is not None:
                    ctx.wait(nc.vector, sMrow, bM + 2)
                if up_bias_col is None:
                    i = nc.vector.scalar_tensor_tensor(hsb[:, ft, :], psU, su, gsb,
                                                       OP.mult, OP.mult)
                else:
                    ub = up_bias_col[:, fi:fi + 1]
                    nc.vector.tensor_scalar(ctx.tmp_sb[:, 0, :], psU, su, ub,
                                            OP.mult, op1=OP.add)
                    i = nc.vector.tensor_tensor(hsb[:, ft, :], ctx.tmp_sb[:, 0, :],
                                                gsb, OP.mult)
                if h_row_mul is not None:
                    i = nc.vector.tensor_tensor(hsb[:, ft, :], hsb[:, ft, :],
                                                mrow, OP.mult)
                ctx.inc(i, sH, 1)
            ctx.wait(nc.tensor, sH, bH + nft)
            for dt2 in range(ND):
                psD = ctx.ps[4 + dt2 % 2]
                if dt2 >= 2:
                    ctx.wait(nc.tensor, sAcc, bA + dt2 - 1)
                for ht in range(nft):
                    i = nc.tensor.matmul(psD, wdT[:, ht, dt2 * 128:(dt2 + 1) * 128],
                                         hsb[:, ht, :],
                                         start=(ht == 0), stop=(ht == nft - 1))
                ctx.inc(i, sDown, 1)
                ctx.wait(nc.vector, sDown)
                i = nc.vector.scalar_tensor_tensor(ctx.x[:, dt2, sl], psD, sd,
                                                   ctx.x[:, dt2, sl], OP.mult, OP.add)
                if out_add_bias_col is not None and f0 == 0:
                    i = nc.vector.tensor_scalar(ctx.x[:, dt2, sl], ctx.x[:, dt2, sl],
                                                out_add_bias_col[:, dt2:dt2 + 1],
                                                None, OP.add)
                ctx.inc(i, sAcc, 1)
            ctx.barrier()
        f0 += fw
    ctx.borrow = prev_borrow
    ctx.arena_release()


def emit_l0_ffn(ctx):
    H = 3072
    s = 1.0 / 127.0
    wg = ctx.din("L0_ffn_w_gate", [H, D], I32)
    wu = ctx.din("L0_ffn_w_up", [H, D], I32)
    wd = ctx.din("L0_ffn_w_down", [D, H], I32)
    wg_bf = ctx.dtmp("l0_wg_bf", [H, D], BF16)
    wu_bf = ctx.dtmp("l0_wu_bf", [H, D], BF16)
    wd_bf = ctx.dtmp("l0_wd_bf", [D, H], BF16)
    ctx.dma(wg_bf, wg)
    ctx.dma(wu_bf, wu)
    ctx.dma(wd_bf, wd)

    emit_norm(ctx, rms=False, eps=1e-5)

    def load_g(dt, f0, fw, dest):
        ctx.dma_t(dest, wg_bf[f0:f0 + fw, dt * 128:(dt + 1) * 128])

    def load_u(dt, f0, fw, dest):
        ctx.dma_t(dest, wu_bf[f0:f0 + fw, dt * 128:(dt + 1) * 128])

    def load_d(h0, dest):
        ctx.dma_t(dest, wd_bf[:, h0:h0 + 128])

    emit_swiglu(ctx, H, load_g, load_u, load_d, act=AF.Silu,
                sg=s, su=s, sd=s, FS=768)


def emit_l3_ffn(ctx):
    nc = ctx.nc
    CB = ctx.din("L3_ffn_CB", [D, D], F32)
    SCB = ctx.din("L3_ffn_SCB", [D], F32)
    BIAS = ctx.din("L3_ffn_bias", [D], F32)
    cb_bf = ctx.dtmp("l3_cb_bf", [D, D], BF16)
    ctx.dma(cb_bf, CB)
    scb_col = emit_feature_cols(ctx, "l3_scb", SCB, D)
    bias_col = emit_feature_cols(ctx, "l3_bias", BIAS, D)
    nc.vector.tensor_scalar(scb_col, scb_col, 1.0 / 127.0, None, OP.mult)

    emit_norm(ctx, rms=False, eps=1e-5)

    ctx.arena_mark()
    cbT = ctx.arena([128, ND, D], BF16)
    xnc = ctx.arena([128, ND, TC], BF16)
    for kt in range(ND):
        ctx.dma_t(cbT[:, kt, :], cb_bf[:, kt * 128:(kt + 1) * 128])
    ctx.barrier()
    sM, sV, sXn = "cb_m", "cb_v", "cb_x"
    idx = 0
    for c in range(ctx.NC):
        sl = slice(c * TC, (c + 1) * TC)
        ctx._dma_sems.add(sXn)
        load_xn_chunk(ctx, xnc, c, sem=sXn)
        ctx.wait(nc.tensor, sXn)
        bV = ctx.cnt.get(sV, 0) - idx
        for nt in range(ND):
            ps = ctx.ps[idx % 2]
            if idx >= 2:
                ctx.wait(nc.tensor, sV, bV + idx - 1)
            for kt in range(ND):
                i = nc.tensor.matmul(ps, cbT[:, kt, nt * 128:(nt + 1) * 128],
                                     xnc[:, kt, :],
                                     start=(kt == 0), stop=(kt == ND - 1))
            ctx.inc(i, sM, 1)
            ctx.wait(nc.vector, sM)
            tmp = ctx.tmp_sb[:, 0, :]
            nc.vector.scalar_tensor_tensor(tmp, ps, scb_col[:, nt:nt + 1],
                                           ctx.x[:, nt, sl], OP.mult, OP.add)
            i = nc.vector.tensor_scalar(ctx.x[:, nt, sl], tmp,
                                        bias_col[:, nt:nt + 1], None, OP.add)
            ctx.inc(i, sV, 1)
            idx += 1
        ctx.barrier()
    ctx.arena_release()


def emit_feature_cols(ctx, name, vec_ap, NF, dtype=F32, use_arena=False):
    nc = ctx.nc
    nt = NF // 128
    if not hasattr(ctx, "_fc_ld"):
        ctx._fc_ld = ctx.tmp_sb[0:32, 0, 0:128]
    tmp = ctx._fc_ld[0:nt, :]
    col = ctx.arena([128, nt], dtype) if use_arena else ctx.sb(name, [128, nt], dtype)
    if len(vec_ap.shape) > 1:
        vec_ap = vec_ap.flatten()
    if ctx.cnt.get("colT", 0):
        ctx.wait(nc.gpsimd, "colT")
    ctx.dma(tmp, vec_ap.rearrange("(n p) -> n p", p=128), sem="colL")
    ctx.wait(nc.tensor, "colL")
    if ctx.cnt.get("colV", 0) > 0:
        ctx.wait(nc.tensor, "colV")
    ps = ctx.ps[6]
    i = nc.tensor.transpose(ps[:, 0:nt], tmp, ctx.sbuf["ident_f32"][0:nt, 0:nt])
    ctx.inc(i, "colT", 1)
    ctx.wait(nc.vector, "colT")
    i = nc.vector.tensor_copy(col, ps[:, 0:nt])
    ctx.inc(i, "colV", 1)
    return col


def make_router_hook(ctx, wr_sb, colsum_col, br_col, ladj_sb, tail,
                     apply_rstd=True):
    nc = ctx.nc
    ones_row = ctx.sbuf["ones_row_f32"]

    def hook(c, r_mean, r_rstd):
        psL, psBm, psBr = ctx.ps[4], ctx.ps[5], ctx.ps[7]
        if c > 0:
            ctx.wait(nc.tensor, "hkV", c)
        for dt in range(ND):
            i = nc.tensor.matmul(psL[0:8, :], wr_sb[:, dt, :],
                                 ctx.x[:, dt, c * TC:(c + 1) * TC],
                                 start=(dt == 0), stop=(dt == ND - 1))
        i = nc.tensor.matmul(psBm[0:8, :], ones_row[0:1, 0:8], r_mean,
                             start=True, stop=True)
        if apply_rstd:
            i = nc.tensor.matmul(psBr[0:8, :], ones_row[0:1, 0:8], r_rstd,
                                 start=True, stop=True)
        ctx.inc(i, "hkM", 1)
        ctx.wait(nc.vector, "hkM")
        t1 = ctx.tmp_sb[0:8, 0, :]
        t2 = ctx.tmp_sb[0:8, 1, :]
        nc.vector.tensor_scalar(t1, psBm[0:8, :], colsum_col, None, OP.mult)
        nc.vector.tensor_sub(t2, psL[0:8, :], t1)
        if apply_rstd:
            nc.vector.tensor_tensor(t2, t2, psBr[0:8, :], OP.mult)
        i = nc.vector.tensor_scalar(ladj_sb, t2, br_col, None, OP.add)
        ctx.inc(i, "hkV", 1)
        tail(c, ladj_sb)

    return hook


def emit_l2_ffn(ctx):
    nc = ctx.nc
    HE = 4096
    Wr = ctx.din("L2_ffn_Wr", [D, 8], F32)
    br = ctx.din("L2_ffn_br", [8], F32)
    W1 = ctx.din("L2_ffn_W1", [8, D, HE], F32)
    b1 = ctx.din("L2_ffn_b1", [8, HE], F32)
    W2 = ctx.din("L2_ffn_W2", [8, HE, D], F32)
    b2 = ctx.din("L2_ffn_b2", [8, D], F32)
    w1_bf = ctx.dtmp("l2f_w1_bf", [8, D, HE], BF16)
    w2_bf = ctx.dtmp("l2f_w2_bf", [8, HE, D], BF16)
    for e in range(8):
        ctx.dma(w1_bf[e], W1[e])
        ctx.dma(w2_bf[e], W2[e])
    wr_sb = ctx.sb("l2f_wr", [128, ND, 8], F32)
    ctx.dma(wr_sb, Wr.rearrange("(n p) e -> p n e", p=128))
    br_col = ctx.sb("l2f_br", [8, 1], F32)
    ctx.dma(br_col, br.unsqueeze(1))
    b1_cols = [emit_feature_cols(ctx, f"l2f_b1_{e}", b1[e:e + 1, :], HE,
                                 use_arena=True)
               for e in range(8)]
    b2_cols = [emit_feature_cols(ctx, f"l2f_b2_{e}", b2[e:e + 1, :], D,
                                 use_arena=True)
               for e in range(8)]
    ctx.barrier()
    for dt in range(ND):
        i = nc.tensor.matmul(ctx.ps[6][0:8, 0:1], wr_sb[:, dt, :],
                             ctx.sbuf["ones_col_f32"],
                             start=(dt == 0), stop=(dt == ND - 1))
    ctx.inc(i, "cs_m", 1)
    ctx.wait(nc.vector, "cs_m")
    colsum_col = ctx.sb("l2f_colsum", [8, 1], F32)
    nc.vector.tensor_copy(colsum_col, ctx.ps[6][0:8, 0:1])
    ctx.barrier()

    ctx.arena_mark()
    mask_fm = ctx.arena([8, ctx.T], BF16)
    ladj_sb = ctx.arena([8, TC], F32)
    mtm = ctx.arena([128, 8], BF16)

    def route_tail(c, ladj):
        ps6 = ctx.ps[6]
        ps7b = ctx.ps[7].bitcast(BF16)
        ctx.wait(nc.tensor, "hkV")
        for j in range(4):
            ctx.wait(nc.tensor, "rtV")
            i = nc.tensor.transpose(ps6[:, j * 8:(j + 1) * 8],
                                    ladj[:, j * 128:(j + 1) * 128],
                                    ctx.sbuf["ident_f32"][0:8, 0:8])
            ctx.inc(i, "rtT", 1)
            ctx.wait(nc.vector, "rtT")
            vmax = ctx.tmp_sb[:, 0, 0:1]
            i = nc.vector.reduce_max(vmax, ps6[:, j * 8:(j + 1) * 8], axis=AX.X)
            ctx.vsync(i)
            i = nc.vector.tensor_scalar(mtm, ps6[:, j * 8:(j + 1) * 8], vmax,
                                        None, OP.is_ge)
            ctx.inc(i, "rtV", 1)
            ctx.wait(nc.tensor, "rtV")
            i = nc.tensor.transpose(ps7b[0:8, 0:128], mtm,
                                    ctx.sbuf["ident_bf"])
            ctx.inc(i, "rtT", 1)
            ctx.wait(nc.vector, "rtT")
            i = nc.vector.tensor_copy(
                mask_fm[:, c * TC + j * 128:c * TC + (j + 1) * 128],
                ps7b[0:8, 0:128])
            ctx.inc(i, "rtV", 1)

    hook = make_router_hook(ctx, wr_sb, colsum_col, br_col, ladj_sb, route_tail)
    emit_norm(ctx, rms=False, eps=1e-5, chunk_hook=hook)

    esel = ctx.arena([8, 8 * 128], BF16)
    ctx.dma(esel, ctx.dram["c_esel_bf"])
    ctx.barrier()

    for e in range(8):
        def load_g(dt, f0, fw, dest, e=e):
            ctx.dma(dest, w1_bf[e, dt * 128:(dt + 1) * 128, f0:f0 + fw])

        def load_d(h0, dest, e=e):
            ctx.dma(dest, w2_bf[e, h0:h0 + 128, :])

        emit_swiglu(ctx, HE, load_g, None, load_d, act=AF.Relu,
                    FS=512, gate_bias_col=b1_cols[e],
                    in_row_mul=(esel, mask_fm, e),
                    out_add_bias_col=b2_cols[e])
    ctx.arena_release()


def emit_l1_ffn(ctx):
    nc = ctx.nc
    FF = 2048
    rw = ctx.din("L1_ffn_router_w", [8, D], F32)
    rb = ctx.din("L1_ffn_router_b", [8], F32)
    temp = ctx.din("L1_ffn_temperature", [1], F32)
    wg = ctx.din("L1_ffn_wg", [8, D, FF], F32)
    wp = ctx.din("L1_ffn_wp", [8, D, FF], F32)
    wo = ctx.din("L1_ffn_wo", [8, FF, D], F32)
    wg_bf = ctx.dtmp("l1f_wg_bf", [8, D, FF], BF16)
    wp_bf = ctx.dtmp("l1f_wp_bf", [8, D, FF], BF16)
    wo_bf = ctx.dtmp("l1f_wo_bf", [8, FF, D], BF16)
    for e in range(8):
        ctx.dma(wg_bf[e], wg[e])
        ctx.dma(wp_bf[e], wp[e])
        ctx.dma(wo_bf[e], wo[e])
    rw_ld = ctx.sb("ld4k", [8, D], F32)
    ctx.dma(rw_ld, rw)
    rwT = ctx.sb("l2f_wr", [128, ND, 8], F32)
    rb_col = ctx.sb("l1f_rb", [8, 1], F32)
    ctx.dma(rb_col, rb.unsqueeze(1))
    temp_sb = ctx.sb("l1f_temp", [1, 2], F32)
    ctx.dma(temp_sb[0:1, 0:1], temp.unsqueeze(1))
    ctx.barrier()
    for dt in range(ND):
        i = nc.tensor.transpose(ctx.ps[6][:, 0:8],
                                rw_ld[:, dt * 128:(dt + 1) * 128],
                                ctx.sbuf["ident_f32"][0:8, 0:8])
        ctx.inc(i, "rwT", 1)
        ctx.wait(nc.vector, "rwT")
        i = nc.vector.tensor_copy(rwT[:, dt, :], ctx.ps[6][:, 0:8])
        ctx.inc(i, "rwV", 1)
        ctx.wait(nc.tensor, "rwV")
    for dt in range(ND):
        i = nc.tensor.matmul(ctx.ps[6][0:8, 0:1], rwT[:, dt, :],
                             ctx.sbuf["ones_col_f32"],
                             start=(dt == 0), stop=(dt == ND - 1))
    ctx.inc(i, "cs_m", 1)
    ctx.wait(nc.vector, "cs_m")
    colsum_col = ctx.sb("l1f_colsum", [8, 1], F32)
    nc.vector.tensor_copy(colsum_col, ctx.ps[6][0:8, 0:1])
    i = nc.vector.tensor_scalar(temp_sb[0:1, 1:2], temp_sb[0:1, 0:1], 0.1,
                                None, OP.max)
    ctx.vsync(i)
    i = nc.vector.reciprocal(temp_sb[0:1, 0:1], temp_sb[0:1, 1:2])
    ctx.inc(i, "l1v", 1)
    ctx.wait(nc.tensor, "l1v")
    i = nc.tensor.matmul(ctx.ps[5][0:8, 0:1], ctx.sbuf["ones_row_f32"][0:1, 0:8],
                         temp_sb[0:1, 0:1], start=True, stop=True)
    ctx.inc(i, "l1m", 1)
    ctx.wait(nc.vector, "l1m")
    invt_col = ctx.sb("l1f_invt", [8, 1], F32)
    nc.vector.tensor_copy(invt_col, ctx.ps[5][0:8, 0:1])
    ctx.barrier()

    ctx.arena_mark()
    w_fm = ctx.arena([8, ctx.T], BF16)
    ctx.arena_mark()
    la = ctx.arena([8, ctx.T], F32)
    et = ctx.arena([8, TC], F32)
    lse_row = ctx.arena([1, TC], F32)
    ladj_sb = ctx.arena([8, TC], F32)
    gtm = ctx.arena([128, 8], F32)
    gn = ctx.arena([128, 8], F32)
    m1 = ctx.arena([128, 8], F32)
    gm = ctx.arena([128, 8], F32)
    sel = ctx.arena([128, 8], F32)
    w_tm = ctx.arena([128, 8], BF16)

    def l1_tail(c, ladj):
        i = nc.vector.tensor_scalar(la[:, c * TC:(c + 1) * TC], ladj,
                                    invt_col, None, OP.mult)
        ctx.inc(i, "l1t", 1)

    hook = make_router_hook(ctx, rwT, colsum_col, rb_col, ladj_sb, l1_tail)
    emit_norm(ctx, rms=False, eps=1e-5, chunk_hook=hook, residual_add_xn=True)

    cA = ctx.tmp_sb[0:8, 0, 0:1]
    cB = ctx.tmp_sb[0:8, 0, 2:3]
    cC = ctx.tmp_sb[0:8, 0, 4:5]
    cD = ctx.tmp_sb[0:8, 0, 6:7]
    cE = ctx.tmp_sb[0:8, 0, 8:9]
    for it in range(3):
        for c in range(ctx.NC):
            sl = slice(c * TC, (c + 1) * TC)
            if c > 0 or it > 0:
                ctx.wait(nc.scalar, "skv")
            i = nc.scalar.activation(et, la[:, sl], AF.Exp)
            ctx.inc(i, "ska", 1)
            ctx.wait(nc.tensor, "ska")
            if c > 0 or it > 0:
                ctx.wait(nc.tensor, "skv")
                ctx.wait(nc.tensor, "skl")
            i = nc.tensor.matmul(ctx.ps[0][0:1, :], ctx.sbuf["ones_col_f32"][0:8, :],
                                 et, start=True, stop=True)
            ctx.inc(i, "skm", 1)
            ctx.wait(nc.scalar, "skm")
            i = nc.scalar.activation(lse_row, ctx.ps[0][0:1, :], AF.Ln)
            ctx.inc(i, "skl", 1)
            ctx.wait(nc.tensor, "skl")
            i = nc.tensor.matmul(ctx.ps[1][0:8, :],
                                 ctx.sbuf["ones_row_f32"][0:1, 0:8],
                                 lse_row, start=True, stop=True)
            ctx.inc(i, "skb", 1)
            ctx.wait(nc.vector, "skb")
            i = nc.vector.tensor_tensor(la[:, sl], la[:, sl], ctx.ps[1][0:8, :],
                                        OP.subtract)
            ctx.inc(i, "skv", 1)
        ctx.wait(nc.vector, "skv")
        i = nc.vector.reduce_max(cA, la, axis=AX.X)
        ctx.vsync(i)
        i = nc.vector.tensor_scalar(cB, cA, -1.0, None, OP.mult)
        ctx.vsync(i)
        i = nc.vector.memset(cC, 0.0)
        ctx.inc(i, "skn", 1)
        ctx.wait(nc.scalar, "skn")
        for c in range(ctx.NC):
            sl = slice(c * TC, (c + 1) * TC)
            if c > 0:
                ctx.wait(nc.scalar, "skv3")
            i = nc.scalar.activation(et, la[:, sl], AF.Exp, bias=cB,
                                     accum_out=cE)
            ctx.inc(i, "ska2", 1)
            ctx.wait(nc.vector, "ska2")
            i = nc.vector.tensor_tensor(cC, cC, cE, OP.add)
            ctx.inc(i, "skv3", 1)
        ctx.wait(nc.vector, "skv3")
        ctx.wait(nc.scalar, "skv3")
        i = nc.scalar.activation(cD, cC, AF.Ln)
        ctx.inc(i, "ska3", 1)
        ctx.wait(nc.vector, "ska3")
        i = nc.vector.tensor_tensor(cD, cD, cA, OP.add)
        ctx.vsync(i)
        i = nc.vector.tensor_scalar(la, la, cD, None, OP.subtract)
        ctx.inc(i, "skv2", 1)
        ctx.wait(nc.scalar, "skv2")
    ctx.wait(nc.scalar, "skv2")
    for t in range(ctx.NT):
        tsl = slice(t * 128, (t + 1) * 128)
        if t > 0:
            ctx.wait(nc.scalar, "tpt")
        i = nc.scalar.activation(et[:, 0:128], la[:, tsl], AF.Exp)
        ctx.inc(i, "tpa", 1)
        ctx.wait(nc.tensor, "tpa")
        if t > 0:
            ctx.wait(nc.tensor, "tpe")
        i = nc.tensor.transpose(ctx.ps[6][:, 0:8], et[:, 0:128],
                                ctx.sbuf["ident_f32"][0:8, 0:8])
        ctx.inc(i, "tpt", 1)
        ctx.wait(nc.vector, "tpt")
        i = nc.vector.tensor_copy(gtm, ctx.ps[6][:, 0:8])
        ctx.inc(i, "tpe", 1)
        ctx.wait(nc.vector, "tpe")
        vS = ctx.tmp_sb[:, 0, 0:1]
        i = nc.vector.reduce_sum(vS, gtm, axis=AX.X)
        ctx.vsync(i)
        i = nc.vector.tensor_scalar(vS, vS, 1e-8, None, OP.add)
        ctx.vsync(i)
        vR = ctx.tmp_sb[:, 0, 2:3]
        i = nc.vector.reciprocal(vR, vS)
        ctx.vsync(i)
        i = nc.vector.tensor_scalar(gn, gtm, vR, None, OP.mult)
        ctx.vsync(i)
        v1 = ctx.tmp_sb[:, 0, 4:5]
        i = nc.vector.reduce_max(v1, gn, axis=AX.X)
        ctx.vsync(i)
        i = nc.vector.tensor_scalar(m1, gn, v1, None, OP.is_ge)
        ctx.vsync(i)
        i = nc.vector.scalar_tensor_tensor(gm, m1, -1e9, gn, OP.mult, OP.add)
        ctx.vsync(i)
        v2 = ctx.tmp_sb[:, 0, 6:7]
        i = nc.vector.reduce_max(v2, gm, axis=AX.X)
        ctx.vsync(i)
        i = nc.vector.tensor_scalar(sel, gn, v2, None, OP.is_ge)
        ctx.vsync(i)
        vW = ctx.tmp_sb[:, 0, 8:9]
        i = nc.vector.tensor_tensor(vW, v1, v2, OP.add)
        ctx.vsync(i)
        i = nc.vector.tensor_scalar(vW, vW, 1e-8, None, OP.add)
        ctx.vsync(i)
        vQ = ctx.tmp_sb[:, 0, 10:11]
        i = nc.vector.reciprocal(vQ, vW)
        ctx.vsync(i)
        i = nc.vector.tensor_tensor(gn, gn, sel, OP.mult)
        ctx.vsync(i)
        i = nc.vector.tensor_scalar(w_tm, gn, vQ, None, OP.mult)
        ctx.inc(i, "tpw", 1)
        ctx.wait(nc.tensor, "tpw")
        i = nc.tensor.transpose(ctx.ps[7].bitcast(BF16)[0:8, 0:128], w_tm,
                                ctx.sbuf["ident_bf"])
        ctx.inc(i, "tpt", 1)
        ctx.wait(nc.vector, "tpt")
        i = nc.vector.tensor_copy(w_fm[:, tsl], ctx.ps[7].bitcast(BF16)[0:8, 0:128])
        ctx.inc(i, "tpe", 1)
    ctx.barrier()
    ctx.arena_release()

    esel = ctx.arena([8, 8 * 128], BF16)
    ctx.dma(esel, ctx.dram["c_esel_bf"])
    ctx.barrier()

    for e in range(8):
        def load_g(dt, f0, fw, dest, e=e):
            ctx.dma(dest, wg_bf[e, dt * 128:(dt + 1) * 128, f0:f0 + fw])

        def load_u(dt, f0, fw, dest, e=e):
            ctx.dma(dest, wp_bf[e, dt * 128:(dt + 1) * 128, f0:f0 + fw])

        def load_d(h0, dest, e=e):
            ctx.dma(dest, wo_bf[e, h0:h0 + 128, :])

        emit_swiglu(ctx, FF, load_g, load_u, load_d, act=AF.Relu,
                    FS=512, h_row_mul=(esel, w_fm, e))
    ctx.arena_release()




def emit_l1_attn(ctx):
    nc = ctx.nc
    T = ctx.T
    NQS = T // 512
    names = ["Wq", "bq", "Wk", "bk", "Wv", "bv", "Wo", "bo"]
    dr = {}
    for n_ in names:
        shp = [D, D] if n_.startswith("W") else [D]
        dr[n_] = ctx.din(f"L1_attn_{n_}", shp, F32)
    w_bf = {}
    for n_ in ["Wq", "Wk", "Wv", "Wo"]:
        w_bf[n_] = ctx.dtmp(f"l1a_{n_}_bf", [D, D], BF16)
        ctx.dma(w_bf[n_], dr[n_])
    bq_col = emit_feature_cols(ctx, "l1a_bq", dr["bq"], D)
    bk_col = emit_feature_cols(ctx, "l1a_bk", dr["bk"], D)
    bo_col = emit_feature_cols(ctx, "l1a_bo", dr["bo"], D)
    bv_ld = ctx.sb("ld4k", [8, D], F32)
    ctx.dma(bv_ld[0:1, :], dr["bv"].unsqueeze(0))
    bv_b = ctx.sb("l1a_bv_b", [128, D], BF16)
    bsel_sb = ctx.sb("bsel", [128, ND, 16], BF16)
    ctx.dma(bsel_sb, ctx.dram["c_bsel_bf"].rearrange("(n p) h -> p n h", p=128))
    kmax = ctx.sb("l1a_kmax", [16, 1], F32)
    ctx.barrier()
    for half in range(2):
        i = nc.tensor.matmul(ctx.ps[0], ctx.sbuf["ones_row_f32"],
                             bv_ld[0:1, half * 512:(half + 1) * 512],
                             start=True, stop=True)
        ctx.inc(i, "l1b", 1)
        ctx.wait(nc.vector, "l1b")
        i = nc.vector.tensor_copy(bv_b[:, half * 512:(half + 1) * 512], ctx.ps[0])
        ctx.inc(i, "l1c", 1)
        ctx.wait(nc.tensor, "l1c")
    nc.vector.tensor_scalar(bq_col, bq_col, 0.125, None, OP.mult)
    nc.vector.memset(kmax, 0.0)

    emit_norm(ctx, rms=False, eps=1e-5)

    q_dram = ctx.dtmp("l1a_q", [D, T], BF16)
    k_dram = ctx.dtmp("l1a_k", [D, T], BF16)
    v_dram = ctx.dtmp("l1a_v", [T, D], BF16)

    ctx.arena_mark()
    wT = ctx.arena([128, ND, D], BF16)
    xnc = ctx.arena([128, ND, TC], BF16)
    ev = ctx.arena([128, TC], BF16)
    sq = ctx.arena([128, TC], BF16)
    vev = ctx.arena([128, D], BF16)

    def proj_fm(wname, bias_col, scale, out_dram, track_knorm):
        for dt in range(ND):
            ctx.dma_t(wT[:, dt, :], w_bf[wname][:, dt * 128:(dt + 1) * 128])
        ctx.barrier()
        for c in range(ctx.NC):
            sl = slice(c * TC, (c + 1) * TC)
            if c > 0:
                ctx.wait(nc.gpsimd, "pa_m")
            load_xn_chunk(ctx, xnc, c, sem="pa_x")
            ctx.wait(nc.tensor, "pa_x")
            for jt in range(ND):
                ps = ctx.ps[jt % 2]
                if jt >= 2:
                    ctx.wait(nc.tensor, "pa_e", ctx.cnt.get("pa_e", 0) - 1)
                for dt in range(ND):
                    i = nc.tensor.matmul(ps, wT[:, dt, jt * 128:(jt + 1) * 128],
                                         xnc[:, dt, :],
                                         start=(dt == 0), stop=(dt == ND - 1))
                ctx.inc(i, "pa_m", 1)
                ctx.wait(nc.vector, "pa_m")
                if ctx.cnt.get("pa_s", 0):
                    ctx.wait(nc.vector, "pa_s")
                i = nc.vector.tensor_scalar(ev, ps, scale,
                                            bias_col[:, jt:jt + 1],
                                            OP.mult, op1=OP.add)
                ctx.inc(i, "pa_e", 1)
                ctx.wait(nc.sync, "pa_e")
                ctx.store(out_dram[jt * 128:(jt + 1) * 128, sl], ev, sem="pa_s")
                if track_knorm:
                    ctx.wait(nc.scalar, "pa_e")
                    i = nc.scalar.activation(sq, ev, AF.Square)
                    ctx.inc(i, "pa_q", 1)
                    ctx.wait(nc.tensor, "pa_q")
                    i = nc.tensor.matmul(ctx.ps[4][0:16, :], bsel_sb[:, jt, :],
                                         sq, start=(jt == 0), stop=(jt == ND - 1),
                                         skip_group_check=True)
                    if jt == ND - 1:
                        ctx.inc(i, "pa_n", 1)
                        ctx.wait(nc.vector, "pa_n")
                        i = nc.vector.reduce_max(ctx.tmp_sb[0:16, 0, 0:1],
                                                 ctx.ps[4][0:16, :], axis=AX.X)
                        ctx.vsync(i)
                        i = nc.vector.tensor_tensor(kmax, kmax,
                                                    ctx.tmp_sb[0:16, 0, 0:1],
                                                    OP.max)
                        ctx.inc(i, "pa_k", 1)
                        ctx.wait(nc.tensor, "pa_k")
            ctx.barrier()

    proj_fm("Wq", bq_col, 0.125, q_dram, False)
    proj_fm("Wk", bk_col, 1.0, k_dram, True)

    for dt in range(ND):
        ctx.dma_t(wT[:, dt, :], w_bf["Wv"][:, dt * 128:(dt + 1) * 128])
    ctx.barrier()
    for tt in range(ctx.NT):
        tsl = slice(tt * 128, (tt + 1) * 128)
        c = (tt * 128) // TC
        if tt % (TC // 128) == 0:
            if tt > 0:
                ctx.wait(nc.gpsimd, "pv_m")
            load_xn_chunk(ctx, xnc, c, sem="pa_x")
            ctx.wait(nc.tensor, "pa_x")
        toff = (tt * 128) % TC
        if ctx.cnt.get("pv_s", 0):
            ctx.wait(nc.vector, "pv_s")
        for half in range(2):
            ps = ctx.ps[half]
            for dt in range(ND):
                i = nc.tensor.matmul(
                    ps, xnc[:, dt, toff:toff + 128],
                    wT[:, dt, half * 512:(half + 1) * 512],
                    start=(dt == 0), stop=(dt == ND - 1))
            ctx.inc(i, "pv_m", 1)
            ctx.wait(nc.vector, "pv_m")
            i = nc.vector.tensor_tensor(vev[:, half * 512:(half + 1) * 512], ps,
                                        bv_b[:, half * 512:(half + 1) * 512],
                                        OP.add)
            ctx.inc(i, "pv_e", 1)
        ctx.wait(nc.sync, "pv_e")
        ctx.store(v_dram[tsl, :], vev, sem="pv_s")
        ctx.wait(nc.tensor, "pv_e")
    ctx.barrier()
    i = nc.scalar.activation(kmax, kmax, AF.Sqrt)
    ctx.inc(i, "pk_a", 1)
    ctx.barrier()
    ctx.arena_release()

    consts = ctx.consts_meta
    mmap = consts["_l1m_map"]
    ctx.arena_mark()
    ctx.borrow = True
    q_all = ctx.arena([128, ND, 512], BF16)
    y_cat = ctx.arena([128, ND, 512], BF16)
    MAXK = max(len(l1_klist(qs, T)) for qs in range(NQS))
    k_all = ctx.arena([128, ND, MAXK * 128], BF16)
    vext = ctx.arena([128, MAXK, 4 * 65], BF16)
    vld = ctx.arena([128, MAXK, 256], BF16)
    esel16 = ctx.arena([16, 16 * 128], BF16)
    maskb = [ctx.arena([128, 512], BF16) for _ in range(2)]
    ptb = [ctx.arena([128, 512], BF16) for _ in range(2)]
    woT = ctx.arena([128, ND, 128], BF16)
    hmask = ctx.arena([16, D], BF16)
    nq16 = ctx.arena([16, 512], F32)
    negC = ctx.arena([16, 512], BF16)
    zrec = ctx.arena([1, 512], F32)
    zb = ctx.arena([128, 512], F32)
    shoff = (ctx._arena_off + 31) & ~31
    qsq = ctx.arena([128, ND, 512], BF16)
    p0 = ctx.arena_at(shoff, [16, T], BF16)
    q0s = ctx.arena([128, 4, 512], BF16)
    ctx.dma(esel16, ctx.dram["c_esel16_bf"])
    ctx.dma(hmask, ctx.dram["c_hmask_bf"])
    ctx.barrier()

    qv = ctx.dram["l1a_q"].rearrange("(n p) t -> p n t", p=128)
    kv = ctx.dram["l1a_k"].rearrange("(n p) t -> p n t", p=128)
    for qs in range(NQS):
        qsl = slice(qs * 512, (qs + 1) * 512)
        klist = l1_klist(qs, T)
        NK = len(klist)
        ctx.dma(q_all, qv[:, :, qsl], sem="pb_l")
        for ki, kt in enumerate(klist):
            ctx.dma(k_all[:, :, ki * 128:(ki + 1) * 128],
                    kv[:, :, kt * 128:(kt + 1) * 128], sem="pb_l")
        ctx.wait(nc.scalar, "pb_l")
        for dt in range(ND):
            i = nc.scalar.activation(qsq[:, dt, :], q_all[:, dt, :], AF.Square)
        ctx.inc(i, "pb_sq", 1)
        ctx.wait(nc.tensor, "pb_sq")
        if qs > 0:
            ctx.wait(nc.tensor, "pb_nc", qs)
        for dt in range(ND):
            i = nc.tensor.matmul(ctx.ps[6][0:16, :], bsel_sb[:, dt, :],
                                 qsq[:, dt, :],
                                 start=(dt == 0), stop=(dt == ND - 1))
        ctx.inc(i, "pb_nm", 1)
        ctx.wait(nc.scalar, "pb_nm")
        i = nc.scalar.activation(nq16, ctx.ps[6][0:16, :], AF.Sqrt)
        ctx.inc(i, "pb_na", 1)
        ctx.wait(nc.vector, "pb_na")
        i = nc.vector.tensor_scalar(nq16, nq16, kmax, None, OP.mult)
        i = nc.vector.tensor_scalar(negC, nq16, -1.0, None, OP.mult)
        ctx.inc(i, "pb_nc", 1)
        ctx.wait(nc.tensor, "pb_nc")
        for hpair in range(2):
            if ctx.cnt.get("pb_hx", 0):
                ctx.wait(nc.gpsimd, "pb_hx")
            for ki, kt in enumerate(klist):
                ctx.dma(vld[:, ki, :],
                        ctx.dram["l1a_v"][kt * 128:(kt + 1) * 128,
                                          hpair * 512:hpair * 512 + 256],
                        sem="pb_v")
                ctx.dma(vld[:, ki, :][0:0 + 128, 0:0],
                        ctx.dram["l1a_v"][kt * 128:kt * 128,
                                          0:0], sem="pb_v") if False else None
            for hg4 in range(2):
                heads = [hpair * 8 + hg4 * 4 + j for j in range(4)]
                if hg4 == 1:
                    if ctx.cnt.get("pb_hx", 0):
                        ctx.wait(nc.gpsimd, "pb_hx")
                    for ki, kt in enumerate(klist):
                        ctx.dma(vld[:, ki, :],
                                ctx.dram["l1a_v"][kt * 128:(kt + 1) * 128,
                                                  hpair * 512 + 256:
                                                  hpair * 512 + 512],
                                sem="pb_v")
                ctx.wait(nc.vector, "pb_v")
                if ctx.cnt.get("pb_av", 0):
                    ctx.wait(nc.vector, "pb_av")
                for ki in range(NK):
                    dst = vext[:, ki, :].rearrange("p (h e) -> p h e", h=4)
                    src = vld[:, ki, :].rearrange("p (h e) -> p h e", h=4)
                    nc.vector.memset(vext[:, ki, :], 1.0)
                    i = nc.vector.tensor_copy(dst[:, :, 0:64], src)
                ctx.inc(i, "pb_hx", 1)
                ctx.wait(nc.tensor, "pb_hx")
                if ctx.cnt.get("pb_zc", 0):
                    ctx.wait(nc.tensor, "pb_zc")
                bSm = ctx.cnt.get("pb_sm", 0)
                bPe = ctx.cnt.get("pb_pe", 0)
                bPt = ctx.cnt.get("pb_pt", 0)
                bAv = ctx.cnt.get("pb_av", 0)
                for ki, kt in enumerate(klist):
                    mid = mmap[(qs, kt)]
                    b = ki % 2
                    if ki >= 2:
                        ctx.wait(nc.gpsimd, "pb_pt", bPt + 4 * (ki - 1))
                    elif bPt > 0:
                        ctx.wait(nc.gpsimd, "pb_pt", bPt)
                    ctx.dma(maskb[b], ctx.dram[f"c_l1m_{mid}"], sem="pb_mk")
                    for hh, h in enumerate(heads):
                        idx = ki * 4 + hh
                        sb_ = idx % 2
                        pss = ctx.ps[sb_]
                        if idx >= 2:
                            ctx.wait(nc.tensor, "pb_pe", bPe + idx - 1)
                        jt, po = h // 2, (h % 2) * 64
                        i = nc.tensor.matmul(
                            pss, k_all[po:po + 64, jt, ki * 128:(ki + 1) * 128],
                            q_all[po:po + 64, jt, :], start=True, stop=False,
                            skip_group_check=True)
                        i = nc.tensor.matmul(
                            pss, esel16[:, h * 128:(h + 1) * 128], negC,
                            start=False, stop=True, skip_group_check=True)
                        ctx.inc(i, "pb_sm", 1)
                        ctx.wait(nc.scalar, "pb_sm")
                        if idx >= 2:
                            ctx.wait(nc.scalar, "pb_av", bAv + idx - 1)
                        i = nc.scalar.activation(ptb[sb_], pss, AF.Exp)
                        ctx.inc(i, "pb_pe", 1)
                        ctx.wait(nc.vector, "pb_pe")
                        if hh == 0:
                            ctx.wait(nc.vector, "pb_mk")
                        i = nc.vector.tensor_tensor(ptb[sb_], ptb[sb_],
                                                    maskb[b], OP.mult)
                        ctx.inc(i, "pb_pt", 1)
                        ctx.wait(nc.tensor, "pb_pt")
                        i = nc.tensor.matmul(ctx.ps[4 + hh][0:65, :],
                                             vext[:, ki, hh * 65:(hh + 1) * 65],
                                             ptb[sb_], start=(ki == 0),
                                             stop=(ki == NK - 1),
                                             skip_group_check=True)
                        ctx.inc(i, "pb_av", 1)
                        if L1_SER_INNER:
                            for s_ in ["pb_sm", "pb_pe", "pb_pt", "pb_av"]:
                                ctx.wait(nc.tensor, s_)
                                ctx.wait(nc.vector, s_)
                                ctx.wait(nc.scalar, s_)
                        if ki == NK - 1:
                            ctx.inc(nc.tensor.matmul(
                                ctx.ps[4 + hh][0:1, 0:1],
                                ctx.sbuf["ones_row_bf"][0:1, 0:1],
                                ctx.sbuf["ones_row_bf"][0:1, 0:1],
                                start=False, stop=False,
                                skip_group_check=True), "pb_dum", 1) if False \
                                else None
                for hh, h in enumerate(heads):
                    psy = ctx.ps[4 + hh]
                    ctx.wait(nc.vector, "pb_av")
                    i = nc.vector.reciprocal(zrec, psy[64:65, :])
                    ctx.inc(i, "pb_zr", 1)
                    ctx.wait(nc.tensor, "pb_zr")
                    if ctx.cnt.get("pb_zc", 0):
                        ctx.wait(nc.tensor, "pb_zc")
                    i = nc.tensor.matmul(ctx.ps[hh % 2], ctx.sbuf["ones_row_f32"],
                                         zrec, start=True, stop=True,
                                         skip_group_check=True)
                    ctx.inc(i, "pb_zb", 1)
                    ctx.wait(nc.vector, "pb_zb")
                    i = nc.vector.tensor_copy(zb, ctx.ps[hh % 2])
                    ctx.inc(i, "pb_zc", 1)
                    jt, po = h // 2, (h % 2) * 64
                    i = nc.vector.tensor_tensor(y_cat[po:po + 64, jt, :],
                                                psy[0:64, :], zb[0:64, :],
                                                OP.mult)
                    ctx.inc(i, "pb_yc", 1)
                    if L1_SER_EVAC:
                        ctx.barrier()
                ctx.wait(nc.tensor, "pb_yc")
        if qs == 0:
            _l1_q0(ctx, qv, kv, bsel_sb, hmask, kmax, p0, q0s, maskb, y_cat,
                   q_all, nq16)
        for dt2 in range(ND):
            if dt2 > 0:
                ctx.wait(nc.sync, "pb_om")
            for et in range(ND):
                ctx.dma_t(woT[:, et, :],
                          w_bf["Wo"][dt2 * 128:(dt2 + 1) * 128,
                                     et * 128:(et + 1) * 128], sem="pb_w")
            ctx.wait(nc.tensor, "pb_w")
            pso = ctx.ps[2 + dt2 % 2]
            if dt2 >= 2:
                ctx.wait(nc.tensor, "pb_oa", ctx.cnt.get("pb_oa", 0) - 1)
            for et in range(ND):
                i = nc.tensor.matmul(pso, woT[:, et, :], y_cat[:, et, :],
                                     start=(et == 0), stop=(et == ND - 1))
            ctx.inc(i, "pb_om", 1)
            ctx.wait(nc.vector, "pb_om")
            nc.vector.tensor_tensor(ctx.tmp_sb[:, 0, :], pso,
                                    ctx.x[:, dt2, qsl], OP.add)
            i = nc.vector.tensor_scalar(ctx.x[:, dt2, qsl], ctx.tmp_sb[:, 0, :],
                                        bo_col[:, dt2:dt2 + 1], None, OP.add)
            ctx.inc(i, "pb_oa", 1)
        ctx.barrier()
    ctx.borrow = False
    ctx.arena_release()


def _l1_q0(ctx, qv, kv, bsel_sb, hmask, kmax, p0, q0s, maskb, y_cat, q_all,
           nq16):
    nc = ctx.nc
    T = ctx.T
    kch = q_all
    ctx.wait(nc.gpsimd, "pb_sm", ctx.cnt.get("pb_sm", 0))
    ctx.wait(nc.gpsimd, "pb_av", ctx.cnt.get("pb_av", 0))
    q0c = ctx.tmp_sb.bitcast(BF16)[:, 0, 0:ND]
    sqc = ctx.tmp_sb.bitcast(BF16)[:, 1, 0:ND]
    ctx.dma(q0c, qv[:, :, 0:1].rearrange("p n o -> p (n o)"), sem="q0l",
            allow_slow_non_contiguous=True)
    ctx.wait(nc.vector, "q0l")
    q0bd = q0s
    for dt in range(ND):
        i = nc.vector.tensor_tensor(
            q0s[:, dt // 4, (dt % 4) * 16:(dt % 4) * 16 + 16],
            bsel_sb[:, dt, :],
            q0c[:, dt:dt + 1].to_broadcast([128, 16]), OP.mult)
    i2 = nc.vector.tensor_tensor(sqc, q0c, q0c, OP.mult)
    ctx.inc(i2, "q0b", 1)
    ctx.wait(nc.tensor, "q0b")
    for dt in range(ND):
        i = nc.tensor.matmul(ctx.ps[6][0:16, 0:1], bsel_sb[:, dt, :],
                             sqc[:, dt:dt + 1], start=(dt == 0),
                             stop=(dt == ND - 1), skip_group_check=True)
    ctx.inc(i, "q0m", 1)
    ctx.wait(nc.scalar, "q0m")
    nC0 = nq16[0:16, 8:9]
    i = nc.scalar.activation(nC0, ctx.ps[6][0:16, 0:1], AF.Sqrt)
    ctx.inc(i, "q0a", 1)
    ctx.wait(nc.vector, "q0a")
    i = nc.vector.tensor_scalar(nC0, nC0, kmax, None, OP.mult)
    ctx.vsync(i)
    i = nc.vector.tensor_scalar(nC0, nC0, -1.0, None, OP.mult)
    ctx.inc(i, "q0n", 1)
    ctx.wait(nc.scalar, "q0n")
    for c in range(ctx.NC):
        sl = slice(c * TC, (c + 1) * TC)
        ctx.dma(q0s[:, 2, :], kv[:, 0:4, c * TC:(c + 1) * TC]
                [:, :, 0:128].rearrange("p n t -> p (n t)"), sem="q0k") \
            if False else None
        ctx.dma(kch, kv[:, :, sl], sem="q0k")
        ctx.wait(nc.tensor, "q0k")
        if c > 0:
            ctx.wait(nc.tensor, "q0p", c)
        for dt in range(ND):
            i = nc.tensor.matmul(ctx.ps[7][0:16, :],
                                 q0s[:, dt // 4, (dt % 4) * 16:(dt % 4) * 16 + 16],
                                 kch[:, dt, :], start=(dt == 0),
                                 stop=(dt == ND - 1), skip_group_check=True)
        ctx.inc(i, "q0sm", 1)
        ctx.wait(nc.scalar, "q0sm")
        i = nc.scalar.activation(p0[:, sl], ctx.ps[7][0:16, :], AF.Exp, bias=nC0)
        ctx.inc(i, "q0p", 1)
    ctx.wait(nc.vector, "q0p")
    z0 = nq16[0:16, 4:5]
    i = nc.vector.reduce_sum(z0, p0, axis=AX.X)
    ctx.vsync(i)
    i = nc.vector.reciprocal(z0, z0)
    ctx.inc(i, "q0z", 1)
    for tt in range(ctx.NT):
        tsl = slice(tt * 128, (tt + 1) * 128)
        if tt > 0:
            ctx.wait(nc.tensor, "q0e", tt)
            ctx.wait(nc.vector, "q0av", tt)
            ctx.wait(nc.gpsimd, "q0av", tt)
        i = nc.tensor.transpose(ctx.ps[6].bitcast(BF16)[:, 0:16], p0[:, tsl],
                                ctx.sbuf["ident_bf"][0:16, 0:16])
        ctx.inc(i, "q0t", 1)
        ctx.wait(nc.vector, "q0t")
        i = nc.vector.tensor_copy(q0s[:, 1, 0:16],
                                  ctx.ps[6].bitcast(BF16)[:, 0:16])
        ctx.inc(i, "q0e", 1)
        ctx.dma(q0s[:, 2, :], ctx.dram["l1a_v"][tsl, 0:512], sem="q0v")
        ctx.dma(q0s[:, 3, :], ctx.dram["l1a_v"][tsl, 512:1024], sem="q0v")
        ctx.wait(nc.tensor, "q0v")
        ctx.wait(nc.tensor, "q0e")
        for half in range(2):
            i = nc.tensor.matmul(ctx.ps[4 + half][0:16, :], q0s[:, 1, 0:16],
                                 q0s[:, 2 + half, :],
                                 start=(tt == 0), stop=(tt == ctx.NT - 1),
                                 skip_group_check=True)
        ctx.inc(i, "q0av", 1)
    ctx.wait(nc.vector, "q0av")
    y0f = ctx.tmp_sb[0:16, :, :]
    for half in range(2):
        i = nc.vector.tensor_tensor(y0f[:, half, :], ctx.ps[4 + half][0:16, :],
                                    hmask[:, half * 512:(half + 1) * 512],
                                    OP.mult)
        i = nc.vector.tensor_scalar(y0f[:, half, :], y0f[:, half, :], z0,
                                    None, OP.mult)
    ctx.inc(i, "q0f", 1)
    ctx.wait(nc.tensor, "q0f")
    for jt in range(ND):
        half, off = jt // 4, (jt % 4) * 128
        if jt > 0:
            ctx.wait(nc.tensor, "q0h", ctx.cnt.get("q0h", 0))
        i = nc.tensor.matmul(ctx.ps[6][:, 0:1], y0f[:, half, off:off + 128],
                             ctx.sbuf["ones_col_f32"][0:16, :],
                             start=True, stop=True, skip_group_check=True)
        ctx.inc(i, "q0g", 1)
        ctx.wait(nc.vector, "q0g")
        i = nc.vector.tensor_copy(y_cat[:, jt, 0:1], ctx.ps[6][:, 0:1])
        ctx.inc(i, "q0h", 1)
    ctx.wait(nc.tensor, "q0h")


def emit_l3_attn(ctx):
    nc = ctx.nc
    T = ctx.T
    NQS = T // 512
    PAD = 638
    qkv_w = ctx.din("L3_attn_qkv_w", [3 * D, D], F32)
    qkv_b = ctx.din("L3_attn_qkv_b", [3 * D], F32)
    proj_w = ctx.din("L3_attn_proj_w", [D, D], F32)
    proj_b = ctx.din("L3_attn_proj_b", [D], F32)
    rpe = ctx.din("L3_attn_rel_pos_emb", [2047, D], F32)
    qkv_bf = ctx.dtmp("l3a_qkv_bf", [3 * D, D], BF16)
    proj_bf = ctx.dtmp("l3a_proj_bf", [D, D], BF16)
    rpe_bf = ctx.dtmp("l3a_rpe_bf", [2047, D], BF16)
    ctx.dma(qkv_bf, qkv_w)
    ctx.dma(proj_bf, proj_w)
    ctx.dma(rpe_bf, rpe)
    bq_col = emit_feature_cols(ctx, "l3a_bq", qkv_b[0:D], D)
    bk_col = emit_feature_cols(ctx, "l3a_bk", qkv_b[D:2 * D], D)
    bo_col = emit_feature_cols(ctx, "l3a_bo", proj_b, D)
    bv_ld = ctx.sb("ld4k", [8, D], F32)
    ctx.dma(bv_ld[0:1, :], qkv_b[2 * D:3 * D].unsqueeze(0))
    bv_b = ctx.sb("l1a_bv_b", [128, D], BF16)
    bsel_sb = ctx.sb("bsel", [128, ND, 16], BF16)
    ctx.dma(bsel_sb, ctx.dram["c_bsel_bf"].rearrange("(n p) h -> p n h", p=128))
    kmax = ctx.sb("l1a_kmax", [16, 1], F32)
    extmax = ctx.sb("l3a_extmax", [16, 1], F32)
    extp2 = ctx.dtmp("l3a_extp2", [16, PAD + T], BF16)
    ctx.barrier()
    for half in range(2):
        i = nc.tensor.matmul(ctx.ps[0], ctx.sbuf["ones_row_f32"],
                             bv_ld[0:1, half * 512:(half + 1) * 512],
                             start=True, stop=True)
        ctx.inc(i, "l3b", 1)
        ctx.wait(nc.vector, "l3b")
        i = nc.vector.tensor_copy(bv_b[:, half * 512:(half + 1) * 512], ctx.ps[0])
        ctx.inc(i, "l3c", 1)
        ctx.wait(nc.tensor, "l3c")
    nc.vector.tensor_scalar(bq_col, bq_col, 0.125, None, OP.mult)
    nc.vector.memset(kmax, 0.0)

    ctx.arena_mark()
    rpet = ctx.arena([128, ND, 128], BF16)
    tabt = ctx.arena([128, 16], F32)
    exts = ctx.arena([16, PAD + T], BF16)
    nc.vector.memset(exts, 0.0)
    for ti, rt in enumerate(range(7, 16)):
        r0 = rt * 128
        nrow = min(128, 2047 - r0)
        if ti > 0:
            ctx.wait(nc.gpsimd, "e3t", ctx.cnt.get("e3t", 0))
        if nrow < 128:
            ctx.wait(nc.gpsimd, "e3v", ctx.cnt.get("e3v", 0))
            nc.gpsimd.memset(rpet, 0.0)
        ctx.dma(rpet[0:nrow, :, :],
                rpe_bf[r0:r0 + nrow, :].rearrange("r (n e) -> r n e", n=ND),
                sem="e3l")
        ctx.wait(nc.vector, "e3l")
        if ti > 0:
            ctx.wait(nc.vector, "e3t")
        i = nc.vector.reduce_sum(
            tabt.rearrange("p (h o) -> p h o", o=1),
            rpet.rearrange("p n e -> p (n e)").rearrange("p (h s) -> p h s", h=16),
            axis=AX.X)
        ctx.inc(i, "e3v", 1)
        ctx.wait(nc.tensor, "e3v")
        if ctx.cnt.get("e3c", 0):
            ctx.wait(nc.tensor, "e3c")
        i = nc.tensor.transpose(ctx.ps[6][0:16, 0:128], tabt,
                                ctx.sbuf["ident_f32"])
        ctx.inc(i, "e3t", 1)
        ctx.wait(nc.vector, "e3t")
        j0 = r0 - 1023
        lo = max(0, -j0)
        hi = min(128, 1024 - j0, T - j0)
        if hi > lo:
            i = nc.vector.tensor_copy(exts[:, PAD + j0 + lo:PAD + j0 + hi],
                                      ctx.ps[6][0:16, lo:hi])
            ctx.inc(i, "e3c", 1)
        if rt == 15 and T > 1024:
            i = nc.vector.tensor_copy(
                exts[:, PAD + 1024:PAD + T],
                ctx.ps[6][0:16, 126:127].to_broadcast([16, T - 1024]))
            ctx.inc(i, "e3c", 1)
    i = nc.vector.reduce_max(extmax, exts, axis=AX.X, apply_absolute_value=True)
    ctx.inc(i, "e3c", 1)
    ctx.wait(nc.sync, "e3c")
    ctx.store(extp2, exts)
    ctx.barrier()
    L = PAD + T
    master_d = ctx.dtmp("l3a_master", [128, 16, L], BF16)
    for kk in range(128):
        ctx.dma(master_d[kk:kk + 1, :, kk:L].squeeze(0),
                extp2[:, 0:L - kk], sem="e3m")
    ctx.barrier()
    ctx.arena_release()

    emit_norm(ctx, rms=False, eps=1e-5)

    q_dram = ctx.dtmp("l3a_q", [D, T], BF16)
    k_dram = ctx.dtmp("l3a_k", [D, T], BF16)
    v_dram = ctx.dtmp("l3a_v", [T, D], BF16)

    ctx.arena_mark()
    wT = ctx.arena([128, ND, D], BF16)
    xnc = ctx.arena([128, ND, TC], BF16)
    ev = ctx.arena([128, TC], BF16)
    sq = ctx.arena([128, TC], BF16)
    vev = ctx.arena([128, D], BF16)

    def proj_fm(row0, bias_col, scale, out_dram, track_knorm):
        for dt in range(ND):
            ctx.dma_t(wT[:, dt, :],
                      qkv_bf[row0:row0 + D, dt * 128:(dt + 1) * 128])
        ctx.barrier()
        for c in range(ctx.NC):
            sl = slice(c * TC, (c + 1) * TC)
            if c > 0:
                ctx.wait(nc.gpsimd, "pa_m")
            load_xn_chunk(ctx, xnc, c, sem="pa_x")
            ctx.wait(nc.tensor, "pa_x")
            for jt in range(ND):
                ps = ctx.ps[jt % 2]
                if jt >= 2:
                    ctx.wait(nc.tensor, "pa_e", ctx.cnt.get("pa_e", 0) - 1)
                for dt in range(ND):
                    i = nc.tensor.matmul(ps, wT[:, dt, jt * 128:(jt + 1) * 128],
                                         xnc[:, dt, :],
                                         start=(dt == 0), stop=(dt == ND - 1))
                ctx.inc(i, "pa_m", 1)
                ctx.wait(nc.vector, "pa_m")
                if ctx.cnt.get("pa_s", 0):
                    ctx.wait(nc.vector, "pa_s")
                i = nc.vector.tensor_scalar(ev, ps, scale,
                                            bias_col[:, jt:jt + 1],
                                            OP.mult, op1=OP.add)
                ctx.inc(i, "pa_e", 1)
                ctx.wait(nc.sync, "pa_e")
                ctx.store(out_dram[jt * 128:(jt + 1) * 128, sl], ev, sem="pa_s")
                if track_knorm:
                    ctx.wait(nc.scalar, "pa_e")
                    i = nc.scalar.activation(sq, ev, AF.Square)
                    ctx.inc(i, "pa_q", 1)
                    ctx.wait(nc.tensor, "pa_q")
                    i = nc.tensor.matmul(ctx.ps[4][0:16, :], bsel_sb[:, jt, :],
                                         sq, start=(jt == 0), stop=(jt == ND - 1),
                                         skip_group_check=True)
                    if jt == ND - 1:
                        ctx.inc(i, "pa_n", 1)
                        ctx.wait(nc.vector, "pa_n")
                        i = nc.vector.reduce_max(ctx.tmp_sb[0:16, 0, 0:1],
                                                 ctx.ps[4][0:16, :], axis=AX.X)
                        ctx.vsync(i)
                        i = nc.vector.tensor_tensor(kmax, kmax,
                                                    ctx.tmp_sb[0:16, 0, 0:1],
                                                    OP.max)
                        ctx.inc(i, "pa_k", 1)
                        ctx.wait(nc.tensor, "pa_k")
            ctx.barrier()

    proj_fm(0, bq_col, 0.125, q_dram, False)
    proj_fm(D, bk_col, 1.0, k_dram, True)
    for dt in range(ND):
        ctx.dma_t(wT[:, dt, :], qkv_bf[2 * D:3 * D, dt * 128:(dt + 1) * 128])
    ctx.barrier()
    for tt in range(ctx.NT):
        tsl = slice(tt * 128, (tt + 1) * 128)
        c = (tt * 128) // TC
        if tt % (TC // 128) == 0:
            if tt > 0:
                ctx.wait(nc.gpsimd, "pv_m")
            load_xn_chunk(ctx, xnc, c, sem="pa_x")
            ctx.wait(nc.tensor, "pa_x")
        toff = (tt * 128) % TC
        if ctx.cnt.get("pv_s", 0):
            ctx.wait(nc.vector, "pv_s")
        for half in range(2):
            ps = ctx.ps[half]
            for dt in range(ND):
                i = nc.tensor.matmul(
                    ps, xnc[:, dt, toff:toff + 128],
                    wT[:, dt, half * 512:(half + 1) * 512],
                    start=(dt == 0), stop=(dt == ND - 1))
            ctx.inc(i, "pv_m", 1)
            ctx.wait(nc.vector, "pv_m")
            i = nc.vector.tensor_tensor(vev[:, half * 512:(half + 1) * 512], ps,
                                        bv_b[:, half * 512:(half + 1) * 512],
                                        OP.add)
            ctx.inc(i, "pv_e", 1)
        ctx.wait(nc.sync, "pv_e")
        ctx.store(v_dram[tsl, :], vev, sem="pv_s")
        ctx.wait(nc.tensor, "pv_e")
    ctx.barrier()
    i = nc.scalar.activation(kmax, kmax, AF.Sqrt)
    ctx.inc(i, "pk_a", 1)
    ctx.barrier()
    ctx.arena_release()

    ctx.arena_mark()
    ctx.borrow = True
    q_all = ctx.arena([128, ND, 512], BF16)
    y_cat = ctx.arena([128, ND, 512], BF16)
    ktile = [ctx.arena([128, ND, 128], BF16) for _ in range(2)]
    vld = ctx.arena([128, 256], BF16)
    vext = [ctx.arena([128, 4 * 65], BF16) for _ in range(2)]
    esel16 = ctx.arena([16, 16 * 128], BF16)
    l3m = ctx.arena([128, 4, 512], BF16)
    biasb = [ctx.arena([128, 512], BF16) for _ in range(2)]
    ptb = [ctx.arena([128, 512], BF16) for _ in range(2)]
    woT = ctx.arena([128, ND, 128], BF16)
    nq16 = ctx.arena([16, 512], F32)
    negC = ctx.arena([16, 512], BF16)
    zrec = ctx.arena([1, 512], F32)
    zb = ctx.arena([128, 512], F32)
    qsq = ctx.arena([128, ND, 512], BF16)
    ctx.dma(esel16, ctx.dram["c_esel16_bf"])
    for r in range(4):
        ctx.dma(l3m[:, r, :], ctx.dram[f"c_l3m_{r}"])
    ctx.barrier()

    qv = ctx.dram["l3a_q"].rearrange("(n p) t -> p n t", p=128)
    kv = ctx.dram["l3a_k"].rearrange("(n p) t -> p n t", p=128)
    for qs in range(NQS):
        qsl = slice(qs * 512, (qs + 1) * 512)
        NK = 4 * qs + 4
        ctx.dma(q_all, qv[:, :, qsl], sem="pb_l")
        ctx.wait(nc.scalar, "pb_l")
        for dt in range(ND):
            i = nc.scalar.activation(qsq[:, dt, :], q_all[:, dt, :], AF.Square)
        ctx.inc(i, "pb_sq", 1)
        ctx.wait(nc.tensor, "pb_sq")
        if qs > 0:
            ctx.wait(nc.tensor, "pb_nc", qs)
        for dt in range(ND):
            i = nc.tensor.matmul(ctx.ps[6][0:16, :], bsel_sb[:, dt, :],
                                 qsq[:, dt, :],
                                 start=(dt == 0), stop=(dt == ND - 1))
        ctx.inc(i, "pb_nm", 1)
        ctx.wait(nc.scalar, "pb_nm")
        i = nc.scalar.activation(nq16, ctx.ps[6][0:16, :], AF.Sqrt)
        ctx.inc(i, "pb_na", 1)
        ctx.wait(nc.vector, "pb_na")
        i = nc.vector.tensor_scalar(nq16, nq16, kmax, None, OP.mult)
        i = nc.vector.tensor_scalar(nq16, nq16, extmax, None, OP.add)
        i = nc.vector.tensor_scalar(negC, nq16, -1.0, None, OP.mult)
        ctx.inc(i, "pb_nc", 1)
        ctx.wait(nc.tensor, "pb_nc")
        for hg in range(4):
            heads = [hg * 4 + j for j in range(4)]
            bSm = ctx.cnt.get("pb_sm", 0)
            bPe = ctx.cnt.get("pb_pe", 0)
            bPt = ctx.cnt.get("pb_pt", 0)
            bAv = ctx.cnt.get("pb_av", 0)
            bKt = ctx.cnt.get("pb_kl", 0)
            bVx = ctx.cnt.get("pb_hx", 0)
            for ki in range(NK):
                kb = ki % 2
                if ki >= 2:
                    ctx.wait(nc.gpsimd, "pb_sm", bSm + 4 * (ki - 1))
                elif ctx.cnt.get("pb_sm", 0) > bSm - 1 and (qs + hg) > 0:
                    ctx.wait(nc.gpsimd, "pb_sm", bSm)
                ctx.dma(ktile[kb], kv[:, :, ki * 128:(ki + 1) * 128], sem="pb_kl")
                if ki >= 2:
                    ctx.wait(nc.gpsimd, "pb_av", bAv + 4 * (ki - 1))
                elif (qs + hg) > 0:
                    ctx.wait(nc.gpsimd, "pb_av", bAv)
                if ctx.cnt.get("pb_hx", 0):
                    ctx.wait(nc.gpsimd, "pb_hx")
                ctx.dma(vld, ctx.dram["l3a_v"][ki * 128:(ki + 1) * 128,
                                               hg * 256:(hg + 1) * 256],
                        sem="pb_vl")
                ctx.wait(nc.vector, "pb_vl")
                dst = vext[kb].rearrange("p (h e) -> p h e", h=4)
                srcv = vld.rearrange("p (h e) -> p h e", h=4)
                nc.vector.memset(vext[kb], 1.0)
                i = nc.vector.tensor_copy(dst[:, :, 0:64], srcv)
                ctx.inc(i, "pb_hx", 1)
                ctx.wait(nc.tensor, "pb_hx")
                ctx.wait(nc.tensor, "pb_kl")
                for hh, h in enumerate(heads):
                    idx = ki * 4 + hh
                    sb_ = idx % 2
                    pss = ctx.ps[sb_]
                    w0 = PAD + qs * 512 - ki * 128
                    bb = biasb[idx % 2]
                    if idx >= 2:
                        ctx.wait(nc.sync, "pb_sm", bSm + idx - 1)
                    elif (qs + hg) > 0:
                        ctx.wait(nc.sync, "pb_sm", bSm)
                    src = ctx.dram["l3a_master"][:, h, w0:w0 + 512]
                    ctx.dma(bb, src, q=nc.sync, sem="pb_bl")
                    if idx >= 2:
                        ctx.wait(nc.tensor, "pb_pe", bPe + idx - 1)
                    jt, po = h // 2, (h % 2) * 64
                    i = nc.tensor.matmul(
                        pss, ktile[kb][po:po + 64, jt, :],
                        q_all[po:po + 64, jt, :], start=True, stop=False,
                        skip_group_check=True)
                    i = nc.tensor.matmul(
                        pss, esel16[:, h * 128:(h + 1) * 128], negC,
                        start=False, stop=False, skip_group_check=True)
                    ctx.wait(nc.tensor, "pb_bl")
                    i = nc.tensor.matmul(
                        pss, ctx.sbuf["ident_bf"], bb,
                        start=False, stop=True, skip_group_check=True)
                    ctx.inc(i, "pb_sm", 1)
                    ctx.wait(nc.scalar, "pb_sm")
                    if idx >= 2:
                        ctx.wait(nc.scalar, "pb_av", bAv + idx - 1)
                    i = nc.scalar.activation(ptb[sb_], pss, AF.Exp)
                    ctx.inc(i, "pb_pe", 1)
                    r = ki - 4 * qs
                    if 0 <= r <= 3:
                        ctx.wait(nc.vector, "pb_pe")
                        i = nc.vector.tensor_tensor(ptb[sb_], ptb[sb_],
                                                    l3m[:, r, :], OP.mult)
                        ctx.inc(i, "pb_pt", 1)
                        ctx.wait(nc.tensor, "pb_pt")
                    else:
                        ctx.wait(nc.tensor, "pb_pe")
                    i = nc.tensor.matmul(ctx.ps[4 + hh][0:65, :],
                                         vext[kb][:, hh * 65:(hh + 1) * 65],
                                         ptb[sb_], start=(ki == 0),
                                         stop=(ki == NK - 1),
                                         skip_group_check=True)
                    ctx.inc(i, "pb_av", 1)
            for hh, h in enumerate(heads):
                psy = ctx.ps[4 + hh]
                ctx.wait(nc.vector, "pb_av")
                i = nc.vector.reciprocal(zrec, psy[64:65, :])
                ctx.inc(i, "pb_zr", 1)
                ctx.wait(nc.tensor, "pb_zr")
                if ctx.cnt.get("pb_zc", 0):
                    ctx.wait(nc.tensor, "pb_zc")
                i = nc.tensor.matmul(ctx.ps[hh % 2], ctx.sbuf["ones_row_f32"],
                                     zrec, start=True, stop=True,
                                     skip_group_check=True)
                ctx.inc(i, "pb_zb", 1)
                ctx.wait(nc.vector, "pb_zb")
                i = nc.vector.tensor_copy(zb, ctx.ps[hh % 2])
                ctx.inc(i, "pb_zc", 1)
                jt, po = h // 2, (h % 2) * 64
                i = nc.vector.tensor_tensor(y_cat[po:po + 64, jt, :],
                                            psy[0:64, :], zb[0:64, :], OP.mult)
                ctx.inc(i, "pb_yc", 1)
            ctx.wait(nc.tensor, "pb_yc")
        for dt2 in range(ND):
            if dt2 > 0:
                ctx.wait(nc.sync, "pb_om")
            for et in range(ND):
                ctx.dma_t(woT[:, et, :],
                          proj_bf[dt2 * 128:(dt2 + 1) * 128,
                                  et * 128:(et + 1) * 128], sem="pb_w")
            ctx.wait(nc.tensor, "pb_w")
            pso = ctx.ps[2 + dt2 % 2]
            if dt2 >= 2:
                ctx.wait(nc.tensor, "pb_oa", ctx.cnt.get("pb_oa", 0) - 1)
            for et in range(ND):
                i = nc.tensor.matmul(pso, woT[:, et, :], y_cat[:, et, :],
                                     start=(et == 0), stop=(et == ND - 1))
            ctx.inc(i, "pb_om", 1)
            ctx.wait(nc.vector, "pb_om")
            nc.vector.tensor_tensor(ctx.tmp_sb[:, 0, :], pso,
                                    ctx.x[:, dt2, qsl], OP.add)
            i = nc.vector.tensor_scalar(ctx.x[:, dt2, qsl], ctx.tmp_sb[:, 0, :],
                                        bo_col[:, dt2:dt2 + 1], None, OP.add)
            ctx.inc(i, "pb_oa", 1)
        ctx.barrier()
    ctx.borrow = False
    ctx.arena_release()


def emit_l0_attn(ctx):
    nc = ctx.nc
    T = ctx.T
    NQS = T // 512
    QL = KVL = 512
    W_dq = ctx.din("L0_attn_W_dq", [QL, D], F32)
    W_uq = ctx.din("L0_attn_W_uq", [D, QL], F32)
    W_dkv = ctx.din("L0_attn_W_dkv", [KVL, D], F32)
    W_uk = ctx.din("L0_attn_W_uk", [D, KVL], F32)
    W_uv = ctx.din("L0_attn_W_uv", [D, KVL], F32)
    W_o = ctx.din("L0_attn_W_o", [D, D], F32)
    bf = {}
    for nm, ap, shp in [("dq", W_dq, [QL, D]), ("uq", W_uq, [D, QL]),
                        ("dkv", W_dkv, [KVL, D]), ("uk", W_uk, [D, KVL]),
                        ("uv", W_uv, [D, KVL]), ("o", W_o, [D, D])]:
        bf[nm] = ctx.dtmp(f"l0a_{nm}_bf", shp, BF16)
        ctx.dma(bf[nm], {"dq": W_dq, "uq": W_uq, "dkv": W_dkv, "uk": W_uk,
                         "uv": W_uv, "o": W_o}[nm])
    a_dram = ctx.dtmp("l0a_A", [D, D], BF16)
    at_dram = ctx.dtmp("l0a_AT", [D, D], BF16)
    mk_dram = ctx.dtmp("l0a_MK", [D, KVL], BF16)
    mv_dram = ctx.dtmp("l0a_MV", [KVL, D], BF16)
    ctx.barrier()

    ctx.arena_mark()
    wa = ctx.arena([128, 4, D], BF16)
    wb = ctx.arena([128, ND, D], BF16)
    ev = ctx.arena([128, TC], BF16)

    def mm_big(out_dram, M, N, KT, lhsT_tiles, rhs_tiles, scale=1.0):
        it = 0
        for mt in range(M // 128):
            for nt0 in range(0, N, 512):
                nw = min(512, N - nt0)
                ps = ctx.ps[it % 2]
                it += 1
                if ctx.cnt.get("pw_e", 0) >= 2:
                    ctx.wait(nc.tensor, "pw_e", ctx.cnt["pw_e"] - 1)
                for kt in range(KT):
                    i = nc.tensor.matmul(ps[:, 0:nw], lhsT_tiles(kt, mt),
                                         rhs_tiles(kt, nt0, nw),
                                         start=(kt == 0), stop=(kt == KT - 1))
                ctx.inc(i, "pw_m", 1)
                ctx.wait(nc.vector, "pw_m")
                if ctx.cnt.get("pw_s", 0):
                    ctx.wait(nc.vector, "pw_s")
                i = nc.vector.tensor_scalar(ev[:, 0:nw], ps[:, 0:nw], scale,
                                            None, OP.mult)
                ctx.inc(i, "pw_e", 1)
                ctx.wait(nc.sync, "pw_e")
                ctx.store(out_dram[mt * 128:(mt + 1) * 128, nt0:nt0 + nw],
                          ev[:, 0:nw], sem="pw_s")

    for rt in range(4):
        ctx.dma_t(wa[:, rt, :], bf["uq"][:, rt * 128:(rt + 1) * 128])
    ctx.dma(wb[:, 0:4, :], bf["dq"].rearrange("(a p) c -> p a c", p=128))
    ctx.barrier()
    mm_big(a_dram, D, D, 4,
           lambda kt, mt: wb[:, kt, mt * 128:(mt + 1) * 128],
           lambda kt, n0, nw: wa[:, kt, n0:n0 + nw])
    mm_big(at_dram, D, D, 4,
           lambda kt, mt: wa[:, kt, mt * 128:(mt + 1) * 128],
           lambda kt, n0, nw: wb[:, kt, n0:n0 + nw])
    ctx.barrier()
    ctx.dma(wa, ctx.dram["l0a_AT"].rearrange("(a p) c -> p a c", p=128)[:, 0:4, :],
            sem="pw_l")
    ctx.dma(wa2 := ctx.arena([128, 4, D], BF16),
            ctx.dram["l0a_AT"].rearrange("(a p) c -> p a c", p=128)[:, 4:8, :],
            sem="pw_l")
    ctx.dma(wb[:, 0:ND, 0:KVL], bf["uk"].rearrange("(a p) l -> p a l", p=128),
            sem="pw_l")
    ctx.barrier()
    mm_big(mk_dram, D, KVL, 8,
           lambda kt, mt: (wa if kt < 4 else wa2)[:, kt % 4, mt * 128:(mt + 1) * 128],
           lambda kt, n0, nw: wb[:, kt, n0:n0 + nw], scale=0.125)
    ctx.barrier()
    ctx.dma(wa[:, 0:4, 0:KVL],
            bf["uv"].rearrange("(a p) l -> p a l", p=128)[:, 0:4, :], sem="pw_l")
    ctx.dma(wa2[:, 0:4, 0:KVL], bf["uv"].rearrange("(a p) l -> p a l", p=128)
            [:, 4:8, :], sem="pw_l")
    for ct in range(ND):
        ctx.dma_t(wb[:, ct, :], bf["o"][:, ct * 128:(ct + 1) * 128])
    ctx.barrier()
    mm_big(mv_dram, KVL, D, 8,
           lambda kt, mt: (wa if kt < 4 else wa2)[:, kt % 4, mt * 128:(mt + 1) * 128],
           lambda kt, n0, nw: wb[:, kt, n0:n0 + nw])
    ctx.barrier()
    ctx.arena_release()
    if L0_PHASE < 1:
        return

    emit_norm(ctx, rms=False, eps=1e-5)

    q_dram = ctx.dtmp("l0a_q", [D, T], BF16)
    ckv_fm_dram = ctx.dtmp("l0a_ckvfm", [KVL, T], BF16)

    prev_borrow = ctx.borrow
    ctx.borrow = True
    ctx.arena_mark()
    ckv_tm = ctx.arena([128, ctx.NT, KVL], BF16)
    kn2 = ctx.arena([128, 2], F32)
    ctx.arena_mark()
    wT = ctx.arena([128, ND, D], BF16)
    xnc = ctx.arena([128, ND, TC], BF16)
    ev1 = ctx.arena([128, TC], BF16)
    ev2 = ctx.arena([128, TC], BF16)
    kn = ctx.arena([128, 1], F32)
    nc.vector.memset(kn, 0.0)
    for dt in range(ND):
        ctx.dma_t(wT[:, dt, 0:KVL], bf["dkv"][:, dt * 128:(dt + 1) * 128])
    ctx.barrier()
    for c in range(ctx.NC):
        sl = slice(c * TC, (c + 1) * TC)
        if c > 0:
            ctx.wait(nc.gpsimd, "p1_m")
        load_xn_chunk(ctx, xnc, c, sem="p1_x")
        ctx.wait(nc.tensor, "p1_x")
        for tt in range(TC // 128):
            gt = c * (TC // 128) + tt
            ps = ctx.ps[tt % 2]
            if ctx.cnt.get("p1_ck", 0) >= 2:
                ctx.wait(nc.tensor, "p1_ck", ctx.cnt["p1_ck"] - 1)
                ctx.wait(nc.tensor, "p1_q", ctx.cnt["p1_q"] - 1)
            for dt in range(ND):
                i = nc.tensor.matmul(ps, xnc[:, dt, tt * 128:(tt + 1) * 128],
                                     wT[:, dt, 0:KVL],
                                     start=(dt == 0), stop=(dt == ND - 1))
            ctx.inc(i, "p1_m", 1)
            ctx.wait(nc.vector, "p1_m")
            i = nc.vector.tensor_copy(ckv_tm[:, gt, :], ps)
            ctx.inc(i, "p1_ck", 1)
            ctx.wait(nc.scalar, "p1_ck")
            i = nc.scalar.activation(ev2, ps, AF.Square, accum_out=kn2[:, 0:1])
            ctx.inc(i, "p1_q", 1)
            ctx.wait(nc.vector, "p1_q")
            i = nc.vector.tensor_tensor(kn, kn, kn2[:, 0:1], OP.max)
            ctx.inc(i, "p1_n", 1)
            ctx.wait(nc.scalar, "p1_n")
        for lt in range(4):
            ps = ctx.ps[4 + lt % 2]
            if ctx.cnt.get("p1_cf", 0) >= 2:
                ctx.wait(nc.tensor, "p1_cf", ctx.cnt["p1_cf"] - 1)
            for dt in range(ND):
                i = nc.tensor.matmul(ps, wT[:, dt, lt * 128:(lt + 1) * 128],
                                     xnc[:, dt, :],
                                     start=(dt == 0), stop=(dt == ND - 1))
            ctx.inc(i, "p1_fm", 1)
            ctx.wait(nc.vector, "p1_fm")
            if ctx.cnt.get("p1_fs", 0):
                ctx.wait(nc.vector, "p1_fs")
            i = nc.vector.tensor_copy(ev1, ps)
            ctx.inc(i, "p1_cf", 1)
            ctx.wait(nc.sync, "p1_cf")
            ctx.store(ckv_fm_dram[lt * 128:(lt + 1) * 128, sl], ev1, sem="p1_fs")
    ctx.barrier()
    i = nc.tensor.transpose(ctx.ps[6][0:1, 0:128], kn,
                            ctx.sbuf["ident_f32"])
    ctx.inc(i, "p1_t", 1)
    ctx.wait(nc.vector, "p1_t")
    i = nc.vector.reduce_max(kn2[0:1, 0:1], ctx.ps[6][0:1, 0:128], axis=AX.X)
    ctx.vsync(i)
    i = nc.vector.tensor_scalar(kn2[0:1, 0:1], kn2[0:1, 0:1], 1e-12, None, OP.add)
    ctx.inc(i, "p1_n2", 1)
    ctx.wait(nc.scalar, "p1_n2")
    i = nc.scalar.activation(kn2[0:1, 1:2], kn2[0:1, 0:1], AF.Sqrt)
    ctx.inc(i, "p1_s", 1)
    ctx.barrier()
    for c in range(ctx.NC):
        sl = slice(c * TC, (c + 1) * TC)
        if c > 0:
            ctx.wait(nc.gpsimd, "p2_m")
        load_xn_chunk(ctx, xnc, c, sem="p1_x")
        ctx.wait(nc.tensor, "p1_x")
        if c == 0:
            ctx.dma(wT, ctx.dram["l0a_A"].rearrange("(a p) j -> p a j", p=128),
                    sem="p2_w")
            ctx.wait(nc.tensor, "p2_w")
        for jt in range(ND):
            ps = ctx.ps[jt % 2]
            if ctx.cnt.get("p2_e", 0) >= 2:
                ctx.wait(nc.tensor, "p2_e", ctx.cnt["p2_e"] - 1)
            for dt in range(ND):
                i = nc.tensor.matmul(ps, wT[:, dt, jt * 128:(jt + 1) * 128],
                                     xnc[:, dt, :],
                                     start=(dt == 0), stop=(dt == ND - 1))
            ctx.inc(i, "p2_m", 1)
            ctx.wait(nc.vector, "p2_m")
            if ctx.cnt.get("p2_s", 0):
                ctx.wait(nc.vector, "p2_s")
            i = nc.vector.tensor_copy(ev1, ps)
            ctx.inc(i, "p2_e", 1)
            ctx.wait(nc.sync, "p2_e")
            ctx.store(q_dram[jt * 128:(jt + 1) * 128, sl], ev1, sem="p2_s")
    ctx.barrier()

    ctx.arena_release()
    if L0_PHASE < 3:
        ctx.borrow = prev_borrow
        ctx.arena_release()
        return

    ctx.arena_mark()
    qlT = ctx.arena([128, 4, 512], BF16)
    qh = ctx.arena([64, 512], BF16)
    mkh = ctx.arena([64, KVL], BF16)
    mvh = ctx.arena([128, 4, 64], BF16)
    fmt = [ctx.arena([128, 4, 128], BF16) for _ in range(2)]
    ptb = [ctx.arena([128, 512], BF16) for _ in range(2)]
    sqv = ctx.arena([128, 4, 512], BF16)
    ylt = ctx.arena([128, 4, 512], BF16)
    l3m = ctx.arena([128, 4, 512], BF16)
    negC = ctx.arena([1, 512], F32)
    zrec = ctx.arena([1, 512], F32)
    zb = ctx.arena([128, 512], F32)
    for r in range(4):
        ctx.dma(l3m[:, r, :], ctx.dram[f"c_l3m_{r}"])
    ctx.barrier()
    for qs in range(NQS):
        qsl = slice(qs * 512, (qs + 1) * 512)
        NK = 4 * qs + 4
        for h in range(16):
            ctx.wait(nc.gpsimd, "b_ql", ctx.cnt.get("b_ql", 0))
            if ctx.cnt.get("b_sm", 0):
                ctx.wait(nc.gpsimd, "b_sm")
            ctx.dma(qh, ctx.dram["l0a_q"][h * 64:(h + 1) * 64, qsl], sem="b_ql")
            ctx.dma(mkh, ctx.dram["l0a_MK"][h * 64:(h + 1) * 64, :], sem="b_ql")
            if ctx.cnt.get("b_om", 0):
                ctx.wait(nc.gpsimd, "b_om")
            ctx.dma(mvh, ctx.dram["l0a_MV"].rearrange("(a p) e -> p a e", p=128)
                    [:, :, h * 64:(h + 1) * 64], sem="b_ql")
            ctx.wait(nc.tensor, "b_ql")
            if ctx.cnt.get("b_ye", 0):
                ctx.wait(nc.tensor, "b_ye")
            if ctx.cnt.get("b_zc", 0):
                ctx.wait(nc.tensor, "b_zc")
            for lt in range(4):
                i = nc.tensor.matmul(ctx.ps[4 + lt], mkh[:, lt * 128:(lt + 1) * 128],
                                     qh, start=True, stop=True,
                                     skip_group_check=True)
            ctx.inc(i, "b_qm", 1)
            ctx.wait(nc.vector, "b_qm")
            if ctx.cnt.get("b_av", 0):
                ctx.wait(nc.vector, "b_av")
            for lt in range(4):
                i = nc.vector.tensor_copy(qlT[:, lt, :], ctx.ps[4 + lt])
            ctx.inc(i, "b_qe", 1)
            ctx.wait(nc.scalar, "b_qe")
            for lt in range(4):
                i = nc.scalar.activation(sqv[:, lt, :], qlT[:, lt, :], AF.Square)
            ctx.inc(i, "b_sq", 1)
            ctx.wait(nc.tensor, "b_sq")
            for lt in range(4):
                i = nc.tensor.matmul(ctx.ps[6][0:1, :], ctx.sbuf["ones_col_bf"],
                                     sqv[:, lt, :], start=(lt == 0),
                                     stop=(lt == 3), skip_group_check=True)
            ctx.inc(i, "b_nm", 1)
            ctx.wait(nc.scalar, "b_nm")
            i = nc.scalar.activation(negC, ctx.ps[6][0:1, :], AF.Sqrt)
            ctx.inc(i, "b_na", 1)
            ctx.wait(nc.vector, "b_na")
            i = nc.vector.tensor_scalar(negC, negC, kn2[0:1, 1:2], None, OP.mult)
            i = nc.vector.tensor_scalar(negC, negC, -1.0, None, OP.mult)
            nb = negC_bf(ctx, negC)
            ctx.inc(i, "b_nc", 1)
            ctx.wait(nc.tensor, "b_nc")
            bSm = ctx.cnt.get("b_sm", 0)
            bPe = ctx.cnt.get("b_pe", 0)
            bPt = ctx.cnt.get("b_pt", 0)
            bAv = ctx.cnt.get("b_av", 0)
            for ki in range(NK):
                kb = ki % 2
                if ki >= 2:
                    ctx.wait(nc.sync, "b_sm", bSm + ki - 1)
                elif ctx.cnt.get("b_sm", 0):
                    ctx.wait(nc.sync, "b_sm", bSm)
                for lt in range(4):
                    ctx.dma(fmt[kb][:, lt, :],
                            ctx.dram["l0a_ckvfm"][lt * 128:(lt + 1) * 128,
                                                  ki * 128:(ki + 1) * 128],
                            q=nc.sync, sem="b_fl")
                sb_ = ki % 2
                pss = ctx.ps[sb_]
                if ki >= 2:
                    ctx.wait(nc.tensor, "b_pe", bPe + ki - 1)
                ctx.wait(nc.tensor, "b_fl")
                for lt in range(4):
                    i = nc.tensor.matmul(pss, fmt[kb][:, lt, :], qlT[:, lt, :],
                                         start=(lt == 0), stop=False,
                                         skip_group_check=True)
                i = nc.tensor.matmul(pss, ctx.sbuf["ones_row_bf"], nb,
                                     start=False, stop=True,
                                     skip_group_check=True)
                ctx.inc(i, "b_sm", 1)
                ctx.wait(nc.scalar, "b_sm")
                if ki >= 2:
                    ctx.wait(nc.scalar, "b_av", bAv + ki - 1)
                i = nc.scalar.activation(ptb[sb_], pss, AF.Exp)
                ctx.inc(i, "b_pe", 1)
                r = ki - 4 * qs
                if 0 <= r <= 3:
                    ctx.wait(nc.vector, "b_pe")
                    i = nc.vector.tensor_tensor(ptb[sb_], ptb[sb_], l3m[:, r, :],
                                                OP.mult)
                    ctx.inc(i, "b_pt", 1)
                    ctx.wait(nc.tensor, "b_pt")
                else:
                    ctx.wait(nc.tensor, "b_pe")
                for lt in range(4):
                    i = nc.tensor.matmul(ctx.ps[2 + lt],
                                         ckv_tm[:, ki, lt * 128:(lt + 1) * 128],
                                         ptb[sb_], start=(ki == 0),
                                         stop=(ki == NK - 1),
                                         skip_group_check=True)
                i = nc.tensor.matmul(ctx.ps[7][0:1, :], ctx.sbuf["ones_col_bf"],
                                     ptb[sb_], start=(ki == 0),
                                     stop=(ki == NK - 1), skip_group_check=True)
                ctx.inc(i, "b_av", 1)
            ctx.wait(nc.vector, "b_av")
            i = nc.vector.reciprocal(zrec, ctx.ps[7][0:1, :])
            ctx.inc(i, "b_zr", 1)
            ctx.wait(nc.tensor, "b_zr")
            i = nc.tensor.matmul(ctx.ps[6], ctx.sbuf["ones_row_f32"], zrec,
                                 start=True, stop=True, skip_group_check=True)
            ctx.inc(i, "b_zb", 1)
            ctx.wait(nc.vector, "b_zb")
            i = nc.vector.tensor_copy(zb, ctx.ps[6])
            ctx.inc(i, "b_zc", 1)
            for lt in range(4):
                i = nc.vector.tensor_tensor(ylt[:, lt, :], ctx.ps[2 + lt], zb,
                                            OP.mult)
            ctx.inc(i, "b_ye", 1)
            ctx.wait(nc.tensor, "b_ye")
            jt, po = h // 2, (h % 2) * 64
            pso = ctx.ps[sb_]
            for lt in range(4):
                i = nc.tensor.matmul(pso[0:64, :], mvh[:, lt, :], ylt[:, lt, :],
                                     start=(lt == 0), stop=(lt == 3),
                                     skip_group_check=True)
            ctx.inc(i, "b_om", 1)
            ctx.wait(nc.vector, "b_om")
            i = nc.vector.tensor_tensor(ctx.x[po:po + 64, jt, qsl],
                                        ctx.x[po:po + 64, jt, qsl],
                                        pso[0:64, :], OP.add)
            ctx.inc(i, "b_xa", 1)
            ctx.wait(nc.tensor, "b_xa")
        ctx.barrier()
    ctx.borrow = prev_borrow
    ctx.arena_release()
    ctx.arena_release()


def negC_bf(ctx, negC_f32):
    nc = ctx.nc
    if not hasattr(ctx, "_negc_bf"):
        ctx._negc_bf = ctx.sb("ld4k", [8, D], F32).bitcast(BF16)[0:1, 1536:2048]
    i = nc.vector.tensor_copy(ctx._negc_bf, negC_f32)
    ctx.inc(i, "b_ncb", 1)
    ctx.wait(ctx.nc.tensor, "b_ncb")
    return ctx._negc_bf


def emit_l2_attn(ctx):
    nc = ctx.nc
    T = ctx.T
    NH, DK, DV, CH = 4, 128, 256, 64
    NG = T // CH
    scale = DK ** -0.5
    ins = {}
    for nm, shp in [("attn_norm_w", [D]), ("Wq", [D, 512]), ("Wk", [D, 512]),
                    ("Wv", [D, D]), ("Wg", [D, D]), ("Wgk1", [D, 16]),
                    ("Wgk2", [16, 512]), ("bgk2", [512]), ("gnorm_w", [256]),
                    ("Wo", [D, D]), ("mlp_norm_w", [D]),
                    ("W_gate", [D, 2816]), ("W_up", [D, 2816]),
                    ("W_down", [2816, D])]:
        ins[nm] = ctx.din(f"L2_attn_{nm}", shp, F32)
    bfw = {}
    for nm in ["Wq", "Wk", "Wv", "Wg", "Wgk1", "Wgk2", "Wo", "W_gate",
               "W_up", "W_down"]:
        bfw[nm] = ctx.dtmp(f"l2a_{nm}_bf", list(ins[nm].shape), BF16)
        ctx.dma(bfw[nm], ins[nm])
    anw_col = emit_feature_cols(ctx, "l2a_anw", ins["attn_norm_w"], D)
    mnw_col = emit_feature_cols(ctx, "l2a_mnw", ins["mlp_norm_w"], D)
    ctx.barrier()

    h1_dram = ctx.dtmp("l2a_h1", [ctx.NC, 128, ND * TC], BF16)
    u2_dram = ctx.dtmp("l2a_u2", [ctx.NC, 128, ND * TC], BF16)
    n2_dram = ctx.dtmp("l2a_n2", [ctx.NC, 128, ND * TC], BF16)
    o2_dram = ctx.dtmp("l2a_o2", [T, D], BF16)

    emit_norm(ctx, rms=False, eps=1e-5, residual_add_xn=True)
    emit_norm(ctx, rms=True, eps=1e-6, gain_col=anw_col,
              src_dram=ctx.xn_dram, out_dram=h1_dram)

    ctx.arena_mark()
    prev_borrow = ctx.borrow
    ctx.borrow = True
    xnc = ctx.arena([128, ND, TC], BF16)
    wq = ctx.arena([128, ND, 128], BF16)
    wk = ctx.arena([128, ND, 128], BF16)
    wv = ctx.arena([128, ND, 256], BF16)
    wg = ctx.arena([128, ND, 256], BF16)
    wgk1 = ctx.arena([128, ND, 16], BF16)
    wgk2h = ctx.arena([16, 128], BF16)
    z16 = ctx.arena([16, T], BF16)
    glog = ctx.arena([128, T // 128, 128], BF16)
    u64 = ctx.arena([64, 64], BF16)
    u64x = ctx.arena([128, 64], BF16)
    expb = ctx.arena([128, CH], BF16)
    expnb = ctx.arena([128, CH], BF16)
    qe = ctx.arena([128, CH], BF16)
    kd1 = ctx.arena([128, CH], BF16)
    kd2 = ctx.arena([128, CH], BF16)
    kd2t = ctx.arena([64, 128], BF16)
    atm = ctx.arena([64, 64], BF16)
    vsb = ctx.arena([64, 256], BF16)
    ggs = ctx.arena([64, 256], F32)
    gg = ctx.arena([64, 256], BF16)
    o2sb = ctx.arena([64, 256], BF16)
    S32 = ctx.arena([128, 256], F32)
    Sbf = ctx.arena([128, 256], BF16)
    gnwb = ctx.arena([64, 256], F32)
    bgk2b = ctx.arena([128, 128], F32)
    gdum = ctx.arena([64, 256], BF16)
    ctx.dma(u64, ctx.dram["c_u64_bf"])
    ctx.dma(u64x, ctx.dram["c_u64x_bf"])
    ld4k = ctx.sb("ld4k", [8, D], F32)
    gn_ld = ld4k[0:1, 0:256]
    ctx.dma(gn_ld, ins["gnorm_w"].unsqueeze(0))
    ctx.dma(ld4k[0:1, 256:768], ins["bgk2"].unsqueeze(0))
    ctx.barrier()
    i = nc.tensor.matmul(ctx.ps[0][0:64, 0:256],
                         ctx.sbuf["ones_row_f32"][0:1, 0:64],
                         gn_ld, start=True, stop=True)
    ctx.inc(i, "g_bc", 1)
    ctx.wait(nc.vector, "g_bc")
    i = nc.vector.tensor_copy(gnwb, ctx.ps[0][0:64, 0:256])
    ctx.inc(i, "g_bv", 1)
    ctx.wait(nc.tensor, "g_bv")

    for h in range(NH):
        ctx.barrier()
        for dt in range(ND):
            ctx.dma(wq[:, dt, :], bfw["Wq"][dt * 128:(dt + 1) * 128,
                                            h * 128:(h + 1) * 128])
            ctx.dma(wk[:, dt, :], bfw["Wk"][dt * 128:(dt + 1) * 128,
                                            h * 128:(h + 1) * 128])
            ctx.dma(wv[:, dt, :], bfw["Wv"][dt * 128:(dt + 1) * 128,
                                            h * 256:(h + 1) * 256])
            ctx.dma(wg[:, dt, :], bfw["Wg"][dt * 128:(dt + 1) * 128,
                                            h * 256:(h + 1) * 256])
            ctx.dma(wgk1[:, dt, :], bfw["Wgk1"][dt * 128:(dt + 1) * 128, :])
        ctx.dma(wgk2h, bfw["Wgk2"][:, h * 128:(h + 1) * 128])
        ctx.barrier()
        i = nc.tensor.matmul(ctx.ps[0][:, 0:128], ctx.sbuf["ones_row_f32"],
                             ld4k[0:1, 256 + h * 128:256 + (h + 1) * 128],
                             start=True, stop=True)
        ctx.inc(i, "g_bc", 1)
        ctx.wait(nc.vector, "g_bc")
        i = nc.vector.tensor_copy(bgk2b, ctx.ps[0][:, 0:128])
        ctx.inc(i, "g_bv", 1)
        ctx.wait(nc.tensor, "g_bv")
        for c in range(ctx.NC):
            sl = slice(c * TC, (c + 1) * TC)
            if c > 0:
                ctx.wait(nc.gpsimd, "g_zm")
            load_xn_chunk(ctx, xnc, c, sem="g_xl", src_dram=h1_dram)
            ctx.wait(nc.tensor, "g_xl")
            ps = ctx.ps[c % 2]
            if c >= 2:
                ctx.wait(nc.tensor, "g_ze", ctx.cnt["g_ze"] - 1)
            for dt in range(ND):
                i = nc.tensor.matmul(ps[0:16, :], wgk1[:, dt, :], xnc[:, dt, :],
                                     start=(dt == 0), stop=(dt == ND - 1))
            ctx.inc(i, "g_zm", 1)
            ctx.wait(nc.vector, "g_zm")
            i = nc.vector.tensor_copy(z16[:, sl], ps[0:16, :])
            ctx.inc(i, "g_ze", 1)
        ctx.barrier()
        for tt in range(T // 128):
            tsl = slice(tt * 128, (tt + 1) * 128)
            ps = ctx.ps[tt % 2]
            if tt >= 2:
                ctx.wait(nc.tensor, "g_gv", ctx.cnt["g_gv"] - 1)
            i = nc.tensor.matmul(ps[:, 0:128], z16[:, tsl], wgk2h,
                                 start=True, stop=True)
            ctx.inc(i, "g_gm", 1)
            ctx.wait(nc.vector, "g_gm")
            t1 = ctx.tmp_sb[:, 0, 0:128]
            t2 = ctx.tmp_sb[:, 1, 0:128]
            nc.vector.tensor_tensor(t1, ps[:, 0:128], bgk2b, OP.add)
            i = nc.vector.tensor_scalar(t2, t1, 0.0, None, OP.min)
            ctx.inc(i, "g_gv1", 1)
            ctx.wait(nc.scalar, "g_gv1")
            a1 = ctx.tmp_sb.bitcast(BF16)[:, 0, 512:640]
            a2 = ctx.tmp_sb.bitcast(BF16)[:, 0, 640:768]
            i = nc.scalar.activation(a1, t1, AF.Abs)
            ctx.vsync(i, eng=nc.scalar)
            i = nc.scalar.activation(a2, a1, AF.Exp, scale=-1.0)
            ctx.vsync(i, eng=nc.scalar)
            i = nc.scalar.activation(a1, a2, AF.Ln, bias=1.0)
            ctx.inc(i, "g_ga", 1)
            ctx.wait(nc.vector, "g_ga")
            nc.vector.tensor_tensor(t1, t2, a1, OP.subtract)
            i = nc.vector.tensor_scalar(glog[:, tt, :], t1, 1.0 / 16.0,
                                        None, OP.mult)
            ctx.inc(i, "g_gv", 1)
        ctx.barrier()
        nc.vector.memset(S32, 0.0)
        i = nc.vector.memset(Sbf, 0.0)
        ctx.inc(i, "g_sv", 1)
        ctx.wait(nc.tensor, "g_sv")
        for g in range(NG):
            c = (g * CH) // TC
            if g * CH % TC == 0:
                load_xn_chunk(ctx, xnc, c, sem="g_xl", src_dram=h1_dram)
                ctx.wait(nc.tensor, "g_xl")
            toff = (g * CH) % TC
            gsl = slice(toff, toff + CH)
            grow = slice((g % 2) * 64, (g % 2) * 64 + 64)
            i = nc.tensor.matmul(ctx.ps[0][:, 0:CH], glog[grow, g // 2, :],
                                 u64x[grow, :],
                                 start=True, stop=True, skip_group_check=True)
            ctx.inc(i, "s_bm", 1)
            ctx.wait(nc.scalar, "s_bm")
            i = nc.scalar.activation(expb, ctx.ps[0][:, 0:CH], AF.Exp)
            i = nc.scalar.activation(expnb, ctx.ps[0][:, 0:CH], AF.Exp, scale=-1.0)
            eblf = ctx.tmp_sb[:, 0, 300:301]
            i = nc.scalar.activation(eblf, ctx.ps[0][:, CH - 1:CH], AF.Exp)
            ctx.inc(i, "s_ea", 1)
            for dt in range(ND):
                i = nc.tensor.matmul(ctx.ps[1][:, 0:CH], wq[:, dt, :],
                                     xnc[:, dt, gsl], start=(dt == 0),
                                     stop=(dt == ND - 1), skip_group_check=True)
            ctx.inc(i, "s_qm", 1)
            for dt in range(ND):
                i = nc.tensor.matmul(ctx.ps[2][:, 0:CH], wk[:, dt, :],
                                     xnc[:, dt, gsl], start=(dt == 0),
                                     stop=(dt == ND - 1), skip_group_check=True)
            ctx.inc(i, "s_km", 1)
            for dt in range(ND):
                i = nc.tensor.matmul(ctx.ps[3][0:CH, 0:DV], xnc[:, dt, gsl],
                                     wv[:, dt, :], start=(dt == 0),
                                     stop=(dt == ND - 1), skip_group_check=True)
            ctx.inc(i, "s_vm", 1)
            for dt in range(ND):
                i = nc.tensor.matmul(ctx.ps[5][0:CH, 0:DV], xnc[:, dt, gsl],
                                     wg[:, dt, :], start=(dt == 0),
                                     stop=(dt == ND - 1), skip_group_check=True)
            ctx.inc(i, "s_gm", 1)
            ctx.wait(nc.vector, "s_ea")
            ctx.wait(nc.vector, "s_qm")
            i = nc.vector.scalar_tensor_tensor(qe, ctx.ps[1][:, 0:CH], scale,
                                               expb, OP.mult, OP.mult)
            ctx.inc(i, "s_qe", 1)
            ctx.wait(nc.vector, "s_km")
            i = nc.vector.tensor_tensor(kd1, ctx.ps[2][:, 0:CH], expnb, OP.mult)
            ctx.inc(i, "s_kd", 1)
            ctx.wait(nc.vector, "s_vm")
            i = nc.vector.tensor_copy(vsb, ctx.ps[3][0:CH, 0:DV])
            ctx.inc(i, "s_ve", 1)
            ctx.wait(nc.tensor, "s_qe")
            ctx.wait(nc.tensor, "s_kd")
            ctx.wait(nc.tensor, "s_ea")
            i = nc.tensor.matmul(ctx.ps[0][0:CH, 0:CH], kd1, qe,
                                 start=True, stop=True, skip_group_check=True)
            ctx.inc(i, "s_am", 1)
            ctx.wait(nc.vector, "s_am")
            i = nc.vector.tensor_tensor(atm, ctx.ps[0][0:CH, 0:CH], u64, OP.mult)
            ctx.inc(i, "s_ae", 1)
            ctx.wait(nc.tensor, "s_ae")
            ctx.wait(nc.tensor, "s_ve")
            i = nc.tensor.matmul(ctx.ps[4][0:CH, 0:DV], atm, vsb,
                                 start=True, stop=False, skip_group_check=True)
            i = nc.tensor.matmul(ctx.ps[4][0:CH, 0:DV], qe, Sbf,
                                 start=False, stop=True, skip_group_check=True)
            ctx.inc(i, "s_om", 1)
            ctx.wait(nc.vector, "s_om") if False else None
            i = nc.vector.tensor_scalar(kd2, kd1, eblf, None, OP.mult)
            ctx.inc(i, "s_k2", 1)
            ctx.wait(nc.tensor, "s_k2")
            i = nc.tensor.transpose(ctx.ps[6].bitcast(BF16)[0:CH, 0:128], kd2,
                                    ctx.sbuf["ident_bf"])
            ctx.inc(i, "s_kt", 1)
            ctx.wait(nc.vector, "s_kt")
            i = nc.vector.tensor_copy(kd2t, ctx.ps[6].bitcast(BF16)[0:CH, 0:128])
            ctx.inc(i, "s_k2e", 1)
            ctx.wait(nc.scalar, "s_om")
            osum = ctx.tmp_sb[0:CH, 0, 200:201]
            i = nc.scalar.activation(gdum, ctx.ps[4][0:CH, 0:DV], AF.Square,
                                     accum_out=osum)
            ctx.inc(i, "s_osq", 1)
            ctx.wait(nc.vector, "s_osq")
            i = nc.vector.tensor_scalar(osum, osum, 1.0 / DV, None, OP.mult)
            ctx.vsync(i)
            i = nc.vector.tensor_scalar(osum, osum, 1e-6, None, OP.add)
            ctx.vsync(i)
            orr = ctx.tmp_sb[0:CH, 0, 202:203]
            i = nc.vector.reciprocal(orr, osum)
            ctx.inc(i, "s_or", 1)
            ctx.wait(nc.scalar, "s_or")
            i = nc.scalar.activation(orr, orr, AF.Sqrt)
            ctx.inc(i, "s_os", 1)
            ctx.wait(nc.scalar, "s_gm")
            sigs = ctx.tmp_sb[0:CH, 1, 0:256]
            i = nc.scalar.activation(sigs, ctx.ps[5][0:CH, 0:DV], AF.Sigmoid)
            ctx.inc(i, "s_sg", 1)
            ctx.wait(nc.vector, "s_sg")
            i = nc.vector.tensor_tensor(gg, ctx.ps[5][0:CH, 0:DV], sigs, OP.mult)
            ctx.inc(i, "s_gg", 1)
            ctx.wait(nc.vector, "s_os")
            i = nc.vector.scalar_tensor_tensor(ggs, ctx.ps[4][0:CH, 0:DV], orr,
                                               gnwb, OP.mult, OP.mult)
            i = nc.vector.tensor_tensor(o2sb, ggs, gg, OP.mult)
            ctx.inc(i, "s_o2", 1)
            ctx.wait(nc.sync, "s_o2")
            ctx.store(o2_dram[g * CH:(g + 1) * CH, h * 256:(h + 1) * 256],
                      o2sb, sem="s_ost")
            ctx.wait(nc.tensor, "s_k2e")
            i = nc.tensor.matmul(ctx.ps[6][:, 0:256], kd2t, vsb,
                                 start=True, stop=True, skip_group_check=True)
            ctx.inc(i, "s_sm", 1)
            ctx.wait(nc.vector, "s_sm")
            i = nc.vector.scalar_tensor_tensor(S32, S32, eblf,
                                               ctx.ps[6][:, 0:256],
                                               OP.mult, OP.add)
            ctx.vsync(i)
            i = nc.vector.tensor_copy(Sbf, S32)
            ctx.inc(i, "s_sb", 1)
            ctx.wait(nc.tensor, "s_sb")
            ctx.barrier()
    ctx.barrier()
    ctx.arena_release()

    ctx.arena_mark()
    o2t = ctx.arena([128, ND, TC], BF16)
    woN = ctx.arena([128, ND, 128], BF16)
    ucb = ctx.arena([128, ND, TC], BF16)
    u2b = ctx.arena([128, TC], BF16)
    for c in range(ctx.NC):
        sl = slice(c * TC, (c + 1) * TC)
        for et in range(ND):
            ctx.dma_t(o2t[:, et, :], o2_dram[c * TC:(c + 1) * TC,
                                             et * 128:(et + 1) * 128],
                      sem="ao_tl")
        load_xn_chunk(ctx, ucb, c, sem="ao_ul")
        ctx.barrier()
        for dt2 in range(ND):
            for et in range(ND):
                ctx.dma(woN[:, et, :], bfw["Wo"][et * 128:(et + 1) * 128,
                                                 dt2 * 128:(dt2 + 1) * 128],
                        sem="ao_wl")
            ctx.wait(nc.tensor, "ao_wl")
            pso = ctx.ps[dt2 % 2]
            if dt2 >= 2:
                ctx.wait(nc.tensor, "ao_e", ctx.cnt.get("ao_e", 0) - 1)
            for et in range(ND):
                i = nc.tensor.matmul(pso, woN[:, et, :], o2t[:, et, :],
                                     start=(et == 0), stop=(et == ND - 1))
            ctx.inc(i, "ao_m", 1)
            ctx.wait(nc.vector, "ao_m")
            if ctx.cnt.get("ao_s", 0):
                ctx.wait(nc.vector, "ao_s")
            i = nc.vector.tensor_tensor(u2b, pso, ucb[:, dt2, :], OP.add)
            nc.vector.tensor_tensor(ctx.x[:, dt2, sl], ctx.x[:, dt2, sl],
                                    pso, OP.add)
            ctx.inc(i, "ao_e", 1)
            ctx.wait(nc.sync, "ao_e")
            dst = u2_dram[c, :, :].rearrange("p (n t) -> p n t", n=ND)
            ctx.store(dst[:, dt2, :], u2b, sem="ao_s")
            ctx.wait(nc.gpsimd, "ao_m")
        ctx.barrier()
    ctx.arena_release()
    ctx.borrow = prev_borrow

    emit_norm(ctx, rms=True, eps=1e-6, gain_col=mnw_col,
              src_dram=u2_dram, out_dram=n2_dram)
    ctx.xn_src = n2_dram

    def load_g(dt, f0, fw, dest):
        ctx.dma(dest, bfw["W_gate"][dt * 128:(dt + 1) * 128, f0:f0 + fw])

    def load_u(dt, f0, fw, dest):
        ctx.dma(dest, bfw["W_up"][dt * 128:(dt + 1) * 128, f0:f0 + fw])

    def load_d(h0, dest):
        ctx.dma(dest, bfw["W_down"][h0:h0 + 128, :])

    emit_swiglu(ctx, 2816, load_g, load_u, load_d, act=AF.Silu, FS=512)
    ctx.xn_src = None


BLOCK_EMITS = {
    "l2_attn": emit_l2_attn,
    "l0_attn": emit_l0_attn,
    "l3_attn": emit_l3_attn,
    "l1_attn": emit_l1_attn,
    "l0_ffn": emit_l0_ffn,
    "l3_ffn": emit_l3_ffn,
    "l2_ffn": emit_l2_ffn,
    "l1_ffn": emit_l1_ffn,
}


SEM_ALIAS = {
"l0_attn": {
"b_av": "ov0",
"b_fl": "ov1",
"b_na": "ov2",
"b_nc": "ov3",
"b_nm": "ov4",
"b_om": "ov5",
"b_pe": "ov6",
"b_pt": "ov7",
"b_qe": "ov8",
"b_ql": "ov9",
"b_qm": "ov10",
"b_sm": "ov11",
"b_sq": "ov12",
"b_xa": "ov13",
"b_ye": "ov14",
"b_zb": "ov15",
"b_zc": "ov16",
"b_zr": "ov17",
"p1_cf": "ov18",
"p1_ck": "ov19",
"p1_fm": "ov20",
"p1_fs": "ov21",
"p1_m": "ov22",
"p1_n": "ov23",
"p1_n2": "ov24",
"p1_q": "ov25",
"p1_s": "ov26",
"p1_t": "ov27",
"p1_x": "ov28",
"p2_e": "ov29",
"p2_m": "ov30",
"p2_s": "ov31",
"p2_w": "ov32",
"pw_e": "ov33",
"pw_m": "ov34",
"pw_s": "ov35"
},
"l0_ffn": {},
"l1_attn": {
"l1b": "ov0",
"l1c": "ov1",
"pb_mk": "ov2",
"pb_v": "ov3",
"q0a": "ov4",
"q0av": "ov5",
"q0b": "ov6",
"q0e": "ov7",
"q0f": "ov8",
"q0g": "ov9",
"q0h": "ov10",
"q0k": "ov11",
"q0l": "ov12",
"q0m": "ov13",
"q0n": "ov14",
"q0p": "ov15",
"q0sm": "ov16",
"q0t": "ov17",
"q0v": "ov18",
"q0z": "ov19"
},
"l1_ffn": {
"l1m": "ov0",
"l1t": "ov1",
"l1v": "ov2",
"rwT": "ov3",
"rwV": "ov4",
"ska": "ov5",
"ska2": "ov6",
"ska3": "ov7",
"skb": "ov8",
"skl": "ov9",
"skm": "ov10",
"skn": "ov11",
"skv": "ov12",
"skv2": "ov13",
"skv3": "ov14",
"tpa": "ov15",
"tpe": "ov16",
"tpt": "ov17",
"tpw": "ov18"
},
"l2_attn": {
"ao_e": "ov0",
"ao_m": "ov1",
"ao_s": "ov2",
"ao_wl": "ov3",
"g_bc": "ov4",
"g_bv": "ov5",
"g_ga": "ov6",
"g_gm": "ov7",
"g_gv": "ov8",
"g_gv1": "ov9",
"g_sv": "ov10",
"g_xl": "ov11",
"g_ze": "ov12",
"g_zm": "ov13",
"s_ae": "ov14",
"s_am": "ov15",
"s_bm": "ov16",
"s_ea": "ov17",
"s_gg": "ov18",
"s_gm": "ov19",
"s_k2": "ov20",
"s_k2e": "ov21",
"s_kd": "ov22",
"s_km": "ov23",
"s_kt": "ov24",
"s_o2": "ov25",
"s_om": "ov26",
"s_or": "ov27",
"s_os": "ov28",
"s_osq": "ov29",
"s_qe": "ov30",
"s_qm": "ov31",
"s_sb": "ov32",
"s_sg": "ov33",
"s_sm": "ov34",
"s_ve": "ov35",
"s_vm": "ov36"
},
"l2_ffn": {
"rtT": "ov0",
"rtV": "ov1"
},
"l3_attn": {
"e3c": "ov0",
"e3l": "ov1",
"e3t": "ov2",
"e3v": "ov3",
"l3b": "ov4",
"l3c": "ov5",
"pb_bl": "ov6",
"pb_kl": "ov7",
"pb_vl": "ov8"
},
"l3_ffn": {}
}


def build_block_prog(names, T):
    nc = bass.Bass("TRN2", target_bir_lowering=False)
    ctx = Ctx(nc, T)
    declare_consts(ctx)
    setup_resident(ctx)
    load_x(ctx)
    for n in names:
        ctx.set_sem_scope(SEM_ALIAS.get(n, {}))
        BLOCK_EMITS[n](ctx)
        ctx.barrier()
        ctx.set_sem_scope({})
    store_x(ctx)
    return ctx


LAUNCH_PLAN = [
    ["l0_attn"], ["l0_ffn"], ["l1_attn"], ["l1_ffn"],
    ["l2_attn"], ["l2_ffn"], ["l3_attn"], ["l3_ffn"],
]


def run_chain(inputs, plan, T, xT_list):
    consts = make_consts(T)
    cur = xT_list
    for names in plan:
        ctx = build_block_prog(names, T)
        in_maps = []
        for b in range(len(cur)):
            m = {}
            for nm in ctx.input_names:
                if nm == "xin":
                    m[nm] = cur[b]
                elif nm.startswith("c_"):
                    m[nm] = consts[nm]
                else:
                    m[nm] = inputs[nm]
            in_maps.append(m)
        res = run_bass_kernel_spmd(ctx.nc, in_maps, core_ids=list(range(len(cur))))
        cur = [res.results[b]["xout"] for b in range(len(cur))]
    return cur


def kernel(**inputs):
    x = inputs["L0_attn_x"]
    T = x.shape[1]
    xT = [np.ascontiguousarray(x[b].T) for b in range(x.shape[0])]
    out = run_chain(inputs, LAUNCH_PLAN, T, xT)
    return np.stack([o.T for o in out], axis=0)


# version 09 (final)
# speedup vs baseline: 1.0141x; 1.0141x over previous
import numpy as np
import ml_dtypes
import concourse.bass as bass
import concourse.mybir as mybir
from concourse.bass_utils import run_bass_kernel_spmd

F32 = mybir.dt.float32
BF16 = mybir.dt.bfloat16
I32 = mybir.dt.int32
AF = mybir.ActivationFunctionType
OP = mybir.AluOpType
AX = mybir.AxisListType

D = 1024
ND = D // 128
TC = 512
N_CORES = 8
BF = ml_dtypes.bfloat16

L0_PHASE = 3
L1_SER_INNER = False
L1_SER_EVAC = False
ARENA_KB = 66
ARENA_BLOCK_KB = 34


class _CntMap(dict):
    def __init__(self):
        super().__init__()
        self.alias = {}

    def _r(self, k):
        return self.alias.get(k, k)

    def __getitem__(self, k):
        return dict.__getitem__(self, self._r(k))

    def __setitem__(self, k, v):
        dict.__setitem__(self, self._r(k), v)

    def __contains__(self, k):
        return dict.__contains__(self, self._r(k))

    def get(self, k, default=None):
        return dict.get(self, self._r(k), default)


class Ctx:
    def __init__(self, nc: bass.Bass, T: int):
        self.nc = nc
        self.T = T
        self.NT = T // 128
        self.NC = T // TC
        self._sems = {}
        self.cnt = _CntMap()
        self._dma_sems = set()
        self.dram = {}
        self.sbuf = {}
        self.input_names = []
        self._uid = 0
        self._arena_off = 0
        self._arena_marks = []

    def sem(self, name):
        name = self.cnt._r(name)
        if name not in self._sems:
            self._sems[name] = self.nc.alloc_semaphore(name)
            self.cnt[name] = 0
        return self._sems[name]

    def set_sem_scope(self, alias_map):
        self.cnt.alias = alias_map

    def inc(self, ins, name, n=1):
        ins.then_inc(self.sem(name), n)
        self.cnt[name] += n
        return self.cnt[name]

    def wait(self, eng, name, val=None):
        eng.wait_ge(self.sem(name), self.cnt[name] if val is None else val)

    def dma(self, dst, src, q=None, sem="gld", **kw):
        q = q or self.nc.gpsimd
        sem = self.cnt._r(sem)
        self._dma_sems.add(sem)
        ins = q.dma_start(dst, src, **kw)
        self.inc(ins, sem, 16)
        return self.cnt[sem]

    def dma_t(self, dst, src, sem="tld"):
        sem = self.cnt._r(sem)
        self._dma_sems.add(sem)
        ins = self.nc.sync.dma_start(dst, src, transpose=True)
        self.inc(ins, sem, 16)
        return self.cnt[sem]

    def store(self, dst, src, sem="sst"):
        sem = self.cnt._r(sem)
        self._dma_sems.add(sem)
        ins = self.nc.sync.dma_start(dst, src)
        self.inc(ins, sem, 16)
        return self.cnt[sem]

    def barrier(self):
        for s in sorted(self._dma_sems):
            if self.cnt[s]:
                self.nc.sync.wait_ge(self.sem(s), self.cnt[s])
        self.nc.all_engine_barrier()

    def din(self, name, shape, dtype):
        ap = self.nc.dram_tensor(name, list(shape), dtype, kind="ExternalInput").ap()
        self.dram[name] = ap
        self.input_names.append(name)
        return ap

    def dout(self, name, shape, dtype):
        ap = self.nc.dram_tensor(name, list(shape), dtype, kind="ExternalOutput").ap()
        self.dram[name] = ap
        return ap

    def dtmp(self, name, shape, dtype):
        ap = self.nc.dram_tensor(name, list(shape), dtype, kind="Internal").ap()
        self.dram[name] = ap
        return ap

    def sb(self, name, shape, dtype):
        if name in self.sbuf:
            ap = self.sbuf[name]
            assert list(ap.tensor.shape) == list(shape) and ap.tensor.dtype == dtype, name
            return ap
        ap = self.nc.alloc_sbuf_tensor(name, list(shape), dtype).ap()
        self.sbuf[name] = ap
        return ap

    def arena_init(self):
        self._arena = self.sb("arena", [128, ARENA_KB * 1024], mybir.dt.uint8)
        self.borrow = False

    def arena_mark(self):
        self._arena_marks.append(self._arena_off)

    def arena_release(self):
        self._arena_off = self._arena_marks.pop()

    def arena_at(self, off_bytes, shape, dtype):
        esz = mybir.dt.size(dtype)
        n = 1
        for s in shape[1:]:
            n *= s
        v = self._arena[0:shape[0], off_bytes:off_bytes + n * esz].bitcast(dtype)
        if len(shape) == 3:
            v = v.rearrange("p (a b) -> p a b", a=shape[1], b=shape[2])
        return v

    def arena(self, shape, dtype):
        esz = mybir.dt.size(dtype)
        n = 1
        for s in shape[1:]:
            n *= s
        nbytes = n * esz
        off = (self._arena_off + 31) & ~31
        cap = (ARENA_KB if self.borrow else ARENA_BLOCK_KB) * 1024
        assert off + nbytes <= cap, \
            f"arena overflow: {off}+{nbytes} > {cap} (borrow={self.borrow})"
        self._arena_off = off + nbytes
        v = self._arena[0:shape[0], off:off + nbytes].bitcast(dtype)
        if len(shape) == 2:
            return v
        if len(shape) == 3:
            return v.rearrange("p (a b) -> p a b", a=shape[1], b=shape[2])
        raise ValueError(shape)

    def uid(self):
        self._uid += 1
        return self._uid

    def vsync(self, ins, eng=None):
        eng = eng or self.nc.vector
        self.inc(ins, "vfence", 1)
        self.wait(eng, "vfence")


def l1_klist(qs, T):
    NTT = T // 128
    lo = max(0, qs * 4 - 1)
    hi = min(NTT - 1, qs * 4 + 3)
    ks = set(range(lo, hi + 1))
    ks.add(0)
    return sorted(ks)


def l1_mask(qs, kt, T):
    q = qs * 512 + np.arange(512)[None, :]
    k = kt * 128 + np.arange(128)[:, None]
    local = (k <= q) & (q - k <= 128)
    glob = (q < 1) | (k < 1)
    return (local | glob)


def make_consts(T=None):
    c = {}
    c["c_ident_bf"] = np.eye(128).astype(BF)
    c["c_ident_f32"] = np.eye(128, dtype=np.float32)
    c["c_ones_col_bf"] = np.ones((128, 1), dtype=BF)
    c["c_ones_col_f32"] = np.ones((128, 1), dtype=np.float32)
    c["c_ones_row_f32"] = np.ones((1, 128), dtype=np.float32)
    c["c_ones_row_bf"] = np.ones((1, 128), dtype=BF)
    kk = np.arange(128)[:, None]
    qq = np.arange(128)[None, :]
    c["c_mask_diag_bf"] = (kk <= qq).astype(BF)
    c["c_mask_prevwin_bf"] = (qq <= kk).astype(BF)
    m = np.zeros((128, 128), np.float32); m[0, :] = 1.0
    c["c_mask_col0_bf"] = m.astype(BF)
    m2 = np.zeros((128, 128), np.float32); m2[:, 0] = 1.0
    c["c_mask_q0row_bf"] = m2.astype(BF)
    c["c_mask_diagwin_bf"] = ((kk <= qq) & (qq - kk <= 128)).astype(BF)
    sel = np.zeros((8, 8 * 128), np.float32)
    for e in range(8):
        sel[e, e * 128:(e + 1) * 128] = 1.0
    c["c_esel_bf"] = sel.astype(BF)
    s64 = np.arange(64)[:, None]; c64 = np.arange(64)[None, :]
    c["c_u64_bf"] = (s64 <= c64).astype(BF)
    c["c_u64x_bf"] = np.tile((s64 <= c64), (2, 1)).astype(BF)
    sel16 = np.zeros((16, 16 * 128), np.float32)
    for h in range(16):
        sel16[h, h * 128:(h + 1) * 128] = 1.0
    c["c_esel16_bf"] = sel16.astype(BF)
    bsel = np.zeros((D, 16), np.float32)
    for h in range(16):
        bsel[h * 64:(h + 1) * 64, h] = 1.0
    c["c_bsel_bf"] = bsel.astype(BF)
    hmask = np.zeros((16, D), np.float32)
    for h in range(16):
        hmask[h, h * 64:(h + 1) * 64] = 1.0
    c["c_hmask_bf"] = hmask.astype(BF)
    if T is not None:
        for r in range(4):
            k_ = np.arange(128)[:, None] + r * 128
            q_ = np.arange(512)[None, :]
            c[f"c_l3m_{r}"] = (k_ <= q_).astype(BF)
        pat = {}
        idmap = {}
        for qs in range(T // 512):
            for kt in l1_klist(qs, T):
                m = l1_mask(qs, kt, T).astype(BF)
                key = m.tobytes()
                if key not in pat:
                    pat[key] = len(pat)
                    c[f"c_l1m_{pat[key]}"] = m
                idmap[(qs, kt)] = pat[key]
        c["_l1m_map"] = idmap
    return c


CONST_SBUF = ["c_ident_bf", "c_ident_f32", "c_ones_col_bf", "c_ones_col_f32",
              "c_ones_row_f32", "c_ones_row_bf"]


def declare_consts(ctx):
    consts = make_consts(ctx.T)
    ctx.consts_meta = consts
    for name, arr in consts.items():
        if name.startswith("_"):
            continue
        dt = BF16 if arr.dtype == BF else F32
        ctx.din(name, arr.shape, dt)
    for name in CONST_SBUF:
        arr = consts[name]
        dt = BF16 if arr.dtype == BF else F32
        sb = ctx.sb(name[2:], list(arr.shape), dt)
        ctx.dma(sb, ctx.dram[name])


def setup_resident(ctx):
    nc = ctx.nc
    T = ctx.T
    ctx.x = ctx.sb("x_res", [128, ND, T], F32)
    ctx.ps = [nc.alloc_psum_tensor(f"ps{i}", [128, TC], F32).ap() for i in range(8)]
    ctx.tmp_sb = ctx.sb("tmp_sb", [128, 2, TC], F32)
    ctx.arena_init()
    base = ARENA_BLOCK_KB * 1024
    ctx.rows = ctx.arena_at(base, [1, 4 * TC], F32)
    ctx.xn_sb = ctx.arena_at(base + 8192, [128, ND, TC], BF16)
    ctx.cast_sb = ctx.arena_at(base + 16384, [128, ND, TC], BF16)
    ctx.sq_sb = ctx.arena_at(base + 24576, [128, ND, TC], BF16)
    ctx.xn_dram = ctx.dtmp("xn_dram", [ctx.NC, 128, ND * TC], BF16)


def load_x(ctx, name="xin"):
    xin = ctx.din(name, [D, ctx.T], F32)
    for dt in range(ND):
        ctx.dma(ctx.x[:, dt, :], xin[dt * 128:(dt + 1) * 128, :])
    ctx.barrier()


def store_x(ctx, name="xout"):
    xout = ctx.dout(name, [D, ctx.T], F32)
    ctx.barrier()
    for dt in range(ND):
        ctx.store(xout[dt * 128:(dt + 1) * 128, :], ctx.x[:, dt, :])
    ctx.barrier()


def load_xn_chunk(ctx, buf, c, sem="xnl", src_dram=None):
    sd = src_dram if src_dram is not None else getattr(ctx, "xn_src", None)
    if sd is None:
        sd = ctx.xn_dram
    src = sd[c, :, :].rearrange("p (n t) -> p n t", n=ND)
    ctx.dma(buf, src, sem=sem)
    return ctx.cnt[sem]


def emit_norm(ctx, rms=False, eps=1e-5, gain_col=None, chunk_hook=None,
              save_rows=None, residual_add_xn=False, src_dram=None,
              out_dram=None):
    nc = ctx.nc
    x = ctx.x
    xn = ctx.xn_sb
    if out_dram is None:
        out_dram = ctx.xn_dram
    ones_bf = ctx.sbuf["ones_col_bf"]
    ones_row = ctx.sbuf["ones_row_f32"]
    sCast, sSq, sStat, sRecip, sSqrt, sRows, sB, sApp = (
        "n_c", "n_s", "n_t", "n_r", "n_q", "n_w", "n_b", "n_a")
    bApp = ctx.cnt.get(sApp, 0)
    psS, psM, psR, psN = ctx.ps[0], ctx.ps[1], ctx.ps[2], ctx.ps[3]
    r_mean = ctx.rows[0:1, 0:TC]
    r_var = ctx.rows[0:1, TC:2 * TC]
    r_rstd = ctx.rows[0:1, 2 * TC:3 * TC]
    r_nmr = ctx.rows[0:1, 3 * TC:4 * TC]
    for c in range(ctx.NC):
        sl = slice(c * TC, (c + 1) * TC)
        xb = ctx.cast_sb
        sq = ctx.sq_sb
        if c > 0:
            ctx.wait(nc.tensor, sApp, bApp + c)
        if src_dram is not None:
            if c > 0:
                ctx.wait(nc.gpsimd, sStat)
                ctx.wait(nc.gpsimd, sSq)
                ctx.wait(nc.gpsimd, sApp)
            ctx.dma(xb, src_dram[c, :, :].rearrange("p (n t) -> p n t", n=ND),
                    sem="nsl")
            ctx.wait(nc.scalar, "nsl")
            ctx.wait(nc.tensor, "nsl")
            for dt in range(ND):
                ia = nc.scalar.activation(sq[:, dt, :], xb[:, dt, :], AF.Square)
            ctx.inc(ia, sSq, 1)
            ctx.wait(nc.tensor, sSq)
        else:
            for dt in range(ND):
                iv = nc.vector.tensor_copy(xb[:, dt, :], x[:, dt, sl])
                ia = nc.scalar.activation(sq[:, dt, :], x[:, dt, sl], AF.Square)
            ctx.inc(iv, sCast, 1)
            ctx.inc(ia, sSq, 1)
            ctx.wait(nc.tensor, sCast)
            ctx.wait(nc.tensor, sSq)
        if not rms:
            for dt in range(ND):
                nc.tensor.matmul(psM[0:1, :], ones_bf, xb[:, dt, :],
                                 start=(dt == 0), stop=(dt == ND - 1))
        for dt in range(ND):
            i = nc.tensor.matmul(psS[0:1, :], ones_bf, sq[:, dt, :],
                                 start=(dt == 0), stop=(dt == ND - 1))
        ctx.inc(i, sStat, 1)
        ctx.wait(nc.vector, sStat)
        if rms:
            nc.vector.tensor_scalar(r_var, psS[0:1, :], 1.0 / D, None, OP.mult)
        else:
            nc.vector.tensor_scalar(r_mean, psM[0:1, :], 1.0 / D, None, OP.mult)
            nc.vector.tensor_scalar(r_var, psS[0:1, :], 1.0 / D, None, OP.mult)
            nc.vector.tensor_tensor(ctx.tmp_sb[0:1, 0, 0:TC], r_mean, r_mean, OP.mult)
            nc.vector.tensor_sub(r_var, r_var, ctx.tmp_sb[0:1, 0, 0:TC])
        nc.vector.tensor_scalar(r_var, r_var, eps, None, OP.add)
        i = nc.vector.reciprocal(ctx.tmp_sb[0:1, 0, 0:TC], r_var)
        ctx.inc(i, sRecip, 1)
        ctx.wait(nc.scalar, sRecip)
        i = nc.scalar.activation(r_rstd, ctx.tmp_sb[0:1, 0, 0:TC], AF.Sqrt)
        ctx.inc(i, sSqrt, 1)
        if not rms:
            ctx.wait(nc.vector, sSqrt)
            i = nc.vector.scalar_tensor_tensor(r_nmr, r_mean, -1.0, r_rstd,
                                               OP.mult, OP.mult)
            ctx.inc(i, sRows, 1)
            ctx.wait(nc.tensor, sRows)
        else:
            ctx.wait(nc.tensor, sSqrt)
        i = nc.tensor.matmul(psR, ones_row, r_rstd, start=True, stop=True)
        if not rms:
            i = nc.tensor.matmul(psN, ones_row, r_nmr, start=True, stop=True)
        ctx.inc(i, sB, 1)
        if chunk_hook is not None:
            chunk_hook(c, r_mean, r_rstd)
        ctx.wait(nc.vector, sB)
        if residual_add_xn:
            if chunk_hook is not None:
                ctx.wait(nc.vector, "hkM")
        for dt in range(ND):
            xin_dt = xb[:, dt, :] if src_dram is not None else x[:, dt, sl]
            if rms:
                if gain_col is not None:
                    i = nc.vector.scalar_tensor_tensor(
                        xn[:, dt, :], xin_dt, gain_col[:, dt:dt + 1], psR,
                        OP.mult, OP.mult)
                else:
                    i = nc.vector.tensor_tensor(xn[:, dt, :], xin_dt, psR,
                                                OP.mult)
            else:
                tmp = ctx.tmp_sb[:, 1, :]
                nc.vector.tensor_tensor(tmp, xin_dt, psR, OP.mult)
                i = nc.vector.tensor_tensor(xn[:, dt, :], tmp, psN, OP.add)
                if residual_add_xn:
                    nc.vector.tensor_tensor(x[:, dt, sl], x[:, dt, sl], tmp,
                                            OP.add)
                    i = nc.vector.tensor_tensor(x[:, dt, sl], x[:, dt, sl],
                                                psN, OP.add)
        ctx.inc(i, sApp, 1)
        ctx.wait(nc.sync, sApp)
        dst = out_dram[c, :, :].rearrange("p (n t) -> p n t", n=ND)
        ctx.store(dst, xn[:, :, :])
        if save_rows is not None:
            ctx.store(save_rows[0][0:1, sl], r_mean)
            ctx.store(save_rows[1][0:1, sl], r_rstd)
        ctx.wait(nc.vector, "sst", ctx.cnt["sst"])
    ctx.barrier()


def emit_swiglu(ctx, H, load_g, load_u, load_d, act=AF.Silu,
                sg=1.0, su=1.0, sd=1.0, FS=768,
                gate_bias_col=None, up_bias_col=None,
                h_row_mul=None, in_row_mul=None, out_add_bias_col=None):
    nc = ctx.nc
    ctx.arena_mark()
    prev_borrow = ctx.borrow
    ctx.borrow = True
    NFT_MAX = FS // 128
    wgT = ctx.arena([128, ND, FS], BF16)
    wuT = ctx.arena([128, ND, FS], BF16) if load_u is not None else None
    wdT = ctx.arena([128, NFT_MAX, D], BF16)
    hsb = ctx.arena([128, NFT_MAX, TC], BF16)
    gsb = ctx.arena([128, TC], F32)
    xnc = ctx.arena([128, ND, TC], BF16)
    mrow = (ctx.arena([128, TC], BF16)
            if (h_row_mul is not None or in_row_mul is not None) else None)

    f0 = 0
    while f0 < H:
        fw = min(FS, H - f0)
        nft = fw // 128
        for dt in range(ND):
            load_g(dt, f0, fw, wgT[:, dt, 0:fw])
            if load_u is not None:
                load_u(dt, f0, fw, wuT[:, dt, 0:fw])
        for ht in range(nft):
            load_d(f0 + ht * 128, wdT[:, ht, :])
        ctx.barrier()
        for c in range(ctx.NC):
            sl = slice(c * TC, (c + 1) * TC)
            sXn, sGate, sUp, sSilu, sH, sDown, sAcc, sMrow = (
                "sw_x", "sw_g", "sw_u", "sw_s", "sw_h", "sw_d", "sw_a", "sw_m")
            bG, bS, bH, bA = (ctx.cnt.get(sGate, 0), ctx.cnt.get(sSilu, 0),
                              ctx.cnt.get(sH, 0), ctx.cnt.get(sAcc, 0))
            bM = ctx.cnt.get(sMrow, 0)
            load_xn_chunk(ctx, xnc, c, sem=sXn)
            ctx._dma_sems.add(sXn)
            ctx.wait(nc.tensor, sXn)
            rmul = h_row_mul if h_row_mul is not None else in_row_mul
            if rmul is not None:
                esel_sb, rowsrc, eidx = rmul
                i = nc.tensor.matmul(ctx.ps[6], esel_sb[:, eidx * 128:(eidx + 1) * 128],
                                     rowsrc[:, sl], start=True, stop=True)
                ctx.inc(i, sMrow, 1)
                ctx.wait(nc.vector, sMrow)
                i = nc.vector.tensor_copy(mrow, ctx.ps[6])
                ctx.inc(i, sMrow, 1)
            if in_row_mul is not None:
                for dt in range(ND):
                    i = nc.vector.tensor_tensor(xnc[:, dt, :], xnc[:, dt, :],
                                                mrow, OP.mult)
                ctx.inc(i, sMrow, 1)
                ctx.wait(nc.tensor, sMrow, ctx.cnt[sMrow])
            for ft in range(nft):
                psG = ctx.ps[ft % 2]
                psU = ctx.ps[2 + ft % 2]
                if ft >= 2:
                    if load_u is not None:
                        ctx.wait(nc.tensor, sSilu, bS + ft - 1)
                    ctx.wait(nc.tensor, sH, bH + ft - 1)
                for dt in range(ND):
                    i = nc.tensor.matmul(psG, wgT[:, dt, ft * 128:(ft + 1) * 128],
                                         xnc[:, dt, :],
                                         start=(dt == 0), stop=(dt == ND - 1))
                ctx.inc(i, sGate, 1)
                if load_u is not None:
                    for dt in range(ND):
                        i = nc.tensor.matmul(psU, wuT[:, dt, ft * 128:(ft + 1) * 128],
                                             xnc[:, dt, :],
                                             start=(dt == 0), stop=(dt == ND - 1))
                    ctx.inc(i, sUp, 1)
                fi = (f0 // 128) + ft
                gb = (0.0 if gate_bias_col is None
                      else gate_bias_col[:, fi:fi + 1])
                ctx.wait(nc.scalar, sGate)
                if load_u is None:
                    i = nc.scalar.activation(hsb[:, ft, :], psG,
                                             act if act else AF.Copy,
                                             bias=gb, scale=sg)
                    ctx.inc(i, sH, 1)
                    continue
                if ft >= 1:
                    ctx.wait(nc.scalar, sH, bH + ft)
                i = nc.scalar.activation(gsb, psG, act if act else AF.Copy,
                                         bias=gb, scale=sg)
                ctx.inc(i, sSilu, 1)
                ctx.wait(nc.vector, sUp)
                ctx.wait(nc.vector, sSilu)
                if h_row_mul is not None:
                    ctx.wait(nc.vector, sMrow, bM + 2)
                if up_bias_col is None:
                    i = nc.vector.scalar_tensor_tensor(hsb[:, ft, :], psU, su, gsb,
                                                       OP.mult, OP.mult)
                else:
                    ub = up_bias_col[:, fi:fi + 1]
                    nc.vector.tensor_scalar(ctx.tmp_sb[:, 0, :], psU, su, ub,
                                            OP.mult, op1=OP.add)
                    i = nc.vector.tensor_tensor(hsb[:, ft, :], ctx.tmp_sb[:, 0, :],
                                                gsb, OP.mult)
                if h_row_mul is not None:
                    i = nc.vector.tensor_tensor(hsb[:, ft, :], hsb[:, ft, :],
                                                mrow, OP.mult)
                ctx.inc(i, sH, 1)
            ctx.wait(nc.tensor, sH, bH + nft)
            for dt2 in range(ND):
                psD = ctx.ps[4 + dt2 % 2]
                if dt2 >= 2:
                    ctx.wait(nc.tensor, sAcc, bA + dt2 - 1)
                for ht in range(nft):
                    i = nc.tensor.matmul(psD, wdT[:, ht, dt2 * 128:(dt2 + 1) * 128],
                                         hsb[:, ht, :],
                                         start=(ht == 0), stop=(ht == nft - 1))
                ctx.inc(i, sDown, 1)
                ctx.wait(nc.vector, sDown)
                i = nc.vector.scalar_tensor_tensor(ctx.x[:, dt2, sl], psD, sd,
                                                   ctx.x[:, dt2, sl], OP.mult, OP.add)
                if out_add_bias_col is not None and f0 == 0:
                    i = nc.vector.tensor_scalar(ctx.x[:, dt2, sl], ctx.x[:, dt2, sl],
                                                out_add_bias_col[:, dt2:dt2 + 1],
                                                None, OP.add)
                ctx.inc(i, sAcc, 1)
            ctx.barrier()
        f0 += fw
    ctx.borrow = prev_borrow
    ctx.arena_release()


def emit_l0_ffn(ctx):
    H = 3072
    s = 1.0 / 127.0
    wg = ctx.din("L0_ffn_w_gate", [H, D], I32)
    wu = ctx.din("L0_ffn_w_up", [H, D], I32)
    wd = ctx.din("L0_ffn_w_down", [D, H], I32)
    wg_bf = ctx.dtmp("l0_wg_bf", [H, D], BF16)
    wu_bf = ctx.dtmp("l0_wu_bf", [H, D], BF16)
    wd_bf = ctx.dtmp("l0_wd_bf", [D, H], BF16)
    ctx.dma(wg_bf, wg)
    ctx.dma(wu_bf, wu)
    ctx.dma(wd_bf, wd)

    emit_norm(ctx, rms=False, eps=1e-5)

    def load_g(dt, f0, fw, dest):
        ctx.dma_t(dest, wg_bf[f0:f0 + fw, dt * 128:(dt + 1) * 128])

    def load_u(dt, f0, fw, dest):
        ctx.dma_t(dest, wu_bf[f0:f0 + fw, dt * 128:(dt + 1) * 128])

    def load_d(h0, dest):
        ctx.dma_t(dest, wd_bf[:, h0:h0 + 128])

    emit_swiglu(ctx, H, load_g, load_u, load_d, act=AF.Silu,
                sg=s, su=s, sd=s, FS=768)


def emit_l3_ffn(ctx):
    nc = ctx.nc
    CB = ctx.din("L3_ffn_CB", [D, D], F32)
    SCB = ctx.din("L3_ffn_SCB", [D], F32)
    BIAS = ctx.din("L3_ffn_bias", [D], F32)
    cb_bf = ctx.dtmp("l3_cb_bf", [D, D], BF16)
    ctx.dma(cb_bf, CB)
    scb_col = emit_feature_cols(ctx, "l3_scb", SCB, D)
    bias_col = emit_feature_cols(ctx, "l3_bias", BIAS, D)
    nc.vector.tensor_scalar(scb_col, scb_col, 1.0 / 127.0, None, OP.mult)

    emit_norm(ctx, rms=False, eps=1e-5)

    ctx.arena_mark()
    cbT = ctx.arena([128, ND, D], BF16)
    xnc = ctx.arena([128, ND, TC], BF16)
    for kt in range(ND):
        ctx.dma_t(cbT[:, kt, :], cb_bf[:, kt * 128:(kt + 1) * 128])
    ctx.barrier()
    sM, sV, sXn = "cb_m", "cb_v", "cb_x"
    idx = 0
    for c in range(ctx.NC):
        sl = slice(c * TC, (c + 1) * TC)
        ctx._dma_sems.add(sXn)
        load_xn_chunk(ctx, xnc, c, sem=sXn)
        ctx.wait(nc.tensor, sXn)
        bV = ctx.cnt.get(sV, 0) - idx
        for nt in range(ND):
            ps = ctx.ps[idx % 2]
            if idx >= 2:
                ctx.wait(nc.tensor, sV, bV + idx - 1)
            for kt in range(ND):
                i = nc.tensor.matmul(ps, cbT[:, kt, nt * 128:(nt + 1) * 128],
                                     xnc[:, kt, :],
                                     start=(kt == 0), stop=(kt == ND - 1))
            ctx.inc(i, sM, 1)
            ctx.wait(nc.vector, sM)
            tmp = ctx.tmp_sb[:, 0, :]
            nc.vector.scalar_tensor_tensor(tmp, ps, scb_col[:, nt:nt + 1],
                                           ctx.x[:, nt, sl], OP.mult, OP.add)
            i = nc.vector.tensor_scalar(ctx.x[:, nt, sl], tmp,
                                        bias_col[:, nt:nt + 1], None, OP.add)
            ctx.inc(i, sV, 1)
            idx += 1
        ctx.barrier()
    ctx.arena_release()


def emit_feature_cols(ctx, name, vec_ap, NF, dtype=F32, use_arena=False):
    nc = ctx.nc
    nt = NF // 128
    if not hasattr(ctx, "_fc_ld"):
        ctx._fc_ld = ctx.tmp_sb[0:32, 0, 0:128]
    tmp = ctx._fc_ld[0:nt, :]
    col = ctx.arena([128, nt], dtype) if use_arena else ctx.sb(name, [128, nt], dtype)
    if len(vec_ap.shape) > 1:
        vec_ap = vec_ap.flatten()
    if ctx.cnt.get("colT", 0):
        ctx.wait(nc.gpsimd, "colT")
    ctx.dma(tmp, vec_ap.rearrange("(n p) -> n p", p=128), sem="colL")
    ctx.wait(nc.tensor, "colL")
    if ctx.cnt.get("colV", 0) > 0:
        ctx.wait(nc.tensor, "colV")
    ps = ctx.ps[6]
    i = nc.tensor.transpose(ps[:, 0:nt], tmp, ctx.sbuf["ident_f32"][0:nt, 0:nt])
    ctx.inc(i, "colT", 1)
    ctx.wait(nc.vector, "colT")
    i = nc.vector.tensor_copy(col, ps[:, 0:nt])
    ctx.inc(i, "colV", 1)
    return col


def make_router_hook(ctx, wr_sb, colsum_col, br_col, ladj_sb, tail,
                     apply_rstd=True):
    nc = ctx.nc
    ones_row = ctx.sbuf["ones_row_f32"]

    def hook(c, r_mean, r_rstd):
        psL, psBm, psBr = ctx.ps[4], ctx.ps[5], ctx.ps[7]
        if c > 0:
            ctx.wait(nc.tensor, "hkV", c)
        for dt in range(ND):
            i = nc.tensor.matmul(psL[0:8, :], wr_sb[:, dt, :],
                                 ctx.x[:, dt, c * TC:(c + 1) * TC],
                                 start=(dt == 0), stop=(dt == ND - 1))
        i = nc.tensor.matmul(psBm[0:8, :], ones_row[0:1, 0:8], r_mean,
                             start=True, stop=True)
        if apply_rstd:
            i = nc.tensor.matmul(psBr[0:8, :], ones_row[0:1, 0:8], r_rstd,
                                 start=True, stop=True)
        ctx.inc(i, "hkM", 1)
        ctx.wait(nc.vector, "hkM")
        t1 = ctx.tmp_sb[0:8, 0, :]
        t2 = ctx.tmp_sb[0:8, 1, :]
        nc.vector.tensor_scalar(t1, psBm[0:8, :], colsum_col, None, OP.mult)
        nc.vector.tensor_sub(t2, psL[0:8, :], t1)
        if apply_rstd:
            nc.vector.tensor_tensor(t2, t2, psBr[0:8, :], OP.mult)
        i = nc.vector.tensor_scalar(ladj_sb, t2, br_col, None, OP.add)
        ctx.inc(i, "hkV", 1)
        tail(c, ladj_sb)

    return hook


def emit_l2_ffn(ctx):
    nc = ctx.nc
    HE = 4096
    Wr = ctx.din("L2_ffn_Wr", [D, 8], F32)
    br = ctx.din("L2_ffn_br", [8], F32)
    W1 = ctx.din("L2_ffn_W1", [8, D, HE], F32)
    b1 = ctx.din("L2_ffn_b1", [8, HE], F32)
    W2 = ctx.din("L2_ffn_W2", [8, HE, D], F32)
    b2 = ctx.din("L2_ffn_b2", [8, D], F32)
    w1_bf = ctx.dtmp("l2f_w1_bf", [8, D, HE], BF16)
    w2_bf = ctx.dtmp("l2f_w2_bf", [8, HE, D], BF16)
    for e in range(8):
        ctx.dma(w1_bf[e], W1[e])
        ctx.dma(w2_bf[e], W2[e])
    wr_sb = ctx.sb("l2f_wr", [128, ND, 8], F32)
    ctx.dma(wr_sb, Wr.rearrange("(n p) e -> p n e", p=128))
    br_col = ctx.sb("l2f_br", [8, 1], F32)
    ctx.dma(br_col, br.unsqueeze(1))
    b1_cols = [emit_feature_cols(ctx, f"l2f_b1_{e}", b1[e:e + 1, :], HE,
                                 use_arena=True)
               for e in range(8)]
    b2_cols = [emit_feature_cols(ctx, f"l2f_b2_{e}", b2[e:e + 1, :], D,
                                 use_arena=True)
               for e in range(8)]
    ctx.barrier()
    for dt in range(ND):
        i = nc.tensor.matmul(ctx.ps[6][0:8, 0:1], wr_sb[:, dt, :],
                             ctx.sbuf["ones_col_f32"],
                             start=(dt == 0), stop=(dt == ND - 1))
    ctx.inc(i, "cs_m", 1)
    ctx.wait(nc.vector, "cs_m")
    colsum_col = ctx.sb("l2f_colsum", [8, 1], F32)
    nc.vector.tensor_copy(colsum_col, ctx.ps[6][0:8, 0:1])
    ctx.barrier()

    ctx.arena_mark()
    mask_fm = ctx.arena([8, ctx.T], BF16)
    ladj_sb = ctx.arena([8, TC], F32)
    mtm = ctx.arena([128, 8], BF16)

    def route_tail(c, ladj):
        ps6 = ctx.ps[6]
        ps7b = ctx.ps[7].bitcast(BF16)
        ctx.wait(nc.tensor, "hkV")
        for j in range(4):
            ctx.wait(nc.tensor, "rtV")
            i = nc.tensor.transpose(ps6[:, j * 8:(j + 1) * 8],
                                    ladj[:, j * 128:(j + 1) * 128],
                                    ctx.sbuf["ident_f32"][0:8, 0:8])
            ctx.inc(i, "rtT", 1)
            ctx.wait(nc.vector, "rtT")
            vmax = ctx.tmp_sb[:, 0, 0:1]
            i = nc.vector.reduce_max(vmax, ps6[:, j * 8:(j + 1) * 8], axis=AX.X)
            ctx.vsync(i)
            i = nc.vector.tensor_scalar(mtm, ps6[:, j * 8:(j + 1) * 8], vmax,
                                        None, OP.is_ge)
            ctx.inc(i, "rtV", 1)
            ctx.wait(nc.tensor, "rtV")
            i = nc.tensor.transpose(ps7b[0:8, 0:128], mtm,
                                    ctx.sbuf["ident_bf"])
            ctx.inc(i, "rtT", 1)
            ctx.wait(nc.vector, "rtT")
            i = nc.vector.tensor_copy(
                mask_fm[:, c * TC + j * 128:c * TC + (j + 1) * 128],
                ps7b[0:8, 0:128])
            ctx.inc(i, "rtV", 1)

    hook = make_router_hook(ctx, wr_sb, colsum_col, br_col, ladj_sb, route_tail)
    emit_norm(ctx, rms=False, eps=1e-5, chunk_hook=hook)

    esel = ctx.arena([8, 8 * 128], BF16)
    ctx.dma(esel, ctx.dram["c_esel_bf"])
    ctx.barrier()

    for e in range(8):
        def load_g(dt, f0, fw, dest, e=e):
            ctx.dma(dest, w1_bf[e, dt * 128:(dt + 1) * 128, f0:f0 + fw])

        def load_d(h0, dest, e=e):
            ctx.dma(dest, w2_bf[e, h0:h0 + 128, :])

        emit_swiglu(ctx, HE, load_g, None, load_d, act=AF.Relu,
                    FS=512, gate_bias_col=b1_cols[e],
                    in_row_mul=(esel, mask_fm, e),
                    out_add_bias_col=b2_cols[e])
    ctx.arena_release()


def emit_l1_ffn(ctx):
    nc = ctx.nc
    FF = 2048
    rw = ctx.din("L1_ffn_router_w", [8, D], F32)
    rb = ctx.din("L1_ffn_router_b", [8], F32)
    temp = ctx.din("L1_ffn_temperature", [1], F32)
    wg = ctx.din("L1_ffn_wg", [8, D, FF], F32)
    wp = ctx.din("L1_ffn_wp", [8, D, FF], F32)
    wo = ctx.din("L1_ffn_wo", [8, FF, D], F32)
    wg_bf = ctx.dtmp("l1f_wg_bf", [8, D, FF], BF16)
    wp_bf = ctx.dtmp("l1f_wp_bf", [8, D, FF], BF16)
    wo_bf = ctx.dtmp("l1f_wo_bf", [8, FF, D], BF16)
    for e in range(8):
        ctx.dma(wg_bf[e], wg[e])
        ctx.dma(wp_bf[e], wp[e])
        ctx.dma(wo_bf[e], wo[e])
    rw_ld = ctx.sb("ld4k", [8, D], F32)
    ctx.dma(rw_ld, rw)
    rwT = ctx.sb("l2f_wr", [128, ND, 8], F32)
    rb_col = ctx.sb("l1f_rb", [8, 1], F32)
    ctx.dma(rb_col, rb.unsqueeze(1))
    temp_sb = ctx.sb("l1f_temp", [1, 2], F32)
    ctx.dma(temp_sb[0:1, 0:1], temp.unsqueeze(1))
    ctx.barrier()
    for dt in range(ND):
        i = nc.tensor.transpose(ctx.ps[6][:, 0:8],
                                rw_ld[:, dt * 128:(dt + 1) * 128],
                                ctx.sbuf["ident_f32"][0:8, 0:8])
        ctx.inc(i, "rwT", 1)
        ctx.wait(nc.vector, "rwT")
        i = nc.vector.tensor_copy(rwT[:, dt, :], ctx.ps[6][:, 0:8])
        ctx.inc(i, "rwV", 1)
        ctx.wait(nc.tensor, "rwV")
    for dt in range(ND):
        i = nc.tensor.matmul(ctx.ps[6][0:8, 0:1], rwT[:, dt, :],
                             ctx.sbuf["ones_col_f32"],
                             start=(dt == 0), stop=(dt == ND - 1))
    ctx.inc(i, "cs_m", 1)
    ctx.wait(nc.vector, "cs_m")
    colsum_col = ctx.sb("l1f_colsum", [8, 1], F32)
    nc.vector.tensor_copy(colsum_col, ctx.ps[6][0:8, 0:1])
    i = nc.vector.tensor_scalar(temp_sb[0:1, 1:2], temp_sb[0:1, 0:1], 0.1,
                                None, OP.max)
    ctx.vsync(i)
    i = nc.vector.reciprocal(temp_sb[0:1, 0:1], temp_sb[0:1, 1:2])
    ctx.inc(i, "l1v", 1)
    ctx.wait(nc.tensor, "l1v")
    i = nc.tensor.matmul(ctx.ps[5][0:8, 0:1], ctx.sbuf["ones_row_f32"][0:1, 0:8],
                         temp_sb[0:1, 0:1], start=True, stop=True)
    ctx.inc(i, "l1m", 1)
    ctx.wait(nc.vector, "l1m")
    invt_col = ctx.sb("l1f_invt", [8, 1], F32)
    nc.vector.tensor_copy(invt_col, ctx.ps[5][0:8, 0:1])
    ctx.barrier()

    ctx.arena_mark()
    w_fm = ctx.arena([8, ctx.T], BF16)
    ctx.arena_mark()
    la = ctx.arena([8, ctx.T], F32)
    et = ctx.arena([8, TC], F32)
    lse_row = ctx.arena([1, TC], F32)
    ladj_sb = ctx.arena([8, TC], F32)
    gtm = ctx.arena([128, 8], F32)
    gn = ctx.arena([128, 8], F32)
    m1 = ctx.arena([128, 8], F32)
    gm = ctx.arena([128, 8], F32)
    sel = ctx.arena([128, 8], F32)
    w_tm = ctx.arena([128, 8], BF16)

    def l1_tail(c, ladj):
        i = nc.vector.tensor_scalar(la[:, c * TC:(c + 1) * TC], ladj,
                                    invt_col, None, OP.mult)
        ctx.inc(i, "l1t", 1)

    hook = make_router_hook(ctx, rwT, colsum_col, rb_col, ladj_sb, l1_tail)
    emit_norm(ctx, rms=False, eps=1e-5, chunk_hook=hook, residual_add_xn=True)

    cA = ctx.tmp_sb[0:8, 0, 0:1]
    cB = ctx.tmp_sb[0:8, 0, 2:3]
    cC = ctx.tmp_sb[0:8, 0, 4:5]
    cD = ctx.tmp_sb[0:8, 0, 6:7]
    cE = ctx.tmp_sb[0:8, 0, 8:9]
    for it in range(3):
        for c in range(ctx.NC):
            sl = slice(c * TC, (c + 1) * TC)
            if c > 0 or it > 0:
                ctx.wait(nc.scalar, "skv")
            i = nc.scalar.activation(et, la[:, sl], AF.Exp)
            ctx.inc(i, "ska", 1)
            ctx.wait(nc.tensor, "ska")
            if c > 0 or it > 0:
                ctx.wait(nc.tensor, "skv")
                ctx.wait(nc.tensor, "skl")
            i = nc.tensor.matmul(ctx.ps[0][0:1, :], ctx.sbuf["ones_col_f32"][0:8, :],
                                 et, start=True, stop=True)
            ctx.inc(i, "skm", 1)
            ctx.wait(nc.scalar, "skm")
            i = nc.scalar.activation(lse_row, ctx.ps[0][0:1, :], AF.Ln)
            ctx.inc(i, "skl", 1)
            ctx.wait(nc.tensor, "skl")
            i = nc.tensor.matmul(ctx.ps[1][0:8, :],
                                 ctx.sbuf["ones_row_f32"][0:1, 0:8],
                                 lse_row, start=True, stop=True)
            ctx.inc(i, "skb", 1)
            ctx.wait(nc.vector, "skb")
            i = nc.vector.tensor_tensor(la[:, sl], la[:, sl], ctx.ps[1][0:8, :],
                                        OP.subtract)
            ctx.inc(i, "skv", 1)
        ctx.wait(nc.vector, "skv")
        i = nc.vector.reduce_max(cA, la, axis=AX.X)
        ctx.vsync(i)
        i = nc.vector.tensor_scalar(cB, cA, -1.0, None, OP.mult)
        ctx.vsync(i)
        i = nc.vector.memset(cC, 0.0)
        ctx.inc(i, "skn", 1)
        ctx.wait(nc.scalar, "skn")
        for c in range(ctx.NC):
            sl = slice(c * TC, (c + 1) * TC)
            if c > 0:
                ctx.wait(nc.scalar, "skv3")
            i = nc.scalar.activation(et, la[:, sl], AF.Exp, bias=cB,
                                     accum_out=cE)
            ctx.inc(i, "ska2", 1)
            ctx.wait(nc.vector, "ska2")
            i = nc.vector.tensor_tensor(cC, cC, cE, OP.add)
            ctx.inc(i, "skv3", 1)
        ctx.wait(nc.vector, "skv3")
        ctx.wait(nc.scalar, "skv3")
        i = nc.scalar.activation(cD, cC, AF.Ln)
        ctx.inc(i, "ska3", 1)
        ctx.wait(nc.vector, "ska3")
        i = nc.vector.tensor_tensor(cD, cD, cA, OP.add)
        ctx.vsync(i)
        i = nc.vector.tensor_scalar(la, la, cD, None, OP.subtract)
        ctx.inc(i, "skv2", 1)
        ctx.wait(nc.scalar, "skv2")
    ctx.wait(nc.scalar, "skv2")
    for t in range(ctx.NT):
        tsl = slice(t * 128, (t + 1) * 128)
        if t > 0:
            ctx.wait(nc.scalar, "tpt")
        i = nc.scalar.activation(et[:, 0:128], la[:, tsl], AF.Exp)
        ctx.inc(i, "tpa", 1)
        ctx.wait(nc.tensor, "tpa")
        if t > 0:
            ctx.wait(nc.tensor, "tpe")
        i = nc.tensor.transpose(ctx.ps[6][:, 0:8], et[:, 0:128],
                                ctx.sbuf["ident_f32"][0:8, 0:8])
        ctx.inc(i, "tpt", 1)
        ctx.wait(nc.vector, "tpt")
        i = nc.vector.tensor_copy(gtm, ctx.ps[6][:, 0:8])
        ctx.inc(i, "tpe", 1)
        ctx.wait(nc.vector, "tpe")
        vS = ctx.tmp_sb[:, 0, 0:1]
        i = nc.vector.reduce_sum(vS, gtm, axis=AX.X)
        ctx.vsync(i)
        i = nc.vector.tensor_scalar(vS, vS, 1e-8, None, OP.add)
        ctx.vsync(i)
        vR = ctx.tmp_sb[:, 0, 2:3]
        i = nc.vector.reciprocal(vR, vS)
        ctx.vsync(i)
        i = nc.vector.tensor_scalar(gn, gtm, vR, None, OP.mult)
        ctx.vsync(i)
        v1 = ctx.tmp_sb[:, 0, 4:5]
        i = nc.vector.reduce_max(v1, gn, axis=AX.X)
        ctx.vsync(i)
        i = nc.vector.tensor_scalar(m1, gn, v1, None, OP.is_ge)
        ctx.vsync(i)
        i = nc.vector.scalar_tensor_tensor(gm, m1, -1e9, gn, OP.mult, OP.add)
        ctx.vsync(i)
        v2 = ctx.tmp_sb[:, 0, 6:7]
        i = nc.vector.reduce_max(v2, gm, axis=AX.X)
        ctx.vsync(i)
        i = nc.vector.tensor_scalar(sel, gn, v2, None, OP.is_ge)
        ctx.vsync(i)
        vW = ctx.tmp_sb[:, 0, 8:9]
        i = nc.vector.tensor_tensor(vW, v1, v2, OP.add)
        ctx.vsync(i)
        i = nc.vector.tensor_scalar(vW, vW, 1e-8, None, OP.add)
        ctx.vsync(i)
        vQ = ctx.tmp_sb[:, 0, 10:11]
        i = nc.vector.reciprocal(vQ, vW)
        ctx.vsync(i)
        i = nc.vector.tensor_tensor(gn, gn, sel, OP.mult)
        ctx.vsync(i)
        i = nc.vector.tensor_scalar(w_tm, gn, vQ, None, OP.mult)
        ctx.inc(i, "tpw", 1)
        ctx.wait(nc.tensor, "tpw")
        i = nc.tensor.transpose(ctx.ps[7].bitcast(BF16)[0:8, 0:128], w_tm,
                                ctx.sbuf["ident_bf"])
        ctx.inc(i, "tpt", 1)
        ctx.wait(nc.vector, "tpt")
        i = nc.vector.tensor_copy(w_fm[:, tsl], ctx.ps[7].bitcast(BF16)[0:8, 0:128])
        ctx.inc(i, "tpe", 1)
    ctx.barrier()
    ctx.arena_release()

    esel = ctx.arena([8, 8 * 128], BF16)
    ctx.dma(esel, ctx.dram["c_esel_bf"])
    ctx.barrier()

    for e in range(8):
        def load_g(dt, f0, fw, dest, e=e):
            ctx.dma(dest, wg_bf[e, dt * 128:(dt + 1) * 128, f0:f0 + fw])

        def load_u(dt, f0, fw, dest, e=e):
            ctx.dma(dest, wp_bf[e, dt * 128:(dt + 1) * 128, f0:f0 + fw])

        def load_d(h0, dest, e=e):
            ctx.dma(dest, wo_bf[e, h0:h0 + 128, :])

        emit_swiglu(ctx, FF, load_g, load_u, load_d, act=AF.Relu,
                    FS=512, h_row_mul=(esel, w_fm, e))
    ctx.arena_release()




def emit_l1_attn(ctx):
    nc = ctx.nc
    T = ctx.T
    NQS = T // 512
    names = ["Wq", "bq", "Wk", "bk", "Wv", "bv", "Wo", "bo"]
    dr = {}
    for n_ in names:
        shp = [D, D] if n_.startswith("W") else [D]
        dr[n_] = ctx.din(f"L1_attn_{n_}", shp, F32)
    w_bf = {}
    for n_ in ["Wq", "Wk", "Wv", "Wo"]:
        w_bf[n_] = ctx.dtmp(f"l1a_{n_}_bf", [D, D], BF16)
        ctx.dma(w_bf[n_], dr[n_])
    bq_col = emit_feature_cols(ctx, "l1a_bq", dr["bq"], D)
    bk_col = emit_feature_cols(ctx, "l1a_bk", dr["bk"], D)
    bo_col = emit_feature_cols(ctx, "l1a_bo", dr["bo"], D)
    bv_ld = ctx.sb("ld4k", [8, D], F32)
    ctx.dma(bv_ld[0:1, :], dr["bv"].unsqueeze(0))
    bv_b = ctx.sb("l1a_bv_b", [128, D], BF16)
    bsel_sb = ctx.sb("bsel", [128, ND, 16], BF16)
    ctx.dma(bsel_sb, ctx.dram["c_bsel_bf"].rearrange("(n p) h -> p n h", p=128))
    kmax = ctx.sb("l1a_kmax", [16, 1], F32)
    ctx.barrier()
    for half in range(2):
        i = nc.tensor.matmul(ctx.ps[0], ctx.sbuf["ones_row_f32"],
                             bv_ld[0:1, half * 512:(half + 1) * 512],
                             start=True, stop=True)
        ctx.inc(i, "l1b", 1)
        ctx.wait(nc.vector, "l1b")
        i = nc.vector.tensor_copy(bv_b[:, half * 512:(half + 1) * 512], ctx.ps[0])
        ctx.inc(i, "l1c", 1)
        ctx.wait(nc.tensor, "l1c")
    nc.vector.tensor_scalar(bq_col, bq_col, 0.125, None, OP.mult)
    nc.vector.memset(kmax, 0.0)

    emit_norm(ctx, rms=False, eps=1e-5)

    q_dram = ctx.dtmp("l1a_q", [D, T], BF16)
    k_dram = ctx.dtmp("l1a_k", [D, T], BF16)
    v_dram = ctx.dtmp("l1a_v", [T, D], BF16)

    ctx.arena_mark()
    wT = ctx.arena([128, ND, D], BF16)
    xnc = ctx.arena([128, ND, TC], BF16)
    ev = ctx.arena([128, TC], BF16)
    sq = ctx.arena([128, TC], BF16)
    vev = ctx.arena([128, D], BF16)

    def proj_fm(wname, bias_col, scale, out_dram, track_knorm):
        for dt in range(ND):
            ctx.dma_t(wT[:, dt, :], w_bf[wname][:, dt * 128:(dt + 1) * 128])
        ctx.barrier()
        for c in range(ctx.NC):
            sl = slice(c * TC, (c + 1) * TC)
            if c > 0:
                ctx.wait(nc.gpsimd, "pa_m")
            load_xn_chunk(ctx, xnc, c, sem="pa_x")
            ctx.wait(nc.tensor, "pa_x")
            for jt in range(ND):
                ps = ctx.ps[jt % 2]
                if jt >= 2:
                    ctx.wait(nc.tensor, "pa_e", ctx.cnt.get("pa_e", 0) - 1)
                for dt in range(ND):
                    i = nc.tensor.matmul(ps, wT[:, dt, jt * 128:(jt + 1) * 128],
                                         xnc[:, dt, :],
                                         start=(dt == 0), stop=(dt == ND - 1))
                ctx.inc(i, "pa_m", 1)
                ctx.wait(nc.vector, "pa_m")
                if ctx.cnt.get("pa_s", 0):
                    ctx.wait(nc.vector, "pa_s")
                i = nc.vector.tensor_scalar(ev, ps, scale,
                                            bias_col[:, jt:jt + 1],
                                            OP.mult, op1=OP.add)
                ctx.inc(i, "pa_e", 1)
                ctx.wait(nc.sync, "pa_e")
                ctx.store(out_dram[jt * 128:(jt + 1) * 128, sl], ev, sem="pa_s")
                if track_knorm:
                    ctx.wait(nc.scalar, "pa_e")
                    i = nc.scalar.activation(sq, ev, AF.Square)
                    ctx.inc(i, "pa_q", 1)
                    ctx.wait(nc.tensor, "pa_q")
                    i = nc.tensor.matmul(ctx.ps[4][0:16, :], bsel_sb[:, jt, :],
                                         sq, start=(jt == 0), stop=(jt == ND - 1),
                                         skip_group_check=True)
                    if jt == ND - 1:
                        ctx.inc(i, "pa_n", 1)
                        ctx.wait(nc.vector, "pa_n")
                        i = nc.vector.reduce_max(ctx.tmp_sb[0:16, 0, 0:1],
                                                 ctx.ps[4][0:16, :], axis=AX.X)
                        ctx.vsync(i)
                        i = nc.vector.tensor_tensor(kmax, kmax,
                                                    ctx.tmp_sb[0:16, 0, 0:1],
                                                    OP.max)
                        ctx.inc(i, "pa_k", 1)
                        ctx.wait(nc.tensor, "pa_k")
            ctx.barrier()

    proj_fm("Wq", bq_col, 0.125, q_dram, False)
    proj_fm("Wk", bk_col, 1.0, k_dram, True)

    for dt in range(ND):
        ctx.dma_t(wT[:, dt, :], w_bf["Wv"][:, dt * 128:(dt + 1) * 128])
    ctx.barrier()
    for tt in range(ctx.NT):
        tsl = slice(tt * 128, (tt + 1) * 128)
        c = (tt * 128) // TC
        if tt % (TC // 128) == 0:
            if tt > 0:
                ctx.wait(nc.gpsimd, "pv_m")
            load_xn_chunk(ctx, xnc, c, sem="pa_x")
            ctx.wait(nc.tensor, "pa_x")
        toff = (tt * 128) % TC
        if ctx.cnt.get("pv_s", 0):
            ctx.wait(nc.vector, "pv_s")
        for half in range(2):
            ps = ctx.ps[half]
            for dt in range(ND):
                i = nc.tensor.matmul(
                    ps, xnc[:, dt, toff:toff + 128],
                    wT[:, dt, half * 512:(half + 1) * 512],
                    start=(dt == 0), stop=(dt == ND - 1))
            ctx.inc(i, "pv_m", 1)
            ctx.wait(nc.vector, "pv_m")
            i = nc.vector.tensor_tensor(vev[:, half * 512:(half + 1) * 512], ps,
                                        bv_b[:, half * 512:(half + 1) * 512],
                                        OP.add)
            ctx.inc(i, "pv_e", 1)
        ctx.wait(nc.sync, "pv_e")
        ctx.store(v_dram[tsl, :], vev, sem="pv_s")
        ctx.wait(nc.tensor, "pv_e")
    ctx.barrier()
    i = nc.scalar.activation(kmax, kmax, AF.Sqrt)
    ctx.inc(i, "pk_a", 1)
    ctx.barrier()
    ctx.arena_release()

    consts = ctx.consts_meta
    mmap = consts["_l1m_map"]
    ctx.arena_mark()
    ctx.borrow = True
    q_all = ctx.arena([128, ND, 512], BF16)
    y_cat = ctx.arena([128, ND, 512], BF16)
    MAXK = max(len(l1_klist(qs, T)) for qs in range(NQS))
    k_all = ctx.arena([128, ND, MAXK * 128], BF16)
    vext = ctx.arena([128, MAXK, 4 * 65], BF16)
    vld = ctx.arena([128, MAXK, 256], BF16)
    esel16 = ctx.arena([16, 16 * 128], BF16)
    maskb = [ctx.arena([128, 512], BF16) for _ in range(2)]
    ptb = [ctx.arena([128, 512], BF16) for _ in range(2)]
    woT = ctx.arena([128, ND, 128], BF16)
    hmask = ctx.arena([16, D], BF16)
    nq16 = ctx.arena([16, 512], F32)
    negC = ctx.arena([16, 512], BF16)
    zrec = ctx.arena([1, 512], F32)
    zb = ctx.arena([128, 512], F32)
    shoff = (ctx._arena_off + 31) & ~31
    qsq = ctx.arena([128, ND, 512], BF16)
    p0 = ctx.arena_at(shoff, [16, T], BF16)
    q0s = ctx.arena([128, 4, 512], BF16)
    ctx.dma(esel16, ctx.dram["c_esel16_bf"])
    ctx.dma(hmask, ctx.dram["c_hmask_bf"])
    ctx.barrier()

    qv = ctx.dram["l1a_q"].rearrange("(n p) t -> p n t", p=128)
    kv = ctx.dram["l1a_k"].rearrange("(n p) t -> p n t", p=128)
    for qs in range(NQS):
        qsl = slice(qs * 512, (qs + 1) * 512)
        klist = l1_klist(qs, T)
        NK = len(klist)
        ctx.dma(q_all, qv[:, :, qsl], sem="pb_l")
        for ki, kt in enumerate(klist):
            ctx.dma(k_all[:, :, ki * 128:(ki + 1) * 128],
                    kv[:, :, kt * 128:(kt + 1) * 128], sem="pb_l")
        ctx.wait(nc.scalar, "pb_l")
        for dt in range(ND):
            i = nc.scalar.activation(qsq[:, dt, :], q_all[:, dt, :], AF.Square)
        ctx.inc(i, "pb_sq", 1)
        ctx.wait(nc.tensor, "pb_sq")
        if qs > 0:
            ctx.wait(nc.tensor, "pb_nc", qs)
        for dt in range(ND):
            i = nc.tensor.matmul(ctx.ps[6][0:16, :], bsel_sb[:, dt, :],
                                 qsq[:, dt, :],
                                 start=(dt == 0), stop=(dt == ND - 1))
        ctx.inc(i, "pb_nm", 1)
        ctx.wait(nc.scalar, "pb_nm")
        i = nc.scalar.activation(nq16, ctx.ps[6][0:16, :], AF.Sqrt)
        ctx.inc(i, "pb_na", 1)
        ctx.wait(nc.vector, "pb_na")
        i = nc.vector.tensor_scalar(nq16, nq16, kmax, None, OP.mult)
        i = nc.vector.tensor_scalar(negC, nq16, -1.0, None, OP.mult)
        ctx.inc(i, "pb_nc", 1)
        ctx.wait(nc.tensor, "pb_nc")
        for hpair in range(2):
            if ctx.cnt.get("pb_hx", 0):
                ctx.wait(nc.gpsimd, "pb_hx")
            for ki, kt in enumerate(klist):
                ctx.dma(vld[:, ki, :],
                        ctx.dram["l1a_v"][kt * 128:(kt + 1) * 128,
                                          hpair * 512:hpair * 512 + 256],
                        sem="pb_v")
                ctx.dma(vld[:, ki, :][0:0 + 128, 0:0],
                        ctx.dram["l1a_v"][kt * 128:kt * 128,
                                          0:0], sem="pb_v") if False else None
            for hg4 in range(2):
                heads = [hpair * 8 + hg4 * 4 + j for j in range(4)]
                if hg4 == 1:
                    if ctx.cnt.get("pb_hx", 0):
                        ctx.wait(nc.gpsimd, "pb_hx")
                    for ki, kt in enumerate(klist):
                        ctx.dma(vld[:, ki, :],
                                ctx.dram["l1a_v"][kt * 128:(kt + 1) * 128,
                                                  hpair * 512 + 256:
                                                  hpair * 512 + 512],
                                sem="pb_v")
                ctx.wait(nc.vector, "pb_v")
                if ctx.cnt.get("pb_av", 0):
                    ctx.wait(nc.vector, "pb_av")
                for ki in range(NK):
                    dst = vext[:, ki, :].rearrange("p (h e) -> p h e", h=4)
                    src = vld[:, ki, :].rearrange("p (h e) -> p h e", h=4)
                    nc.vector.memset(vext[:, ki, :], 1.0)
                    i = nc.vector.tensor_copy(dst[:, :, 0:64], src)
                ctx.inc(i, "pb_hx", 1)
                ctx.wait(nc.tensor, "pb_hx")
                if ctx.cnt.get("pb_zc", 0):
                    ctx.wait(nc.tensor, "pb_zc")
                bSm = ctx.cnt.get("pb_sm", 0)
                bPe = ctx.cnt.get("pb_pe", 0)
                bPt = ctx.cnt.get("pb_pt", 0)
                bAv = ctx.cnt.get("pb_av", 0)
                for ki, kt in enumerate(klist):
                    mid = mmap[(qs, kt)]
                    b = ki % 2
                    if ki >= 2:
                        ctx.wait(nc.gpsimd, "pb_pt", bPt + 4 * (ki - 1))
                    elif bPt > 0:
                        ctx.wait(nc.gpsimd, "pb_pt", bPt)
                    ctx.dma(maskb[b], ctx.dram[f"c_l1m_{mid}"], sem="pb_mk")
                    for hh, h in enumerate(heads):
                        idx = ki * 4 + hh
                        sb_ = idx % 2
                        pss = ctx.ps[sb_]
                        if idx >= 2:
                            ctx.wait(nc.tensor, "pb_pe", bPe + idx - 1)
                        jt, po = h // 2, (h % 2) * 64
                        i = nc.tensor.matmul(
                            pss, k_all[po:po + 64, jt, ki * 128:(ki + 1) * 128],
                            q_all[po:po + 64, jt, :], start=True, stop=False,
                            skip_group_check=True)
                        i = nc.tensor.matmul(
                            pss, esel16[:, h * 128:(h + 1) * 128], negC,
                            start=False, stop=True, skip_group_check=True)
                        ctx.inc(i, "pb_sm", 1)
                        ctx.wait(nc.scalar, "pb_sm")
                        if idx >= 2:
                            ctx.wait(nc.scalar, "pb_av", bAv + idx - 1)
                        i = nc.scalar.activation(ptb[sb_], pss, AF.Exp)
                        ctx.inc(i, "pb_pe", 1)
                        ctx.wait(nc.vector, "pb_pe")
                        if hh == 0:
                            ctx.wait(nc.vector, "pb_mk")
                        i = nc.vector.tensor_tensor(ptb[sb_], ptb[sb_],
                                                    maskb[b], OP.mult)
                        ctx.inc(i, "pb_pt", 1)
                        ctx.wait(nc.tensor, "pb_pt")
                        i = nc.tensor.matmul(ctx.ps[4 + hh][0:65, :],
                                             vext[:, ki, hh * 65:(hh + 1) * 65],
                                             ptb[sb_], start=(ki == 0),
                                             stop=(ki == NK - 1),
                                             skip_group_check=True)
                        ctx.inc(i, "pb_av", 1)
                        if L1_SER_INNER:
                            for s_ in ["pb_sm", "pb_pe", "pb_pt", "pb_av"]:
                                ctx.wait(nc.tensor, s_)
                                ctx.wait(nc.vector, s_)
                                ctx.wait(nc.scalar, s_)
                        if ki == NK - 1:
                            ctx.inc(nc.tensor.matmul(
                                ctx.ps[4 + hh][0:1, 0:1],
                                ctx.sbuf["ones_row_bf"][0:1, 0:1],
                                ctx.sbuf["ones_row_bf"][0:1, 0:1],
                                start=False, stop=False,
                                skip_group_check=True), "pb_dum", 1) if False \
                                else None
                for hh, h in enumerate(heads):
                    psy = ctx.ps[4 + hh]
                    ctx.wait(nc.vector, "pb_av")
                    i = nc.vector.reciprocal(zrec, psy[64:65, :])
                    ctx.inc(i, "pb_zr", 1)
                    ctx.wait(nc.tensor, "pb_zr")
                    if ctx.cnt.get("pb_zc", 0):
                        ctx.wait(nc.tensor, "pb_zc")
                    i = nc.tensor.matmul(ctx.ps[hh % 2], ctx.sbuf["ones_row_f32"],
                                         zrec, start=True, stop=True,
                                         skip_group_check=True)
                    ctx.inc(i, "pb_zb", 1)
                    ctx.wait(nc.vector, "pb_zb")
                    i = nc.vector.tensor_copy(zb, ctx.ps[hh % 2])
                    ctx.inc(i, "pb_zc", 1)
                    jt, po = h // 2, (h % 2) * 64
                    i = nc.vector.tensor_tensor(y_cat[po:po + 64, jt, :],
                                                psy[0:64, :], zb[0:64, :],
                                                OP.mult)
                    ctx.inc(i, "pb_yc", 1)
                    if L1_SER_EVAC:
                        ctx.barrier()
                ctx.wait(nc.tensor, "pb_yc")
        if qs == 0:
            _l1_q0(ctx, qv, kv, bsel_sb, hmask, kmax, p0, q0s, maskb, y_cat,
                   q_all, nq16)
        for dt2 in range(ND):
            if dt2 > 0:
                ctx.wait(nc.sync, "pb_om")
            for et in range(ND):
                ctx.dma_t(woT[:, et, :],
                          w_bf["Wo"][dt2 * 128:(dt2 + 1) * 128,
                                     et * 128:(et + 1) * 128], sem="pb_w")
            ctx.wait(nc.tensor, "pb_w")
            pso = ctx.ps[2 + dt2 % 2]
            if dt2 >= 2:
                ctx.wait(nc.tensor, "pb_oa", ctx.cnt.get("pb_oa", 0) - 1)
            for et in range(ND):
                i = nc.tensor.matmul(pso, woT[:, et, :], y_cat[:, et, :],
                                     start=(et == 0), stop=(et == ND - 1))
            ctx.inc(i, "pb_om", 1)
            ctx.wait(nc.vector, "pb_om")
            nc.vector.tensor_tensor(ctx.tmp_sb[:, 0, :], pso,
                                    ctx.x[:, dt2, qsl], OP.add)
            i = nc.vector.tensor_scalar(ctx.x[:, dt2, qsl], ctx.tmp_sb[:, 0, :],
                                        bo_col[:, dt2:dt2 + 1], None, OP.add)
            ctx.inc(i, "pb_oa", 1)
        ctx.barrier()
    ctx.borrow = False
    ctx.arena_release()


def _l1_q0(ctx, qv, kv, bsel_sb, hmask, kmax, p0, q0s, maskb, y_cat, q_all,
           nq16):
    nc = ctx.nc
    T = ctx.T
    kch = q_all
    ctx.wait(nc.gpsimd, "pb_sm", ctx.cnt.get("pb_sm", 0))
    ctx.wait(nc.gpsimd, "pb_av", ctx.cnt.get("pb_av", 0))
    q0c = ctx.tmp_sb.bitcast(BF16)[:, 0, 0:ND]
    sqc = ctx.tmp_sb.bitcast(BF16)[:, 1, 0:ND]
    ctx.dma(q0c, qv[:, :, 0:1].rearrange("p n o -> p (n o)"), sem="q0l",
            allow_slow_non_contiguous=True)
    ctx.wait(nc.vector, "q0l")
    q0bd = q0s
    for dt in range(ND):
        i = nc.vector.tensor_tensor(
            q0s[:, dt // 4, (dt % 4) * 16:(dt % 4) * 16 + 16],
            bsel_sb[:, dt, :],
            q0c[:, dt:dt + 1].to_broadcast([128, 16]), OP.mult)
    i2 = nc.vector.tensor_tensor(sqc, q0c, q0c, OP.mult)
    ctx.inc(i2, "q0b", 1)
    ctx.wait(nc.tensor, "q0b")
    for dt in range(ND):
        i = nc.tensor.matmul(ctx.ps[6][0:16, 0:1], bsel_sb[:, dt, :],
                             sqc[:, dt:dt + 1], start=(dt == 0),
                             stop=(dt == ND - 1), skip_group_check=True)
    ctx.inc(i, "q0m", 1)
    ctx.wait(nc.scalar, "q0m")
    nC0 = nq16[0:16, 8:9]
    i = nc.scalar.activation(nC0, ctx.ps[6][0:16, 0:1], AF.Sqrt)
    ctx.inc(i, "q0a", 1)
    ctx.wait(nc.vector, "q0a")
    i = nc.vector.tensor_scalar(nC0, nC0, kmax, None, OP.mult)
    ctx.vsync(i)
    i = nc.vector.tensor_scalar(nC0, nC0, -1.0, None, OP.mult)
    ctx.inc(i, "q0n", 1)
    ctx.wait(nc.scalar, "q0n")
    for c in range(ctx.NC):
        sl = slice(c * TC, (c + 1) * TC)
        ctx.dma(q0s[:, 2, :], kv[:, 0:4, c * TC:(c + 1) * TC]
                [:, :, 0:128].rearrange("p n t -> p (n t)"), sem="q0k") \
            if False else None
        ctx.dma(kch, kv[:, :, sl], sem="q0k")
        ctx.wait(nc.tensor, "q0k")
        if c > 0:
            ctx.wait(nc.tensor, "q0p", c)
        for dt in range(ND):
            i = nc.tensor.matmul(ctx.ps[7][0:16, :],
                                 q0s[:, dt // 4, (dt % 4) * 16:(dt % 4) * 16 + 16],
                                 kch[:, dt, :], start=(dt == 0),
                                 stop=(dt == ND - 1), skip_group_check=True)
        ctx.inc(i, "q0sm", 1)
        ctx.wait(nc.scalar, "q0sm")
        i = nc.scalar.activation(p0[:, sl], ctx.ps[7][0:16, :], AF.Exp, bias=nC0)
        ctx.inc(i, "q0p", 1)
    ctx.wait(nc.vector, "q0p")
    z0 = nq16[0:16, 4:5]
    i = nc.vector.reduce_sum(z0, p0, axis=AX.X)
    ctx.vsync(i)
    i = nc.vector.reciprocal(z0, z0)
    ctx.inc(i, "q0z", 1)
    for tt in range(ctx.NT):
        tsl = slice(tt * 128, (tt + 1) * 128)
        if tt > 0:
            ctx.wait(nc.tensor, "q0e", tt)
            ctx.wait(nc.vector, "q0av", tt)
            ctx.wait(nc.gpsimd, "q0av", tt)
        i = nc.tensor.transpose(ctx.ps[6].bitcast(BF16)[:, 0:16], p0[:, tsl],
                                ctx.sbuf["ident_bf"][0:16, 0:16])
        ctx.inc(i, "q0t", 1)
        ctx.wait(nc.vector, "q0t")
        i = nc.vector.tensor_copy(q0s[:, 1, 0:16],
                                  ctx.ps[6].bitcast(BF16)[:, 0:16])
        ctx.inc(i, "q0e", 1)
        ctx.dma(q0s[:, 2, :], ctx.dram["l1a_v"][tsl, 0:512], sem="q0v")
        ctx.dma(q0s[:, 3, :], ctx.dram["l1a_v"][tsl, 512:1024], sem="q0v")
        ctx.wait(nc.tensor, "q0v")
        ctx.wait(nc.tensor, "q0e")
        for half in range(2):
            i = nc.tensor.matmul(ctx.ps[4 + half][0:16, :], q0s[:, 1, 0:16],
                                 q0s[:, 2 + half, :],
                                 start=(tt == 0), stop=(tt == ctx.NT - 1),
                                 skip_group_check=True)
        ctx.inc(i, "q0av", 1)
    ctx.wait(nc.vector, "q0av")
    y0f = ctx.tmp_sb[0:16, :, :]
    for half in range(2):
        i = nc.vector.tensor_tensor(y0f[:, half, :], ctx.ps[4 + half][0:16, :],
                                    hmask[:, half * 512:(half + 1) * 512],
                                    OP.mult)
        i = nc.vector.tensor_scalar(y0f[:, half, :], y0f[:, half, :], z0,
                                    None, OP.mult)
    ctx.inc(i, "q0f", 1)
    ctx.wait(nc.tensor, "q0f")
    for jt in range(ND):
        half, off = jt // 4, (jt % 4) * 128
        if jt > 0:
            ctx.wait(nc.tensor, "q0h", ctx.cnt.get("q0h", 0))
        i = nc.tensor.matmul(ctx.ps[6][:, 0:1], y0f[:, half, off:off + 128],
                             ctx.sbuf["ones_col_f32"][0:16, :],
                             start=True, stop=True, skip_group_check=True)
        ctx.inc(i, "q0g", 1)
        ctx.wait(nc.vector, "q0g")
        i = nc.vector.tensor_copy(y_cat[:, jt, 0:1], ctx.ps[6][:, 0:1])
        ctx.inc(i, "q0h", 1)
    ctx.wait(nc.tensor, "q0h")


def emit_l3_attn(ctx):
    nc = ctx.nc
    T = ctx.T
    NQS = T // 512
    PAD = 638
    qkv_w = ctx.din("L3_attn_qkv_w", [3 * D, D], F32)
    qkv_b = ctx.din("L3_attn_qkv_b", [3 * D], F32)
    proj_w = ctx.din("L3_attn_proj_w", [D, D], F32)
    proj_b = ctx.din("L3_attn_proj_b", [D], F32)
    rpe = ctx.din("L3_attn_rel_pos_emb", [2047, D], F32)
    qkv_bf = ctx.dtmp("l3a_qkv_bf", [3 * D, D], BF16)
    proj_bf = ctx.dtmp("l3a_proj_bf", [D, D], BF16)
    rpe_bf = ctx.dtmp("l3a_rpe_bf", [2047, D], BF16)
    ctx.dma(qkv_bf, qkv_w)
    ctx.dma(proj_bf, proj_w)
    ctx.dma(rpe_bf, rpe)
    bq_col = emit_feature_cols(ctx, "l3a_bq", qkv_b[0:D], D)
    bk_col = emit_feature_cols(ctx, "l3a_bk", qkv_b[D:2 * D], D)
    bo_col = emit_feature_cols(ctx, "l3a_bo", proj_b, D)
    bv_ld = ctx.sb("ld4k", [8, D], F32)
    ctx.dma(bv_ld[0:1, :], qkv_b[2 * D:3 * D].unsqueeze(0))
    bv_b = ctx.sb("l1a_bv_b", [128, D], BF16)
    bsel_sb = ctx.sb("bsel", [128, ND, 16], BF16)
    ctx.dma(bsel_sb, ctx.dram["c_bsel_bf"].rearrange("(n p) h -> p n h", p=128))
    kmax = ctx.sb("l1a_kmax", [16, 1], F32)
    extmax = ctx.sb("l3a_extmax", [16, 1], F32)
    extp2 = ctx.dtmp("l3a_extp2", [16, PAD + T], BF16)
    ctx.barrier()
    for half in range(2):
        i = nc.tensor.matmul(ctx.ps[0], ctx.sbuf["ones_row_f32"],
                             bv_ld[0:1, half * 512:(half + 1) * 512],
                             start=True, stop=True)
        ctx.inc(i, "l3b", 1)
        ctx.wait(nc.vector, "l3b")
        i = nc.vector.tensor_copy(bv_b[:, half * 512:(half + 1) * 512], ctx.ps[0])
        ctx.inc(i, "l3c", 1)
        ctx.wait(nc.tensor, "l3c")
    nc.vector.tensor_scalar(bq_col, bq_col, 0.125, None, OP.mult)
    nc.vector.memset(kmax, 0.0)

    ctx.arena_mark()
    rpet = ctx.arena([128, ND, 128], BF16)
    tabt = ctx.arena([128, 16], F32)
    exts = ctx.arena([16, PAD + T], BF16)
    nc.vector.memset(exts, 0.0)
    for ti, rt in enumerate(range(7, 16)):
        r0 = rt * 128
        nrow = min(128, 2047 - r0)
        if ti > 0:
            ctx.wait(nc.gpsimd, "e3t", ctx.cnt.get("e3t", 0))
        if nrow < 128:
            ctx.wait(nc.gpsimd, "e3v", ctx.cnt.get("e3v", 0))
            nc.gpsimd.memset(rpet, 0.0)
        ctx.dma(rpet[0:nrow, :, :],
                rpe_bf[r0:r0 + nrow, :].rearrange("r (n e) -> r n e", n=ND),
                sem="e3l")
        ctx.wait(nc.vector, "e3l")
        if ti > 0:
            ctx.wait(nc.vector, "e3t")
        i = nc.vector.reduce_sum(
            tabt.rearrange("p (h o) -> p h o", o=1),
            rpet.rearrange("p n e -> p (n e)").rearrange("p (h s) -> p h s", h=16),
            axis=AX.X)
        ctx.inc(i, "e3v", 1)
        ctx.wait(nc.tensor, "e3v")
        if ctx.cnt.get("e3c", 0):
            ctx.wait(nc.tensor, "e3c")
        i = nc.tensor.transpose(ctx.ps[6][0:16, 0:128], tabt,
                                ctx.sbuf["ident_f32"])
        ctx.inc(i, "e3t", 1)
        ctx.wait(nc.vector, "e3t")
        j0 = r0 - 1023
        lo = max(0, -j0)
        hi = min(128, 1024 - j0, T - j0)
        if hi > lo:
            i = nc.vector.tensor_copy(exts[:, PAD + j0 + lo:PAD + j0 + hi],
                                      ctx.ps[6][0:16, lo:hi])
            ctx.inc(i, "e3c", 1)
        if rt == 15 and T > 1024:
            i = nc.vector.tensor_copy(
                exts[:, PAD + 1024:PAD + T],
                ctx.ps[6][0:16, 126:127].to_broadcast([16, T - 1024]))
            ctx.inc(i, "e3c", 1)
    i = nc.vector.reduce_max(extmax, exts, axis=AX.X, apply_absolute_value=True)
    ctx.inc(i, "e3c", 1)
    ctx.wait(nc.sync, "e3c")
    ctx.store(extp2, exts)
    ctx.barrier()
    L = PAD + T
    master_d = ctx.dtmp("l3a_master", [128, 16, L], BF16)
    for kk in range(128):
        ctx.dma(master_d[kk:kk + 1, :, kk:L].squeeze(0),
                extp2[:, 0:L - kk], sem="e3m")
    ctx.barrier()
    ctx.arena_release()

    emit_norm(ctx, rms=False, eps=1e-5)

    q_dram = ctx.dtmp("l3a_q", [D, T], BF16)
    k_dram = ctx.dtmp("l3a_k", [D, T], BF16)
    v_dram = ctx.dtmp("l3a_v", [T, D], BF16)

    ctx.arena_mark()
    wT = ctx.arena([128, ND, D], BF16)
    xnc = ctx.arena([128, ND, TC], BF16)
    ev = ctx.arena([128, TC], BF16)
    sq = ctx.arena([128, TC], BF16)
    vev = ctx.arena([128, D], BF16)

    def proj_fm(row0, bias_col, scale, out_dram, track_knorm):
        for dt in range(ND):
            ctx.dma_t(wT[:, dt, :],
                      qkv_bf[row0:row0 + D, dt * 128:(dt + 1) * 128])
        ctx.barrier()
        for c in range(ctx.NC):
            sl = slice(c * TC, (c + 1) * TC)
            if c > 0:
                ctx.wait(nc.gpsimd, "pa_m")
            load_xn_chunk(ctx, xnc, c, sem="pa_x")
            ctx.wait(nc.tensor, "pa_x")
            for jt in range(ND):
                ps = ctx.ps[jt % 2]
                if jt >= 2:
                    ctx.wait(nc.tensor, "pa_e", ctx.cnt.get("pa_e", 0) - 1)
                for dt in range(ND):
                    i = nc.tensor.matmul(ps, wT[:, dt, jt * 128:(jt + 1) * 128],
                                         xnc[:, dt, :],
                                         start=(dt == 0), stop=(dt == ND - 1))
                ctx.inc(i, "pa_m", 1)
                ctx.wait(nc.vector, "pa_m")
                if ctx.cnt.get("pa_s", 0):
                    ctx.wait(nc.vector, "pa_s")
                i = nc.vector.tensor_scalar(ev, ps, scale,
                                            bias_col[:, jt:jt + 1],
                                            OP.mult, op1=OP.add)
                ctx.inc(i, "pa_e", 1)
                ctx.wait(nc.sync, "pa_e")
                ctx.store(out_dram[jt * 128:(jt + 1) * 128, sl], ev, sem="pa_s")
                if track_knorm:
                    ctx.wait(nc.scalar, "pa_e")
                    i = nc.scalar.activation(sq, ev, AF.Square)
                    ctx.inc(i, "pa_q", 1)
                    ctx.wait(nc.tensor, "pa_q")
                    i = nc.tensor.matmul(ctx.ps[4][0:16, :], bsel_sb[:, jt, :],
                                         sq, start=(jt == 0), stop=(jt == ND - 1),
                                         skip_group_check=True)
                    if jt == ND - 1:
                        ctx.inc(i, "pa_n", 1)
                        ctx.wait(nc.vector, "pa_n")
                        i = nc.vector.reduce_max(ctx.tmp_sb[0:16, 0, 0:1],
                                                 ctx.ps[4][0:16, :], axis=AX.X)
                        ctx.vsync(i)
                        i = nc.vector.tensor_tensor(kmax, kmax,
                                                    ctx.tmp_sb[0:16, 0, 0:1],
                                                    OP.max)
                        ctx.inc(i, "pa_k", 1)
                        ctx.wait(nc.tensor, "pa_k")
            ctx.barrier()

    proj_fm(0, bq_col, 0.125, q_dram, False)
    proj_fm(D, bk_col, 1.0, k_dram, True)
    for dt in range(ND):
        ctx.dma_t(wT[:, dt, :], qkv_bf[2 * D:3 * D, dt * 128:(dt + 1) * 128])
    ctx.barrier()
    for tt in range(ctx.NT):
        tsl = slice(tt * 128, (tt + 1) * 128)
        c = (tt * 128) // TC
        if tt % (TC // 128) == 0:
            if tt > 0:
                ctx.wait(nc.gpsimd, "pv_m")
            load_xn_chunk(ctx, xnc, c, sem="pa_x")
            ctx.wait(nc.tensor, "pa_x")
        toff = (tt * 128) % TC
        if ctx.cnt.get("pv_s", 0):
            ctx.wait(nc.vector, "pv_s")
        for half in range(2):
            ps = ctx.ps[half]
            for dt in range(ND):
                i = nc.tensor.matmul(
                    ps, xnc[:, dt, toff:toff + 128],
                    wT[:, dt, half * 512:(half + 1) * 512],
                    start=(dt == 0), stop=(dt == ND - 1))
            ctx.inc(i, "pv_m", 1)
            ctx.wait(nc.vector, "pv_m")
            i = nc.vector.tensor_tensor(vev[:, half * 512:(half + 1) * 512], ps,
                                        bv_b[:, half * 512:(half + 1) * 512],
                                        OP.add)
            ctx.inc(i, "pv_e", 1)
        ctx.wait(nc.sync, "pv_e")
        ctx.store(v_dram[tsl, :], vev, sem="pv_s")
        ctx.wait(nc.tensor, "pv_e")
    ctx.barrier()
    i = nc.scalar.activation(kmax, kmax, AF.Sqrt)
    ctx.inc(i, "pk_a", 1)
    ctx.barrier()
    ctx.arena_release()

    ctx.arena_mark()
    ctx.borrow = True
    q_all = ctx.arena([128, ND, 512], BF16)
    y_cat = ctx.arena([128, ND, 512], BF16)
    ktile = [ctx.arena([128, ND, 128], BF16) for _ in range(2)]
    vld = ctx.arena([128, 256], BF16)
    vext = [ctx.arena([128, 4 * 65], BF16) for _ in range(2)]
    esel16 = ctx.arena([16, 16 * 128], BF16)
    l3m = ctx.arena([128, 4, 512], BF16)
    biasb = [ctx.arena([128, 512], BF16) for _ in range(2)]
    ptb = [ctx.arena([128, 512], BF16) for _ in range(2)]
    woT = ctx.arena([128, ND, 128], BF16)
    nq16 = ctx.arena([16, 512], F32)
    negC = ctx.arena([16, 512], BF16)
    zrec = ctx.arena([1, 512], F32)
    zb = ctx.arena([128, 512], F32)
    qsq = ctx.arena([128, ND, 512], BF16)
    ctx.dma(esel16, ctx.dram["c_esel16_bf"])
    for r in range(4):
        ctx.dma(l3m[:, r, :], ctx.dram[f"c_l3m_{r}"])
    ctx.barrier()

    qv = ctx.dram["l3a_q"].rearrange("(n p) t -> p n t", p=128)
    kv = ctx.dram["l3a_k"].rearrange("(n p) t -> p n t", p=128)
    for qs in range(NQS):
        qsl = slice(qs * 512, (qs + 1) * 512)
        NK = 4 * qs + 4
        ctx.dma(q_all, qv[:, :, qsl], sem="pb_l")
        ctx.wait(nc.scalar, "pb_l")
        for dt in range(ND):
            i = nc.scalar.activation(qsq[:, dt, :], q_all[:, dt, :], AF.Square)
        ctx.inc(i, "pb_sq", 1)
        ctx.wait(nc.tensor, "pb_sq")
        if qs > 0:
            ctx.wait(nc.tensor, "pb_nc", qs)
        for dt in range(ND):
            i = nc.tensor.matmul(ctx.ps[6][0:16, :], bsel_sb[:, dt, :],
                                 qsq[:, dt, :],
                                 start=(dt == 0), stop=(dt == ND - 1))
        ctx.inc(i, "pb_nm", 1)
        ctx.wait(nc.scalar, "pb_nm")
        i = nc.scalar.activation(nq16, ctx.ps[6][0:16, :], AF.Sqrt)
        ctx.inc(i, "pb_na", 1)
        ctx.wait(nc.vector, "pb_na")
        i = nc.vector.tensor_scalar(nq16, nq16, kmax, None, OP.mult)
        i = nc.vector.tensor_scalar(nq16, nq16, extmax, None, OP.add)
        i = nc.vector.tensor_scalar(negC, nq16, -1.0, None, OP.mult)
        ctx.inc(i, "pb_nc", 1)
        ctx.wait(nc.tensor, "pb_nc")
        for hg in range(4):
            heads = [hg * 4 + j for j in range(4)]
            bSm = ctx.cnt.get("pb_sm", 0)
            bPe = ctx.cnt.get("pb_pe", 0)
            bPt = ctx.cnt.get("pb_pt", 0)
            bAv = ctx.cnt.get("pb_av", 0)
            bKt = ctx.cnt.get("pb_kl", 0)
            bVx = ctx.cnt.get("pb_hx", 0)
            for ki in range(NK):
                kb = ki % 2
                if ki >= 2:
                    ctx.wait(nc.gpsimd, "pb_sm", bSm + 4 * (ki - 1))
                elif ctx.cnt.get("pb_sm", 0) > bSm - 1 and (qs + hg) > 0:
                    ctx.wait(nc.gpsimd, "pb_sm", bSm)
                ctx.dma(ktile[kb], kv[:, :, ki * 128:(ki + 1) * 128], sem="pb_kl")
                if ki >= 2:
                    ctx.wait(nc.gpsimd, "pb_av", bAv + 4 * (ki - 1))
                elif (qs + hg) > 0:
                    ctx.wait(nc.gpsimd, "pb_av", bAv)
                if ctx.cnt.get("pb_hx", 0):
                    ctx.wait(nc.gpsimd, "pb_hx")
                ctx.dma(vld, ctx.dram["l3a_v"][ki * 128:(ki + 1) * 128,
                                               hg * 256:(hg + 1) * 256],
                        sem="pb_vl")
                ctx.wait(nc.vector, "pb_vl")
                dst = vext[kb].rearrange("p (h e) -> p h e", h=4)
                srcv = vld.rearrange("p (h e) -> p h e", h=4)
                nc.vector.memset(vext[kb], 1.0)
                i = nc.vector.tensor_copy(dst[:, :, 0:64], srcv)
                ctx.inc(i, "pb_hx", 1)
                ctx.wait(nc.tensor, "pb_hx")
                ctx.wait(nc.tensor, "pb_kl")
                for hh, h in enumerate(heads):
                    idx = ki * 4 + hh
                    sb_ = idx % 2
                    pss = ctx.ps[sb_]
                    w0 = PAD + qs * 512 - ki * 128
                    bb = biasb[idx % 2]
                    if idx >= 2:
                        ctx.wait(nc.sync, "pb_sm", bSm + idx - 1)
                    elif (qs + hg) > 0:
                        ctx.wait(nc.sync, "pb_sm", bSm)
                    src = ctx.dram["l3a_master"][:, h, w0:w0 + 512]
                    ctx.dma(bb, src, q=nc.sync, sem="pb_bl")
                    if idx >= 2:
                        ctx.wait(nc.tensor, "pb_pe", bPe + idx - 1)
                    jt, po = h // 2, (h % 2) * 64
                    i = nc.tensor.matmul(
                        pss, ktile[kb][po:po + 64, jt, :],
                        q_all[po:po + 64, jt, :], start=True, stop=False,
                        skip_group_check=True)
                    i = nc.tensor.matmul(
                        pss, esel16[:, h * 128:(h + 1) * 128], negC,
                        start=False, stop=False, skip_group_check=True)
                    ctx.wait(nc.tensor, "pb_bl")
                    i = nc.tensor.matmul(
                        pss, ctx.sbuf["ident_bf"], bb,
                        start=False, stop=True, skip_group_check=True)
                    ctx.inc(i, "pb_sm", 1)
                    ctx.wait(nc.scalar, "pb_sm")
                    if idx >= 2:
                        ctx.wait(nc.scalar, "pb_av", bAv + idx - 1)
                    i = nc.scalar.activation(ptb[sb_], pss, AF.Exp)
                    ctx.inc(i, "pb_pe", 1)
                    r = ki - 4 * qs
                    if 0 <= r <= 3:
                        ctx.wait(nc.vector, "pb_pe")
                        i = nc.vector.tensor_tensor(ptb[sb_], ptb[sb_],
                                                    l3m[:, r, :], OP.mult)
                        ctx.inc(i, "pb_pt", 1)
                        ctx.wait(nc.tensor, "pb_pt")
                    else:
                        ctx.wait(nc.tensor, "pb_pe")
                    i = nc.tensor.matmul(ctx.ps[4 + hh][0:65, :],
                                         vext[kb][:, hh * 65:(hh + 1) * 65],
                                         ptb[sb_], start=(ki == 0),
                                         stop=(ki == NK - 1),
                                         skip_group_check=True)
                    ctx.inc(i, "pb_av", 1)
            for hh, h in enumerate(heads):
                psy = ctx.ps[4 + hh]
                ctx.wait(nc.vector, "pb_av")
                i = nc.vector.reciprocal(zrec, psy[64:65, :])
                ctx.inc(i, "pb_zr", 1)
                ctx.wait(nc.tensor, "pb_zr")
                if ctx.cnt.get("pb_zc", 0):
                    ctx.wait(nc.tensor, "pb_zc")
                i = nc.tensor.matmul(ctx.ps[hh % 2], ctx.sbuf["ones_row_f32"],
                                     zrec, start=True, stop=True,
                                     skip_group_check=True)
                ctx.inc(i, "pb_zb", 1)
                ctx.wait(nc.vector, "pb_zb")
                i = nc.vector.tensor_copy(zb, ctx.ps[hh % 2])
                ctx.inc(i, "pb_zc", 1)
                jt, po = h // 2, (h % 2) * 64
                i = nc.vector.tensor_tensor(y_cat[po:po + 64, jt, :],
                                            psy[0:64, :], zb[0:64, :], OP.mult)
                ctx.inc(i, "pb_yc", 1)
            ctx.wait(nc.tensor, "pb_yc")
        for dt2 in range(ND):
            if dt2 > 0:
                ctx.wait(nc.sync, "pb_om")
            for et in range(ND):
                ctx.dma_t(woT[:, et, :],
                          proj_bf[dt2 * 128:(dt2 + 1) * 128,
                                  et * 128:(et + 1) * 128], sem="pb_w")
            ctx.wait(nc.tensor, "pb_w")
            pso = ctx.ps[2 + dt2 % 2]
            if dt2 >= 2:
                ctx.wait(nc.tensor, "pb_oa", ctx.cnt.get("pb_oa", 0) - 1)
            for et in range(ND):
                i = nc.tensor.matmul(pso, woT[:, et, :], y_cat[:, et, :],
                                     start=(et == 0), stop=(et == ND - 1))
            ctx.inc(i, "pb_om", 1)
            ctx.wait(nc.vector, "pb_om")
            nc.vector.tensor_tensor(ctx.tmp_sb[:, 0, :], pso,
                                    ctx.x[:, dt2, qsl], OP.add)
            i = nc.vector.tensor_scalar(ctx.x[:, dt2, qsl], ctx.tmp_sb[:, 0, :],
                                        bo_col[:, dt2:dt2 + 1], None, OP.add)
            ctx.inc(i, "pb_oa", 1)
        ctx.barrier()
    ctx.borrow = False
    ctx.arena_release()


def emit_l0_attn(ctx):
    nc = ctx.nc
    T = ctx.T
    NQS = T // 512
    QL = KVL = 512
    W_dq = ctx.din("L0_attn_W_dq", [QL, D], F32)
    W_uq = ctx.din("L0_attn_W_uq", [D, QL], F32)
    W_dkv = ctx.din("L0_attn_W_dkv", [KVL, D], F32)
    W_uk = ctx.din("L0_attn_W_uk", [D, KVL], F32)
    W_uv = ctx.din("L0_attn_W_uv", [D, KVL], F32)
    W_o = ctx.din("L0_attn_W_o", [D, D], F32)
    bf = {}
    for nm, ap, shp in [("dq", W_dq, [QL, D]), ("uq", W_uq, [D, QL]),
                        ("dkv", W_dkv, [KVL, D]), ("uk", W_uk, [D, KVL]),
                        ("uv", W_uv, [D, KVL]), ("o", W_o, [D, D])]:
        bf[nm] = ctx.dtmp(f"l0a_{nm}_bf", shp, BF16)
        ctx.dma(bf[nm], {"dq": W_dq, "uq": W_uq, "dkv": W_dkv, "uk": W_uk,
                         "uv": W_uv, "o": W_o}[nm])
    a_dram = ctx.dtmp("l0a_A", [D, D], BF16)
    at_dram = ctx.dtmp("l0a_AT", [D, D], BF16)
    mk_dram = ctx.dtmp("l0a_MK", [D, KVL], BF16)
    mv_dram = ctx.dtmp("l0a_MV", [KVL, D], BF16)
    ctx.barrier()

    ctx.arena_mark()
    wa = ctx.arena([128, 4, D], BF16)
    wb = ctx.arena([128, ND, D], BF16)
    ev = ctx.arena([128, TC], BF16)

    def mm_big(out_dram, M, N, KT, lhsT_tiles, rhs_tiles, scale=1.0):
        it = 0
        for mt in range(M // 128):
            for nt0 in range(0, N, 512):
                nw = min(512, N - nt0)
                ps = ctx.ps[it % 2]
                it += 1
                if ctx.cnt.get("pw_e", 0) >= 2:
                    ctx.wait(nc.tensor, "pw_e", ctx.cnt["pw_e"] - 1)
                for kt in range(KT):
                    i = nc.tensor.matmul(ps[:, 0:nw], lhsT_tiles(kt, mt),
                                         rhs_tiles(kt, nt0, nw),
                                         start=(kt == 0), stop=(kt == KT - 1))
                ctx.inc(i, "pw_m", 1)
                ctx.wait(nc.vector, "pw_m")
                if ctx.cnt.get("pw_s", 0):
                    ctx.wait(nc.vector, "pw_s")
                i = nc.vector.tensor_scalar(ev[:, 0:nw], ps[:, 0:nw], scale,
                                            None, OP.mult)
                ctx.inc(i, "pw_e", 1)
                ctx.wait(nc.sync, "pw_e")
                ctx.store(out_dram[mt * 128:(mt + 1) * 128, nt0:nt0 + nw],
                          ev[:, 0:nw], sem="pw_s")

    for rt in range(4):
        ctx.dma_t(wa[:, rt, :], bf["uq"][:, rt * 128:(rt + 1) * 128])
    ctx.dma(wb[:, 0:4, :], bf["dq"].rearrange("(a p) c -> p a c", p=128))
    ctx.barrier()
    mm_big(a_dram, D, D, 4,
           lambda kt, mt: wb[:, kt, mt * 128:(mt + 1) * 128],
           lambda kt, n0, nw: wa[:, kt, n0:n0 + nw])
    mm_big(at_dram, D, D, 4,
           lambda kt, mt: wa[:, kt, mt * 128:(mt + 1) * 128],
           lambda kt, n0, nw: wb[:, kt, n0:n0 + nw])
    ctx.barrier()
    ctx.dma(wa, ctx.dram["l0a_AT"].rearrange("(a p) c -> p a c", p=128)[:, 0:4, :],
            sem="pw_l")
    ctx.dma(wa2 := ctx.arena([128, 4, D], BF16),
            ctx.dram["l0a_AT"].rearrange("(a p) c -> p a c", p=128)[:, 4:8, :],
            sem="pw_l")
    ctx.dma(wb[:, 0:ND, 0:KVL], bf["uk"].rearrange("(a p) l -> p a l", p=128),
            sem="pw_l")
    ctx.barrier()
    mm_big(mk_dram, D, KVL, 8,
           lambda kt, mt: (wa if kt < 4 else wa2)[:, kt % 4, mt * 128:(mt + 1) * 128],
           lambda kt, n0, nw: wb[:, kt, n0:n0 + nw], scale=0.125)
    ctx.barrier()
    ctx.dma(wa[:, 0:4, 0:KVL],
            bf["uv"].rearrange("(a p) l -> p a l", p=128)[:, 0:4, :], sem="pw_l")
    ctx.dma(wa2[:, 0:4, 0:KVL], bf["uv"].rearrange("(a p) l -> p a l", p=128)
            [:, 4:8, :], sem="pw_l")
    for ct in range(ND):
        ctx.dma_t(wb[:, ct, :], bf["o"][:, ct * 128:(ct + 1) * 128])
    ctx.barrier()
    mm_big(mv_dram, KVL, D, 8,
           lambda kt, mt: (wa if kt < 4 else wa2)[:, kt % 4, mt * 128:(mt + 1) * 128],
           lambda kt, n0, nw: wb[:, kt, n0:n0 + nw])
    ctx.barrier()
    ctx.arena_release()
    if L0_PHASE < 1:
        return

    emit_norm(ctx, rms=False, eps=1e-5)

    q_dram = ctx.dtmp("l0a_q", [D, T], BF16)
    ckv_fm_dram = ctx.dtmp("l0a_ckvfm", [KVL, T], BF16)

    prev_borrow = ctx.borrow
    ctx.borrow = True
    ctx.arena_mark()
    ckv_tm = ctx.arena([128, ctx.NT, KVL], BF16)
    kn2 = ctx.arena([128, 2], F32)
    ctx.arena_mark()
    wT = ctx.arena([128, ND, D], BF16)
    xnc = ctx.arena([128, ND, TC], BF16)
    ev1 = ctx.arena([128, TC], BF16)
    ev2 = ctx.arena([128, TC], BF16)
    kn = ctx.arena([128, 1], F32)
    nc.vector.memset(kn, 0.0)
    for dt in range(ND):
        ctx.dma_t(wT[:, dt, 0:KVL], bf["dkv"][:, dt * 128:(dt + 1) * 128])
    ctx.barrier()
    for c in range(ctx.NC):
        sl = slice(c * TC, (c + 1) * TC)
        if c > 0:
            ctx.wait(nc.gpsimd, "p1_m")
        load_xn_chunk(ctx, xnc, c, sem="p1_x")
        ctx.wait(nc.tensor, "p1_x")
        for tt in range(TC // 128):
            gt = c * (TC // 128) + tt
            ps = ctx.ps[tt % 2]
            if ctx.cnt.get("p1_ck", 0) >= 2:
                ctx.wait(nc.tensor, "p1_ck", ctx.cnt["p1_ck"] - 1)
                ctx.wait(nc.tensor, "p1_q", ctx.cnt["p1_q"] - 1)
            for dt in range(ND):
                i = nc.tensor.matmul(ps, xnc[:, dt, tt * 128:(tt + 1) * 128],
                                     wT[:, dt, 0:KVL],
                                     start=(dt == 0), stop=(dt == ND - 1))
            ctx.inc(i, "p1_m", 1)
            ctx.wait(nc.vector, "p1_m")
            i = nc.vector.tensor_copy(ckv_tm[:, gt, :], ps)
            ctx.inc(i, "p1_ck", 1)
            ctx.wait(nc.scalar, "p1_ck")
            i = nc.scalar.activation(ev2, ps, AF.Square, accum_out=kn2[:, 0:1])
            ctx.inc(i, "p1_q", 1)
            ctx.wait(nc.vector, "p1_q")
            i = nc.vector.tensor_tensor(kn, kn, kn2[:, 0:1], OP.max)
            ctx.inc(i, "p1_n", 1)
            ctx.wait(nc.scalar, "p1_n")
        for lt in range(4):
            ps = ctx.ps[4 + lt % 2]
            if ctx.cnt.get("p1_cf", 0) >= 2:
                ctx.wait(nc.tensor, "p1_cf", ctx.cnt["p1_cf"] - 1)
            for dt in range(ND):
                i = nc.tensor.matmul(ps, wT[:, dt, lt * 128:(lt + 1) * 128],
                                     xnc[:, dt, :],
                                     start=(dt == 0), stop=(dt == ND - 1))
            ctx.inc(i, "p1_fm", 1)
            ctx.wait(nc.vector, "p1_fm")
            if ctx.cnt.get("p1_fs", 0):
                ctx.wait(nc.vector, "p1_fs")
            i = nc.vector.tensor_copy(ev1, ps)
            ctx.inc(i, "p1_cf", 1)
            ctx.wait(nc.sync, "p1_cf")
            ctx.store(ckv_fm_dram[lt * 128:(lt + 1) * 128, sl], ev1, sem="p1_fs")
    ctx.barrier()
    i = nc.tensor.transpose(ctx.ps[6][0:1, 0:128], kn,
                            ctx.sbuf["ident_f32"])
    ctx.inc(i, "p1_t", 1)
    ctx.wait(nc.vector, "p1_t")
    i = nc.vector.reduce_max(kn2[0:1, 0:1], ctx.ps[6][0:1, 0:128], axis=AX.X)
    ctx.vsync(i)
    i = nc.vector.tensor_scalar(kn2[0:1, 0:1], kn2[0:1, 0:1], 1e-12, None, OP.add)
    ctx.inc(i, "p1_n2", 1)
    ctx.wait(nc.scalar, "p1_n2")
    i = nc.scalar.activation(kn2[0:1, 1:2], kn2[0:1, 0:1], AF.Sqrt)
    ctx.inc(i, "p1_s", 1)
    ctx.barrier()
    for c in range(ctx.NC):
        sl = slice(c * TC, (c + 1) * TC)
        if c > 0:
            ctx.wait(nc.gpsimd, "p2_m")
        load_xn_chunk(ctx, xnc, c, sem="p1_x")
        ctx.wait(nc.tensor, "p1_x")
        if c == 0:
            ctx.dma(wT, ctx.dram["l0a_A"].rearrange("(a p) j -> p a j", p=128),
                    sem="p2_w")
            ctx.wait(nc.tensor, "p2_w")
        for jt in range(ND):
            ps = ctx.ps[jt % 2]
            if ctx.cnt.get("p2_e", 0) >= 2:
                ctx.wait(nc.tensor, "p2_e", ctx.cnt["p2_e"] - 1)
            for dt in range(ND):
                i = nc.tensor.matmul(ps, wT[:, dt, jt * 128:(jt + 1) * 128],
                                     xnc[:, dt, :],
                                     start=(dt == 0), stop=(dt == ND - 1))
            ctx.inc(i, "p2_m", 1)
            ctx.wait(nc.vector, "p2_m")
            if ctx.cnt.get("p2_s", 0):
                ctx.wait(nc.vector, "p2_s")
            i = nc.vector.tensor_copy(ev1, ps)
            ctx.inc(i, "p2_e", 1)
            ctx.wait(nc.sync, "p2_e")
            ctx.store(q_dram[jt * 128:(jt + 1) * 128, sl], ev1, sem="p2_s")
    ctx.barrier()

    ctx.arena_release()
    if L0_PHASE < 3:
        ctx.borrow = prev_borrow
        ctx.arena_release()
        return

    ctx.arena_mark()
    qlT = ctx.arena([128, 4, 512], BF16)
    qh = ctx.arena([64, 512], BF16)
    mkh = ctx.arena([64, KVL], BF16)
    mvh = ctx.arena([128, 4, 64], BF16)
    fmt = [ctx.arena([128, 4, 128], BF16) for _ in range(2)]
    ptb = [ctx.arena([128, 512], BF16) for _ in range(2)]
    sqv = ctx.arena([128, 4, 512], BF16)
    ylt = ctx.arena([128, 4, 512], BF16)
    l3m = ctx.arena([128, 4, 512], BF16)
    negC = ctx.arena([1, 512], F32)
    zrec = ctx.arena([1, 512], F32)
    zb = ctx.arena([128, 512], F32)
    for r in range(4):
        ctx.dma(l3m[:, r, :], ctx.dram[f"c_l3m_{r}"])
    ctx.barrier()
    for qs in range(NQS):
        qsl = slice(qs * 512, (qs + 1) * 512)
        NK = 4 * qs + 4
        for h in range(16):
            ctx.wait(nc.gpsimd, "b_ql", ctx.cnt.get("b_ql", 0))
            if ctx.cnt.get("b_sm", 0):
                ctx.wait(nc.gpsimd, "b_sm")
            ctx.dma(qh, ctx.dram["l0a_q"][h * 64:(h + 1) * 64, qsl], sem="b_ql")
            ctx.dma(mkh, ctx.dram["l0a_MK"][h * 64:(h + 1) * 64, :], sem="b_ql")
            if ctx.cnt.get("b_om", 0):
                ctx.wait(nc.gpsimd, "b_om")
            ctx.dma(mvh, ctx.dram["l0a_MV"].rearrange("(a p) e -> p a e", p=128)
                    [:, :, h * 64:(h + 1) * 64], sem="b_ql")
            ctx.wait(nc.tensor, "b_ql")
            if ctx.cnt.get("b_ye", 0):
                ctx.wait(nc.tensor, "b_ye")
            if ctx.cnt.get("b_zc", 0):
                ctx.wait(nc.tensor, "b_zc")
            for lt in range(4):
                i = nc.tensor.matmul(ctx.ps[4 + lt], mkh[:, lt * 128:(lt + 1) * 128],
                                     qh, start=True, stop=True,
                                     skip_group_check=True)
            ctx.inc(i, "b_qm", 1)
            ctx.wait(nc.vector, "b_qm")
            if ctx.cnt.get("b_av", 0):
                ctx.wait(nc.vector, "b_av")
            for lt in range(4):
                i = nc.vector.tensor_copy(qlT[:, lt, :], ctx.ps[4 + lt])
            ctx.inc(i, "b_qe", 1)
            ctx.wait(nc.scalar, "b_qe")
            for lt in range(4):
                i = nc.scalar.activation(sqv[:, lt, :], qlT[:, lt, :], AF.Square)
            ctx.inc(i, "b_sq", 1)
            ctx.wait(nc.tensor, "b_sq")
            for lt in range(4):
                i = nc.tensor.matmul(ctx.ps[6][0:1, :], ctx.sbuf["ones_col_bf"],
                                     sqv[:, lt, :], start=(lt == 0),
                                     stop=(lt == 3), skip_group_check=True)
            ctx.inc(i, "b_nm", 1)
            ctx.wait(nc.scalar, "b_nm")
            i = nc.scalar.activation(negC, ctx.ps[6][0:1, :], AF.Sqrt)
            ctx.inc(i, "b_na", 1)
            ctx.wait(nc.vector, "b_na")
            i = nc.vector.tensor_scalar(negC, negC, kn2[0:1, 1:2], None, OP.mult)
            i = nc.vector.tensor_scalar(negC, negC, -1.0, None, OP.mult)
            nb = negC_bf(ctx, negC)
            ctx.inc(i, "b_nc", 1)
            ctx.wait(nc.tensor, "b_nc")
            bSm = ctx.cnt.get("b_sm", 0)
            bPe = ctx.cnt.get("b_pe", 0)
            bPt = ctx.cnt.get("b_pt", 0)
            bAv = ctx.cnt.get("b_av", 0)
            for ki in range(NK):
                kb = ki % 2
                if ki >= 2:
                    ctx.wait(nc.sync, "b_sm", bSm + ki - 1)
                elif ctx.cnt.get("b_sm", 0):
                    ctx.wait(nc.sync, "b_sm", bSm)
                ctx.dma(fmt[kb][:, 0:4, :],
                        ctx.dram["l0a_ckvfm"][0:512, ki * 128:(ki + 1) * 128]
                        .rearrange("(l p) t -> p l t", p=128),
                        q=nc.sync, sem="b_fl")
                sb_ = ki % 2
                pss = ctx.ps[sb_]
                if ki >= 2:
                    ctx.wait(nc.tensor, "b_pe", bPe + ki - 1)
                ctx.wait(nc.tensor, "b_fl")
                for lt in range(4):
                    i = nc.tensor.matmul(pss, fmt[kb][:, lt, :], qlT[:, lt, :],
                                         start=(lt == 0), stop=False,
                                         skip_group_check=True)
                i = nc.tensor.matmul(pss, ctx.sbuf["ones_row_bf"], nb,
                                     start=False, stop=True,
                                     skip_group_check=True)
                ctx.inc(i, "b_sm", 1)
                ctx.wait(nc.scalar, "b_sm")
                if ki >= 2:
                    ctx.wait(nc.scalar, "b_av", bAv + ki - 1)
                i = nc.scalar.activation(ptb[sb_], pss, AF.Exp)
                ctx.inc(i, "b_pe", 1)
                r = ki - 4 * qs
                if 0 <= r <= 3:
                    ctx.wait(nc.vector, "b_pe")
                    i = nc.vector.tensor_tensor(ptb[sb_], ptb[sb_], l3m[:, r, :],
                                                OP.mult)
                    ctx.inc(i, "b_pt", 1)
                    ctx.wait(nc.tensor, "b_pt")
                else:
                    ctx.wait(nc.tensor, "b_pe")
                for lt in range(4):
                    i = nc.tensor.matmul(ctx.ps[2 + lt],
                                         ckv_tm[:, ki, lt * 128:(lt + 1) * 128],
                                         ptb[sb_], start=(ki == 0),
                                         stop=(ki == NK - 1),
                                         skip_group_check=True)
                i = nc.tensor.matmul(ctx.ps[7][0:1, :], ctx.sbuf["ones_col_bf"],
                                     ptb[sb_], start=(ki == 0),
                                     stop=(ki == NK - 1), skip_group_check=True)
                ctx.inc(i, "b_av", 1)
            ctx.wait(nc.vector, "b_av")
            i = nc.vector.reciprocal(zrec, ctx.ps[7][0:1, :])
            ctx.inc(i, "b_zr", 1)
            ctx.wait(nc.tensor, "b_zr")
            i = nc.tensor.matmul(ctx.ps[6], ctx.sbuf["ones_row_f32"], zrec,
                                 start=True, stop=True, skip_group_check=True)
            ctx.inc(i, "b_zb", 1)
            ctx.wait(nc.vector, "b_zb")
            i = nc.vector.tensor_copy(zb, ctx.ps[6])
            ctx.inc(i, "b_zc", 1)
            for lt in range(4):
                i = nc.vector.tensor_tensor(ylt[:, lt, :], ctx.ps[2 + lt], zb,
                                            OP.mult)
            ctx.inc(i, "b_ye", 1)
            ctx.wait(nc.tensor, "b_ye")
            jt, po = h // 2, (h % 2) * 64
            pso = ctx.ps[sb_]
            for lt in range(4):
                i = nc.tensor.matmul(pso[0:64, :], mvh[:, lt, :], ylt[:, lt, :],
                                     start=(lt == 0), stop=(lt == 3),
                                     skip_group_check=True)
            ctx.inc(i, "b_om", 1)
            ctx.wait(nc.vector, "b_om")
            i = nc.vector.tensor_tensor(ctx.x[po:po + 64, jt, qsl],
                                        ctx.x[po:po + 64, jt, qsl],
                                        pso[0:64, :], OP.add)
            ctx.inc(i, "b_xa", 1)
            ctx.wait(nc.tensor, "b_xa")
        ctx.barrier()
    ctx.borrow = prev_borrow
    ctx.arena_release()
    ctx.arena_release()


def negC_bf(ctx, negC_f32):
    nc = ctx.nc
    if not hasattr(ctx, "_negc_bf"):
        ctx._negc_bf = ctx.sb("ld4k", [8, D], F32).bitcast(BF16)[0:1, 1536:2048]
    i = nc.vector.tensor_copy(ctx._negc_bf, negC_f32)
    ctx.inc(i, "b_ncb", 1)
    ctx.wait(ctx.nc.tensor, "b_ncb")
    return ctx._negc_bf


def emit_l2_attn(ctx):
    nc = ctx.nc
    T = ctx.T
    NH, DK, DV, CH = 4, 128, 256, 64
    NG = T // CH
    scale = DK ** -0.5
    ins = {}
    for nm, shp in [("attn_norm_w", [D]), ("Wq", [D, 512]), ("Wk", [D, 512]),
                    ("Wv", [D, D]), ("Wg", [D, D]), ("Wgk1", [D, 16]),
                    ("Wgk2", [16, 512]), ("bgk2", [512]), ("gnorm_w", [256]),
                    ("Wo", [D, D]), ("mlp_norm_w", [D]),
                    ("W_gate", [D, 2816]), ("W_up", [D, 2816]),
                    ("W_down", [2816, D])]:
        ins[nm] = ctx.din(f"L2_attn_{nm}", shp, F32)
    bfw = {}
    for nm in ["Wq", "Wk", "Wv", "Wg", "Wgk1", "Wgk2", "Wo", "W_gate",
               "W_up", "W_down"]:
        bfw[nm] = ctx.dtmp(f"l2a_{nm}_bf", list(ins[nm].shape), BF16)
        ctx.dma(bfw[nm], ins[nm])
    anw_col = emit_feature_cols(ctx, "l2a_anw", ins["attn_norm_w"], D)
    mnw_col = emit_feature_cols(ctx, "l2a_mnw", ins["mlp_norm_w"], D)
    ctx.barrier()

    h1_dram = ctx.dtmp("l2a_h1", [ctx.NC, 128, ND * TC], BF16)
    u2_dram = ctx.dtmp("l2a_u2", [ctx.NC, 128, ND * TC], BF16)
    n2_dram = ctx.dtmp("l2a_n2", [ctx.NC, 128, ND * TC], BF16)
    o2_dram = ctx.dtmp("l2a_o2", [T, D], BF16)

    emit_norm(ctx, rms=False, eps=1e-5, residual_add_xn=True)
    emit_norm(ctx, rms=True, eps=1e-6, gain_col=anw_col,
              src_dram=ctx.xn_dram, out_dram=h1_dram)

    ctx.arena_mark()
    prev_borrow = ctx.borrow
    ctx.borrow = True
    xnc = ctx.arena([128, ND, TC], BF16)
    wq = ctx.arena([128, ND, 128], BF16)
    wk = ctx.arena([128, ND, 128], BF16)
    wv = ctx.arena([128, ND, 256], BF16)
    wg = ctx.arena([128, ND, 256], BF16)
    wgk1 = ctx.arena([128, ND, 16], BF16)
    wgk2h = ctx.arena([16, 128], BF16)
    z16 = ctx.arena([16, T], BF16)
    glog = ctx.arena([128, T // 128, 128], BF16)
    u64 = ctx.arena([64, 64], BF16)
    u64x = ctx.arena([128, 64], BF16)
    expb = ctx.arena([128, CH], BF16)
    expnb = ctx.arena([128, CH], BF16)
    qe = ctx.arena([128, CH], BF16)
    kd1 = ctx.arena([128, CH], BF16)
    kd2 = ctx.arena([128, CH], BF16)
    kd2t = ctx.arena([64, 128], BF16)
    atm = ctx.arena([64, 64], BF16)
    vsb = ctx.arena([64, 256], BF16)
    ggs = ctx.arena([64, 256], F32)
    gg = ctx.arena([64, 256], BF16)
    o2sb = ctx.arena([64, 256], BF16)
    S32 = ctx.arena([128, 256], F32)
    Sbf = ctx.arena([128, 256], BF16)
    gnwb = ctx.arena([64, 256], F32)
    bgk2b = ctx.arena([128, 128], F32)
    gdum = ctx.arena([64, 256], BF16)
    ctx.dma(u64, ctx.dram["c_u64_bf"])
    ctx.dma(u64x, ctx.dram["c_u64x_bf"])
    ld4k = ctx.sb("ld4k", [8, D], F32)
    gn_ld = ld4k[0:1, 0:256]
    ctx.dma(gn_ld, ins["gnorm_w"].unsqueeze(0))
    ctx.dma(ld4k[0:1, 256:768], ins["bgk2"].unsqueeze(0))
    ctx.barrier()
    i = nc.tensor.matmul(ctx.ps[0][0:64, 0:256],
                         ctx.sbuf["ones_row_f32"][0:1, 0:64],
                         gn_ld, start=True, stop=True)
    ctx.inc(i, "g_bc", 1)
    ctx.wait(nc.vector, "g_bc")
    i = nc.vector.tensor_copy(gnwb, ctx.ps[0][0:64, 0:256])
    ctx.inc(i, "g_bv", 1)
    ctx.wait(nc.tensor, "g_bv")

    for h in range(NH):
        ctx.barrier()
        for dt in range(ND):
            ctx.dma(wq[:, dt, :], bfw["Wq"][dt * 128:(dt + 1) * 128,
                                            h * 128:(h + 1) * 128])
            ctx.dma(wk[:, dt, :], bfw["Wk"][dt * 128:(dt + 1) * 128,
                                            h * 128:(h + 1) * 128])
            ctx.dma(wv[:, dt, :], bfw["Wv"][dt * 128:(dt + 1) * 128,
                                            h * 256:(h + 1) * 256])
            ctx.dma(wg[:, dt, :], bfw["Wg"][dt * 128:(dt + 1) * 128,
                                            h * 256:(h + 1) * 256])
            ctx.dma(wgk1[:, dt, :], bfw["Wgk1"][dt * 128:(dt + 1) * 128, :])
        ctx.dma(wgk2h, bfw["Wgk2"][:, h * 128:(h + 1) * 128])
        ctx.barrier()
        i = nc.tensor.matmul(ctx.ps[0][:, 0:128], ctx.sbuf["ones_row_f32"],
                             ld4k[0:1, 256 + h * 128:256 + (h + 1) * 128],
                             start=True, stop=True)
        ctx.inc(i, "g_bc", 1)
        ctx.wait(nc.vector, "g_bc")
        i = nc.vector.tensor_copy(bgk2b, ctx.ps[0][:, 0:128])
        ctx.inc(i, "g_bv", 1)
        ctx.wait(nc.tensor, "g_bv")
        for c in range(ctx.NC):
            sl = slice(c * TC, (c + 1) * TC)
            if c > 0:
                ctx.wait(nc.gpsimd, "g_zm")
            load_xn_chunk(ctx, xnc, c, sem="g_xl", src_dram=h1_dram)
            ctx.wait(nc.tensor, "g_xl")
            ps = ctx.ps[c % 2]
            if c >= 2:
                ctx.wait(nc.tensor, "g_ze", ctx.cnt["g_ze"] - 1)
            for dt in range(ND):
                i = nc.tensor.matmul(ps[0:16, :], wgk1[:, dt, :], xnc[:, dt, :],
                                     start=(dt == 0), stop=(dt == ND - 1))
            ctx.inc(i, "g_zm", 1)
            ctx.wait(nc.vector, "g_zm")
            i = nc.vector.tensor_copy(z16[:, sl], ps[0:16, :])
            ctx.inc(i, "g_ze", 1)
        ctx.barrier()
        for tt in range(T // 128):
            tsl = slice(tt * 128, (tt + 1) * 128)
            ps = ctx.ps[tt % 2]
            if tt >= 2:
                ctx.wait(nc.tensor, "g_gv", ctx.cnt["g_gv"] - 1)
            i = nc.tensor.matmul(ps[:, 0:128], z16[:, tsl], wgk2h,
                                 start=True, stop=True)
            ctx.inc(i, "g_gm", 1)
            ctx.wait(nc.vector, "g_gm")
            t1 = ctx.tmp_sb[:, 0, 0:128]
            t2 = ctx.tmp_sb[:, 1, 0:128]
            nc.vector.tensor_tensor(t1, ps[:, 0:128], bgk2b, OP.add)
            i = nc.vector.tensor_scalar(t2, t1, 0.0, None, OP.min)
            ctx.inc(i, "g_gv1", 1)
            ctx.wait(nc.scalar, "g_gv1")
            a1 = ctx.tmp_sb.bitcast(BF16)[:, 0, 512:640]
            a2 = ctx.tmp_sb.bitcast(BF16)[:, 0, 640:768]
            i = nc.scalar.activation(a1, t1, AF.Abs)
            ctx.vsync(i, eng=nc.scalar)
            i = nc.scalar.activation(a2, a1, AF.Exp, scale=-1.0)
            ctx.vsync(i, eng=nc.scalar)
            i = nc.scalar.activation(a1, a2, AF.Ln, bias=1.0)
            ctx.inc(i, "g_ga", 1)
            ctx.wait(nc.vector, "g_ga")
            nc.vector.tensor_tensor(t1, t2, a1, OP.subtract)
            i = nc.vector.tensor_scalar(glog[:, tt, :], t1, 1.0 / 16.0,
                                        None, OP.mult)
            ctx.inc(i, "g_gv", 1)
        ctx.barrier()
        nc.vector.memset(S32, 0.0)
        i = nc.vector.memset(Sbf, 0.0)
        ctx.inc(i, "g_sv", 1)
        ctx.wait(nc.tensor, "g_sv")
        for g in range(NG):
            c = (g * CH) // TC
            if g * CH % TC == 0:
                load_xn_chunk(ctx, xnc, c, sem="g_xl", src_dram=h1_dram)
                ctx.wait(nc.tensor, "g_xl")
            toff = (g * CH) % TC
            gsl = slice(toff, toff + CH)
            grow = slice((g % 2) * 64, (g % 2) * 64 + 64)
            i = nc.tensor.matmul(ctx.ps[0][:, 0:CH], glog[grow, g // 2, :],
                                 u64x[grow, :],
                                 start=True, stop=True, skip_group_check=True)
            ctx.inc(i, "s_bm", 1)
            ctx.wait(nc.scalar, "s_bm")
            i = nc.scalar.activation(expb, ctx.ps[0][:, 0:CH], AF.Exp)
            i = nc.scalar.activation(expnb, ctx.ps[0][:, 0:CH], AF.Exp, scale=-1.0)
            eblf = ctx.tmp_sb[:, 0, 300:301]
            i = nc.scalar.activation(eblf, ctx.ps[0][:, CH - 1:CH], AF.Exp)
            ctx.inc(i, "s_ea", 1)
            for dt in range(ND):
                i = nc.tensor.matmul(ctx.ps[1][:, 0:CH], wq[:, dt, :],
                                     xnc[:, dt, gsl], start=(dt == 0),
                                     stop=(dt == ND - 1), skip_group_check=True)
            ctx.inc(i, "s_qm", 1)
            for dt in range(ND):
                i = nc.tensor.matmul(ctx.ps[2][:, 0:CH], wk[:, dt, :],
                                     xnc[:, dt, gsl], start=(dt == 0),
                                     stop=(dt == ND - 1), skip_group_check=True)
            ctx.inc(i, "s_km", 1)
            for dt in range(ND):
                i = nc.tensor.matmul(ctx.ps[3][0:CH, 0:DV], xnc[:, dt, gsl],
                                     wv[:, dt, :], start=(dt == 0),
                                     stop=(dt == ND - 1), skip_group_check=True)
            ctx.inc(i, "s_vm", 1)
            for dt in range(ND):
                i = nc.tensor.matmul(ctx.ps[5][0:CH, 0:DV], xnc[:, dt, gsl],
                                     wg[:, dt, :], start=(dt == 0),
                                     stop=(dt == ND - 1), skip_group_check=True)
            ctx.inc(i, "s_gm", 1)
            ctx.wait(nc.vector, "s_ea")
            ctx.wait(nc.vector, "s_qm")
            i = nc.vector.scalar_tensor_tensor(qe, ctx.ps[1][:, 0:CH], scale,
                                               expb, OP.mult, OP.mult)
            ctx.inc(i, "s_qe", 1)
            ctx.wait(nc.vector, "s_km")
            i = nc.vector.tensor_tensor(kd1, ctx.ps[2][:, 0:CH], expnb, OP.mult)
            ctx.inc(i, "s_kd", 1)
            ctx.wait(nc.vector, "s_vm")
            i = nc.vector.tensor_copy(vsb, ctx.ps[3][0:CH, 0:DV])
            ctx.inc(i, "s_ve", 1)
            ctx.wait(nc.tensor, "s_qe")
            ctx.wait(nc.tensor, "s_kd")
            ctx.wait(nc.tensor, "s_ea")
            i = nc.tensor.matmul(ctx.ps[0][0:CH, 0:CH], kd1, qe,
                                 start=True, stop=True, skip_group_check=True)
            ctx.inc(i, "s_am", 1)
            ctx.wait(nc.vector, "s_am")
            i = nc.vector.tensor_tensor(atm, ctx.ps[0][0:CH, 0:CH], u64, OP.mult)
            ctx.inc(i, "s_ae", 1)
            ctx.wait(nc.tensor, "s_ae")
            ctx.wait(nc.tensor, "s_ve")
            i = nc.tensor.matmul(ctx.ps[4][0:CH, 0:DV], atm, vsb,
                                 start=True, stop=False, skip_group_check=True)
            i = nc.tensor.matmul(ctx.ps[4][0:CH, 0:DV], qe, Sbf,
                                 start=False, stop=True, skip_group_check=True)
            ctx.inc(i, "s_om", 1)
            ctx.wait(nc.vector, "s_om") if False else None
            i = nc.vector.tensor_scalar(kd2, kd1, eblf, None, OP.mult)
            ctx.inc(i, "s_k2", 1)
            ctx.wait(nc.tensor, "s_k2")
            i = nc.tensor.transpose(ctx.ps[6].bitcast(BF16)[0:CH, 0:128], kd2,
                                    ctx.sbuf["ident_bf"])
            ctx.inc(i, "s_kt", 1)
            ctx.wait(nc.vector, "s_kt")
            i = nc.vector.tensor_copy(kd2t, ctx.ps[6].bitcast(BF16)[0:CH, 0:128])
            ctx.inc(i, "s_k2e", 1)
            ctx.wait(nc.scalar, "s_om")
            osum = ctx.tmp_sb[0:CH, 0, 200:201]
            i = nc.scalar.activation(gdum, ctx.ps[4][0:CH, 0:DV], AF.Square,
                                     accum_out=osum)
            ctx.inc(i, "s_osq", 1)
            ctx.wait(nc.vector, "s_osq")
            i = nc.vector.tensor_scalar(osum, osum, 1.0 / DV, None, OP.mult)
            ctx.vsync(i)
            i = nc.vector.tensor_scalar(osum, osum, 1e-6, None, OP.add)
            ctx.vsync(i)
            orr = ctx.tmp_sb[0:CH, 0, 202:203]
            i = nc.vector.reciprocal(orr, osum)
            ctx.inc(i, "s_or", 1)
            ctx.wait(nc.scalar, "s_or")
            i = nc.scalar.activation(orr, orr, AF.Sqrt)
            ctx.inc(i, "s_os", 1)
            ctx.wait(nc.scalar, "s_gm")
            sigs = ctx.tmp_sb[0:CH, 1, 0:256]
            i = nc.scalar.activation(sigs, ctx.ps[5][0:CH, 0:DV], AF.Sigmoid)
            ctx.inc(i, "s_sg", 1)
            ctx.wait(nc.vector, "s_sg")
            i = nc.vector.tensor_tensor(gg, ctx.ps[5][0:CH, 0:DV], sigs, OP.mult)
            ctx.inc(i, "s_gg", 1)
            ctx.wait(nc.vector, "s_os")
            i = nc.vector.scalar_tensor_tensor(ggs, ctx.ps[4][0:CH, 0:DV], orr,
                                               gnwb, OP.mult, OP.mult)
            i = nc.vector.tensor_tensor(o2sb, ggs, gg, OP.mult)
            ctx.inc(i, "s_o2", 1)
            ctx.wait(nc.sync, "s_o2")
            ctx.store(o2_dram[g * CH:(g + 1) * CH, h * 256:(h + 1) * 256],
                      o2sb, sem="s_ost")
            ctx.wait(nc.tensor, "s_k2e")
            i = nc.tensor.matmul(ctx.ps[6][:, 0:256], kd2t, vsb,
                                 start=True, stop=True, skip_group_check=True)
            ctx.inc(i, "s_sm", 1)
            ctx.wait(nc.vector, "s_sm")
            i = nc.vector.scalar_tensor_tensor(S32, S32, eblf,
                                               ctx.ps[6][:, 0:256],
                                               OP.mult, OP.add)
            ctx.vsync(i)
            i = nc.vector.tensor_copy(Sbf, S32)
            ctx.inc(i, "s_sb", 1)
            ctx.wait(nc.tensor, "s_sb")
            ctx.barrier()
    ctx.barrier()
    ctx.arena_release()

    ctx.arena_mark()
    o2t = ctx.arena([128, ND, TC], BF16)
    woN = ctx.arena([128, ND, 128], BF16)
    ucb = ctx.arena([128, ND, TC], BF16)
    u2b = ctx.arena([128, TC], BF16)
    for c in range(ctx.NC):
        sl = slice(c * TC, (c + 1) * TC)
        for et in range(ND):
            ctx.dma_t(o2t[:, et, :], o2_dram[c * TC:(c + 1) * TC,
                                             et * 128:(et + 1) * 128],
                      sem="ao_tl")
        load_xn_chunk(ctx, ucb, c, sem="ao_ul")
        ctx.barrier()
        for dt2 in range(ND):
            for et in range(ND):
                ctx.dma(woN[:, et, :], bfw["Wo"][et * 128:(et + 1) * 128,
                                                 dt2 * 128:(dt2 + 1) * 128],
                        sem="ao_wl")
            ctx.wait(nc.tensor, "ao_wl")
            pso = ctx.ps[dt2 % 2]
            if dt2 >= 2:
                ctx.wait(nc.tensor, "ao_e", ctx.cnt.get("ao_e", 0) - 1)
            for et in range(ND):
                i = nc.tensor.matmul(pso, woN[:, et, :], o2t[:, et, :],
                                     start=(et == 0), stop=(et == ND - 1))
            ctx.inc(i, "ao_m", 1)
            ctx.wait(nc.vector, "ao_m")
            if ctx.cnt.get("ao_s", 0):
                ctx.wait(nc.vector, "ao_s")
            i = nc.vector.tensor_tensor(u2b, pso, ucb[:, dt2, :], OP.add)
            nc.vector.tensor_tensor(ctx.x[:, dt2, sl], ctx.x[:, dt2, sl],
                                    pso, OP.add)
            ctx.inc(i, "ao_e", 1)
            ctx.wait(nc.sync, "ao_e")
            dst = u2_dram[c, :, :].rearrange("p (n t) -> p n t", n=ND)
            ctx.store(dst[:, dt2, :], u2b, sem="ao_s")
            ctx.wait(nc.gpsimd, "ao_m")
        ctx.barrier()
    ctx.arena_release()
    ctx.borrow = prev_borrow

    emit_norm(ctx, rms=True, eps=1e-6, gain_col=mnw_col,
              src_dram=u2_dram, out_dram=n2_dram)
    ctx.xn_src = n2_dram

    def load_g(dt, f0, fw, dest):
        ctx.dma(dest, bfw["W_gate"][dt * 128:(dt + 1) * 128, f0:f0 + fw])

    def load_u(dt, f0, fw, dest):
        ctx.dma(dest, bfw["W_up"][dt * 128:(dt + 1) * 128, f0:f0 + fw])

    def load_d(h0, dest):
        ctx.dma(dest, bfw["W_down"][h0:h0 + 128, :])

    emit_swiglu(ctx, 2816, load_g, load_u, load_d, act=AF.Silu, FS=512)
    ctx.xn_src = None


BLOCK_EMITS = {
    "l2_attn": emit_l2_attn,
    "l0_attn": emit_l0_attn,
    "l3_attn": emit_l3_attn,
    "l1_attn": emit_l1_attn,
    "l0_ffn": emit_l0_ffn,
    "l3_ffn": emit_l3_ffn,
    "l2_ffn": emit_l2_ffn,
    "l1_ffn": emit_l1_ffn,
}


SEM_ALIAS = {
"l0_attn": {
"b_av": "ov0",
"b_fl": "ov1",
"b_na": "ov2",
"b_nc": "ov3",
"b_nm": "ov4",
"b_om": "ov5",
"b_pe": "ov6",
"b_pt": "ov7",
"b_qe": "ov8",
"b_ql": "ov9",
"b_qm": "ov10",
"b_sm": "ov11",
"b_sq": "ov12",
"b_xa": "ov13",
"b_ye": "ov14",
"b_zb": "ov15",
"b_zc": "ov16",
"b_zr": "ov17",
"p1_cf": "ov18",
"p1_ck": "ov19",
"p1_fm": "ov20",
"p1_fs": "ov21",
"p1_m": "ov22",
"p1_n": "ov23",
"p1_n2": "ov24",
"p1_q": "ov25",
"p1_s": "ov26",
"p1_t": "ov27",
"p1_x": "ov28",
"p2_e": "ov29",
"p2_m": "ov30",
"p2_s": "ov31",
"p2_w": "ov32",
"pw_e": "ov33",
"pw_m": "ov34",
"pw_s": "ov35"
},
"l0_ffn": {},
"l1_attn": {
"l1b": "ov0",
"l1c": "ov1",
"pb_mk": "ov2",
"pb_v": "ov3",
"q0a": "ov4",
"q0av": "ov5",
"q0b": "ov6",
"q0e": "ov7",
"q0f": "ov8",
"q0g": "ov9",
"q0h": "ov10",
"q0k": "ov11",
"q0l": "ov12",
"q0m": "ov13",
"q0n": "ov14",
"q0p": "ov15",
"q0sm": "ov16",
"q0t": "ov17",
"q0v": "ov18",
"q0z": "ov19"
},
"l1_ffn": {
"l1m": "ov0",
"l1t": "ov1",
"l1v": "ov2",
"rwT": "ov3",
"rwV": "ov4",
"ska": "ov5",
"ska2": "ov6",
"ska3": "ov7",
"skb": "ov8",
"skl": "ov9",
"skm": "ov10",
"skn": "ov11",
"skv": "ov12",
"skv2": "ov13",
"skv3": "ov14",
"tpa": "ov15",
"tpe": "ov16",
"tpt": "ov17",
"tpw": "ov18"
},
"l2_attn": {
"ao_e": "ov0",
"ao_m": "ov1",
"ao_s": "ov2",
"ao_wl": "ov3",
"g_bc": "ov4",
"g_bv": "ov5",
"g_ga": "ov6",
"g_gm": "ov7",
"g_gv": "ov8",
"g_gv1": "ov9",
"g_sv": "ov10",
"g_xl": "ov11",
"g_ze": "ov12",
"g_zm": "ov13",
"s_ae": "ov14",
"s_am": "ov15",
"s_bm": "ov16",
"s_ea": "ov17",
"s_gg": "ov18",
"s_gm": "ov19",
"s_k2": "ov20",
"s_k2e": "ov21",
"s_kd": "ov22",
"s_km": "ov23",
"s_kt": "ov24",
"s_o2": "ov25",
"s_om": "ov26",
"s_or": "ov27",
"s_os": "ov28",
"s_osq": "ov29",
"s_qe": "ov30",
"s_qm": "ov31",
"s_sb": "ov32",
"s_sg": "ov33",
"s_sm": "ov34",
"s_ve": "ov35",
"s_vm": "ov36"
},
"l2_ffn": {
"rtT": "ov0",
"rtV": "ov1"
},
"l3_attn": {
"e3c": "ov0",
"e3l": "ov1",
"e3t": "ov2",
"e3v": "ov3",
"l3b": "ov4",
"l3c": "ov5",
"pb_bl": "ov6",
"pb_kl": "ov7",
"pb_vl": "ov8"
},
"l3_ffn": {}
}


def build_block_prog(names, T):
    nc = bass.Bass("TRN2", target_bir_lowering=False)
    ctx = Ctx(nc, T)
    declare_consts(ctx)
    setup_resident(ctx)
    load_x(ctx)
    base = 0
    for n in names:
        priv = sorted(SEM_ALIAS.get(n, {}))
        ctx.set_sem_scope({nm: f"ov{base + i}" for i, nm in enumerate(priv)})
        base += len(priv)
        BLOCK_EMITS[n](ctx)
        ctx.barrier()
        ctx.set_sem_scope({})
    store_x(ctx)
    mx = max(dict.values(ctx.cnt), default=0)
    assert mx < 60000, "semaphore count overflow risk: %d" % mx
    return ctx


LAUNCH_PLAN = [
    ["l0_attn", "l0_ffn"],
    ["l1_attn", "l1_ffn"],
    ["l2_attn"],
    ["l2_ffn", "l3_attn", "l3_ffn"],
]


def run_chain(inputs, plan, T, xT_list):
    consts = make_consts(T)
    cur = xT_list
    for names in plan:
        ctx = build_block_prog(names, T)
        in_maps = []
        for b in range(len(cur)):
            m = {}
            for nm in ctx.input_names:
                if nm == "xin":
                    m[nm] = cur[b]
                elif nm.startswith("c_"):
                    m[nm] = consts[nm]
                else:
                    m[nm] = inputs[nm]
            in_maps.append(m)
        res = run_bass_kernel_spmd(ctx.nc, in_maps, core_ids=list(range(len(cur))))
        cur = [res.results[b]["xout"] for b in range(len(cur))]
    return cur


def kernel(**inputs):
    x = inputs["L0_attn_x"]
    T = x.shape[1]
    xT = [np.ascontiguousarray(x[b].T) for b in range(x.shape[0])]
    out = run_chain(inputs, LAUNCH_PLAN, T, xT)
    return np.stack([o.T for o in out], axis=0)
